# version 11
# baseline (speedup 1.0000x reference)
import numpy as np
import concourse.bass as bass
import concourse.mybir as mybir
from concourse.bass_utils import run_bass_kernel_spmd

F32 = mybir.dt.float32
F32R = mybir.dt.float32r
AF = mybir.ActivationFunctionType

D = 1024
T = 2048
H = 16
FC = 8
NSC = 16
NG = 4
HS = 2048
NE = 8
EPS_ATT = 1e-8
EPS_LN = 1e-5
BACK = 11016
PS = 2568


class EM:
    ENGS = ("sync", "scalar", "vector", "tensor", "gpsimd")

    def __init__(self):
        self.streams = {e: [] for e in self.ENGS}
        self.cnt = {ch: 0 for ch in ("w", "wA", "wB", "wX", "wS", "d", "e", "pe", "dv", "ac")}
        self.lastwait = {e: {ch: 0 for ch in self.cnt} for e in self.ENGS}

    def emit(self, eng, fn, inc=None, k=1):
        if inc is not None:
            if eng in ("vector", "scalar"):
                self.wait(eng, inc, self.cnt[inc])
            self.cnt[inc] += k
            def wrapped(e, sems, fn=fn, inc=inc, k=k):
                fn(e, sems).then_inc(sems[inc], k)
            self.streams[eng].append(wrapped)
            return self.cnt[inc]
        self.streams[eng].append(lambda e, sems, fn=fn: fn(e, sems))
        return None

    def wait(self, eng, ch, val):
        if val is None or val <= 0:
            return
        assert val <= self.cnt[ch], (eng, ch, val, self.cnt[ch])
        if self.lastwait[eng][ch] >= val:
            return
        self.lastwait[eng][ch] = val
        self.streams[eng].append(
            lambda e, sems, ch=ch, val=val: e.wait_ge(sems[ch], val))


class Banks:
    def __init__(self, em):
        self.em = em
        self.free_at = {}

    def acquire(self, eng, *keys):
        for key in keys:
            ent = self.free_at.get(key)
            if ent is not None:
                self.em.wait(eng, ent[0], ent[1])

    def release(self, key, ch, val):
        self.free_at[key] = (ch, val)


SCAN_N = NSC
APH_BC = True
APH_NRM = True
APH_PROJ = True
APH_STATE = True
SCAN_WO = True
SCAN_ATR = True
SCAN_AT = True
SCAN_OT = True
SCAN_DEN = True
SCAN_DELTA = True
SCAN_KTR = True


def build_nc(taps=(), limit=99):
    nc = bass.Bass("TRN2", target_bir_lowering=False)

    xin = nc.dram_tensor("xin", [D, T], F32, kind="ExternalInput").ap()
    xprev = nc.dram_tensor("xprev", [D, T], F32, kind="ExternalInput").ap()
    maskA = nc.dram_tensor("maskA", [128, 1], F32, kind="ExternalInput").ap()
    wqT = nc.dram_tensor("wqT", [D, D], F32, kind="ExternalInput").ap()
    wkT = nc.dram_tensor("wkT", [D, D], F32, kind="ExternalInput").ap()
    wvT = nc.dram_tensor("wvT", [D, D], F32, kind="ExternalInput").ap()
    woT = nc.dram_tensor("woT", [D, D], F32, kind="ExternalInput").ap()
    sw1 = nc.dram_tensor("sw1", [D, HS], F32, kind="ExternalInput").ap()
    sw3 = nc.dram_tensor("sw3", [D, HS], F32, kind="ExternalInput").ap()
    sw2 = nc.dram_tensor("sw2", [HS, D], F32, kind="ExternalInput").ap()
    w1 = nc.dram_tensor("w1", [NE, D, D], F32, kind="ExternalInput").ap()
    w2 = nc.dram_tensor("w2", [NE, D, D], F32, kind="ExternalInput").ap()
    sb1c = nc.dram_tensor("sb1c", [128, 16], F32, kind="ExternalInput").ap()
    sb3c = nc.dram_tensor("sb3c", [128, 16], F32, kind="ExternalInput").ap()
    sb2c = nc.dram_tensor("sb2c", [128, 8], F32, kind="ExternalInput").ap()
    b1c = nc.dram_tensor("b1c", [NE, 128, 8], F32, kind="ExternalInput").ap()
    b2c = nc.dram_tensor("b2c", [NE, 128, 8], F32, kind="ExternalInput").ap()
    tmaskD = nc.dram_tensor("tmaskD", [128, 128], F32, kind="ExternalInput").ap()
    eyeD = nc.dram_tensor("eyeD", [128, 128], F32, kind="ExternalInput").ap()
    onesD = nc.dram_tensor("onesD", [128, 128], F32, kind="ExternalInput").ap()
    out = nc.dram_tensor("out", [D, T], F32, kind="ExternalOutput").ap()

    spk = "ExternalOutput" if "spills" in taps else "Internal"
    q_sp = nc.dram_tensor("q_sp", [NG, FC, 128, 512], F32, kind=spk).ap()
    k_sp = nc.dram_tensor("k_sp", [NG, FC, 128, 512], F32, kind=spk).ap()
    v_sp = nc.dram_tensor("v_sp", [NSC, 128, 1040], F32, kind=spk).ap()

    tap_aps = {}
    for tname in taps:
        if tname == "spills":
            continue
        cols = BACK if tname == "back" else FC * T
        tap_aps[tname] = nc.dram_tensor(
            "tap_" + tname, [128, cols], F32, kind="ExternalOutput").ap()

    em = EM()
    bk = Banks(em)
    ctxs = []

    def sbuf(name, shape, dtype):
        cm = nc.sbuf_tensor(name, shape, dtype)
        ctxs.append(cm)
        return cm.__enter__()

    def psum(name, shape):
        cm = nc.psum_tensor(name, shape, F32)
        ctxs.append(cm)
        return cm.__enter__()

    xn = sbuf("xn", [128, FC * T], F32R)
    acc = sbuf("acc", [128, FC * T], F32)
    slotA = sbuf("slotA", [128, 4096], F32R)
    slotB = sbuf("slotB", [128, 4096], F32R)
    consts = sbuf("consts", [128, 128], F32)
    tmask = sbuf("tmask", [128, 128], F32)
    eyeS = sbuf("eyeS", [128, 128], F32)
    maskS = sbuf("maskS", [128, 1], F32)
    sb1s = sbuf("sb1s", [128, 16], F32)
    sb3s = sbuf("sb3s", [128, 16], F32)
    sb2s = sbuf("sb2s", [128, 8], F32)
    b1s = sbuf("b1s", [128, NE * 8], F32)
    b2s = sbuf("b2s", [128, NE * 8], F32)
    b2cl = sbuf("b2cl", [128, NE * 8], F32)
    epss = sbuf("epss", [128, 1], F32)
    backing = sbuf("backing", [128, BACK], F32)
    base = nc.lookup_mloc(backing).addr

    def overlay(name, cols, dtype, coloff):
        return nc.alloc_sbuf_tensor_at(name, [128, cols], dtype,
                                       offset=base + coloff * 4)

    stS = overlay("stS", 520, F32, 0)
    stsA = overlay("stsA", 2048, F32, 520)
    stsB = overlay("stsB", 2048, F32, 520)
    xaS = overlay("xaS", 4096, F32, PS)
    sqA = overlay("sqA", 1024, F32, PS + 4096)
    tmpA = overlay("tmpA", 1024, F32, PS + 5120)
    xaF = overlay("xaF", 1024, F32, PS)
    xnA = overlay("xnA", 1024, F32R, PS + 1024)
    teA = overlay("teA", 512, F32, PS + 2048)
    ktA = overlay("ktA", 512, F32, PS + 2560)
    vtA = overlay("vtA", 520, F32, PS + 3072)
    sqB = overlay("sqB", 1024, F32, PS)
    tmpB = overlay("tmpB", 1024, F32, PS + 1024)
    nrmB = overlay("nrmB", 4096, F32, PS + 2048)
    stg = overlay("stg", 4096, F32, PS)
    trS = overlay("trS", 512, F32, PS + 4096)
    vstg = overlay("vstg", 2080, F32, PS + 4608)
    qS = overlay("qS", 1024, F32, PS)
    kfS = overlay("kfS", 1024, F32, PS + 1024)
    vtS = overlay("vtS", 1040, F32, PS + 2048)
    ktS = overlay("ktS", 1024, F32, PS + 3088)
    amS = overlay("amS", 1024, F32, PS + 4112)
    atkS = overlay("atkS", 1024, F32, PS + 5136)
    denS = overlay("denS", 32, F32, PS + 6160)
    afmS = overlay("afmS", 2048, F32R, PS + 6192)
    sbkS = overlay("sbkS", 2048, F32R, PS)
    s1S = overlay("s1S", 512, F32, PS + 2048)
    u2S = overlay("u2S", 512, F32, PS + 2560)
    sw2S = overlay("sw2S", 4096, F32R, PS + 3072)
    wxS = overlay("wxS", 4096, F32R, PS)
    heS = overlay("heS", 4096, F32R, PS + 4096)
    tmpM = overlay("tmpM", 256, F32, PS + 8192)

    P01 = psum("P01", [128, 1024])
    P23 = psum("P23", [128, 1024])
    P45 = psum("P45", [128, 1024])
    P6 = psum("P6", [128, 512])
    P7 = psum("P7", [128, 512])

    def pA(t, p0, c0, dims):
        stride = t.shape[1]
        return bass.AP(t, p0 * stride + c0, [[stride, dims[0]]] + dims[1:])

    def dma(eng, dst, src, ch=None):
        if ch is None:
            ch = "w" if eng == "gpsimd" else "d"
        return em.emit(eng, lambda e, s, d=dst, sr=src: e.dma_start(out=d, in_=sr),
                       inc=ch, k=16)

    def MM(lhsT, rhs, dst, start, stop):
        return em.emit("tensor", lambda e, s, l=lhsT, r=rhs, d=dst, st=start, sp=stop:
                       e.matmul(d, l, r, start=st, stop=sp), inc="pe")

    slot_free = {"A": 0, "B": 0}

    def wload(dst_ap, src_ap, wait_pe=None, slot=None):
        if wait_pe is None:
            wait_pe = slot_free.get(slot, em.cnt["pe"]) if slot else em.cnt["pe"]
        em.wait("gpsimd", "pe", wait_pe)
        ch = ("w" + slot) if slot in ("A", "B", "X", "S") else "w"
        return (ch, dma("gpsimd", dst_ap, src_ap, ch=ch))

    def slot_used(slot):
        slot_free[slot] = em.cnt["pe"]

    def wwait(eng, *tokens):
        for ch, val in tokens:
            em.wait(eng, ch, val)

    def whalf(wt, eg):
        return bass.AP(wt.tensor, eg * 512, [[1024, 128], [131072, 8], [1, 512]])

    def finish():
        dv_all = em.cnt["dv"]
        em.wait("sync", "dv", dv_all)
        em.wait("sync", "pe", em.cnt["pe"])
        em.wait("sync", "ac", em.cnt["ac"])
        dfin = dma("sync", bass.AP(out.tensor, 0, [[T, 128], [128 * T, 8], [1, T]]),
                   bass.AP(acc, 0, [[FC * T, 128], [T, 8], [1, T]]))
        if "back" in tap_aps:
            dfin = dma("sync", tap_aps["back"][:, :],
                       bass.AP(backing, 0, [[BACK, 128], [1, BACK]]))
        em.wait("sync", "d", dfin)
        semctxs, sems = [], {}
        for ch in em.cnt:
            cm = nc.semaphore("sem_" + ch)
            semctxs.append(cm)
            sems[ch] = cm.__enter__()
        blk_cm = nc.Block()
        block = blk_cm.__enter__()
        for eng in EM.ENGS:
            stream = em.streams[eng]
            getattr(block, eng)(lambda e, stream=stream: [f(e, sems) for f in stream])
        blk_cm.__exit__(None, None, None)
        for cm in semctxs:
            cm.__exit__(None, None, None)
        for cm in reversed(ctxs):
            cm.__exit__(None, None, None)
        return nc

    dma("gpsimd", consts[:, :], onesD)
    dma("gpsimd", tmask[:, :], tmaskD)
    dma("gpsimd", eyeS[:, :], eyeD)
    dma("gpsimd", maskS[:, :], maskA)
    dma("gpsimd", sb1s[:, :], sb1c)
    dma("gpsimd", sb3s[:, :], sb3c)
    dma("gpsimd", sb2s[:, :], sb2c)
    dma("gpsimd", b1s[:, :], bass.AP(b1c.tensor, 0, [[8, 128], [1024, 8], [1, 8]]))
    wconst = dma("gpsimd", b2s[:, :], bass.AP(b2c.tensor, 0, [[8, 128], [1024, 8], [1, 8]]))

    dx = dma("sync", bass.AP(acc, 0, [[FC * T, 128], [T, 8], [1, T]]),
             bass.AP(xin.tensor, 0, [[T, 128], [128 * T, 8], [1, T]]), ch="e")

    em.wait("vector", "w", wconst)
    em.emit("vector", lambda e, s: e.memset(epss[:, :], EPS_LN), inc="dv")
    em.emit("vector", lambda e, s: e.tensor_scalar_mul(sb3s[:, :], sb3s[:, :], 0.5), inc="dv")
    em.emit("vector", lambda e, s: e.tensor_scalar_mul(sb2s[:, :], sb2s[:, :], 0.5), inc="dv")
    for r in range(8):
        e0, e1 = (3 * r) % 8, (3 * r - 1) % 8
        em.emit("vector", lambda e, s, r=r, e0=e0, e1=e1: e.tensor_add(
            b2cl[:, r * 8:(r + 1) * 8], b2s[:, e0 * 8:(e0 + 1) * 8],
            b2s[:, e1 * 8:(e1 + 1) * 8]), inc="dv")
    em.emit("vector", lambda e, s: e.tensor_scalar_mul(b2cl[:, :], b2cl[:, :], 0.25), inc="dv")
    em.emit("vector", lambda e, s: e.memset(stS[:, :], 0.0), inc="dv")
    dv_sinit = em.cnt["dv"]

    BPAIR = [((P01, 0, "P01a"), (P01, 512, "P01b")),
             ((P23, 0, "P23a"), (P23, 512, "P23b")),
             ((P45, 0, "P45a"), (P45, 512, "P45b")),
             ((P6, 0, "P6"), (P7, 0, "P7"))]

    def emit_stats(st, sq, tmp, src_chunk, pre=None):
        sq_mm = [0, 0]
        for g in range(NG):
            (bS, oS, kS), (bQ, oQ, kQ) = BPAIR[g]
            if pre is not None:
                dn = pre(g)
                em.wait("tensor", "d", dn)
                em.wait("scalar", "d", dn)
            bk.acquire("tensor", kS, kQ)
            for fc in range(FC):
                par = fc % 2
                src = src_chunk(g, fc)
                MM(consts[:, 0:1], src, pA(bS, 0, oS, [1, [1, 512]]),
                   start=(fc == 0), stop=(fc == FC - 1))
                em.wait("scalar", "pe", sq_mm[par])
                an = em.emit("scalar", lambda e, s, src=src, par=par: e.activation(
                    pA(sq, 0, par * 512, [128, [1, 512]]), src, AF.Square), inc="ac")
                em.wait("tensor", "ac", an)
                sq_mm[par] = MM(consts[:, 0:1], pA(sq, 0, par * 512, [128, [1, 512]]),
                                pA(bQ, 0, oQ, [1, [1, 512]]),
                                start=(fc == 0), stop=(fc == FC - 1))
        pe_done = em.cnt["pe"]
        em.wait("vector", "pe", pe_done)
        for g in range(NG):
            (bS, oS, kS), (bQ, oQ, kQ) = BPAIR[g]
            em.emit("vector", lambda e, s, bS=bS, oS=oS, g=g: e.tensor_scalar_mul(
                pA(st, 0, g * 512, [1, [1, 512]]),
                pA(bS, 0, oS, [1, [1, 512]]), 1.0 / D), inc="dv")
            dvq = em.emit("vector", lambda e, s, bQ=bQ, oQ=oQ: e.tensor_scalar_mul(
                pA(tmp, 0, 0, [1, [1, 512]]),
                pA(bQ, 0, oQ, [1, [1, 512]]), 1.0 / D), inc="dv")
            bk.release(kS, "dv", dvq)
            bk.release(kQ, "dv", dvq)
            em.emit("vector", lambda e, s, g=g: e.tensor_mul(
                pA(tmp, 0, 512, [1, [1, 512]]),
                pA(st, 0, g * 512, [1, [1, 512]]),
                pA(st, 0, g * 512, [1, [1, 512]])), inc="dv")
            dvv = em.emit("vector", lambda e, s: e.tensor_sub(
                pA(tmp, 0, 0, [1, [1, 512]]),
                pA(tmp, 0, 0, [1, [1, 512]]),
                pA(tmp, 0, 512, [1, [1, 512]])), inc="dv")
            em.wait("scalar", "dv", dvv)
            an = em.emit("scalar", lambda e, s: e.activation(
                pA(tmp, 0, 512, [1, [1, 512]]),
                pA(tmp, 0, 0, [1, [1, 512]]), AF.Sqrt, bias=epss[0:1, 0:1]), inc="ac")
            em.wait("vector", "ac", an)
            em.emit("vector", lambda e, s, g=g: e.reciprocal(
                pA(st, 32, g * 512, [1, [1, 512]]),
                pA(tmp, 0, 512, [1, [1, 512]])), inc="dv")
            em.emit("vector", lambda e, s: e.drain())
        return em.cnt["dv"]

    if limit <= 0:
        return finish()
    def a1_pre(g):
        if g > 0:
            em.wait("sync", "pe", em.cnt["pe"])
            em.wait("sync", "ac", em.cnt["ac"])
        return dma("sync", bass.AP(xaS, 0, [[4096, 128], [512, 8], [1, 512]]),
                   bass.AP(xprev.tensor, g * 512, [[T, 128], [128 * T, 8], [1, 512]]))

    em.wait("tensor", "w", wconst)
    dvA = emit_stats(stsA, sqA, tmpA,
                     lambda g, fc: pA(xaS, 0, fc * 512, [128, [1, 512]]),
                     pre=a1_pre)

    if limit <= 1:
        return finish()
    em.wait("gpsimd", "dv", dvA)
    em.wait("tensor", "dv", dvA)
    dv_prev_state = dv_sinit
    pe_prev_proj = em.cnt["pe"]
    em.wait("sync", "ac", em.cnt["ac"])
    for eg in range(2):
        wk = wload(slotA[:, :], whalf(wkT, eg), slot="A")
        wv = wload(slotB[:, :], whalf(wvT, eg), slot="B")
        wwait("tensor", wk, wv)
        for sc in range(NSC):
            t0 = sc * 128
            em.wait("sync", "pe", pe_prev_proj)
            dn = dma("sync", bass.AP(xaF, 0, [[1024, 128], [128, 8], [1, 128]]),
                     bass.AP(xprev.tensor, t0, [[T, 128], [128 * T, 8], [1, 128]]))
            if not APH_BC:
                continue
            bk.acquire("tensor", "P6", "P7")
            MM(consts[0:1, :], pA(stsA, 0, t0, [1, [1, 128]]),
               pA(P6, 0, 0, [128, [1, 128]]), True, True)
            bmr = MM(consts[32:33, :], pA(stsA, 32, t0, [1, [1, 128]]),
                     pA(P7, 0, 0, [128, [1, 128]]), True, True)
            if not APH_NRM:
                bk.release("P6", "pe", bmr)
                bk.release("P7", "pe", bmr)
                continue
            em.wait("vector", "d", dn)
            em.wait("vector", "pe", bmr)
            em.emit("vector", lambda e, s: e.tensor_sub(
                pA(xaF, 0, 0, [128, [128, 8], [1, 128]]),
                pA(xaF, 0, 0, [128, [128, 8], [1, 128]]),
                pA(P6, 0, 0, [128, [0, 8], [1, 128]])), inc="dv")
            dvn = em.emit("vector", lambda e, s: e.tensor_mul(
                pA(xnA, 0, 0, [128, [128, 8], [1, 128]]),
                pA(xaF, 0, 0, [128, [128, 8], [1, 128]]),
                pA(P7, 0, 0, [128, [0, 8], [1, 128]])), inc="dv")
            bk.release("P6", "dv", dvn)
            bk.release("P7", "dv", dvn)
            em.wait("tensor", "dv", dvn)
            if not APH_PROJ:
                continue
            bk.acquire("tensor", "P01a")
            for fc in range(FC):
                MM(pA(xnA, 0, fc * 128, [128, [1, 128]]),
                   pA(slotA, 0, fc * 512, [128, [1, 512]]),
                   pA(P01, 0, 0, [128, [1, 512]]),
                   start=(fc == 0), stop=(fc == FC - 1))
            pe_k = em.cnt["pe"]
            bk.acquire("tensor", "P23a")
            for fc in range(FC):
                MM(pA(xnA, 0, fc * 128, [128, [1, 128]]),
                   pA(slotB, 0, fc * 512, [128, [1, 512]]),
                   pA(P23, 0, 0, [128, [1, 512]]),
                   start=(fc == 0), stop=(fc == FC - 1))
            pe_v = em.cnt["pe"]
            pe_prev_proj = pe_v
            em.wait("scalar", "pe", pe_k)
            a1 = em.emit("scalar", lambda e, s: e.activation(
                teA[:, :], pA(P01, 0, 0, [128, [1, 512]]), AF.Exp), inc="ac")
            em.wait("vector", "ac", a1)
            em.emit("vector", lambda e, s: e.tensor_scalar_min(
                teA[:, :], teA[:, :], 1.0), inc="dv")
            em.emit("vector", lambda e, s: e.tensor_scalar_max(
                ktA[:, :], pA(P01, 0, 0, [128, [1, 512]]), 0.0), inc="dv")
            dvk = em.emit("vector", lambda e, s: e.tensor_add(
                ktA[:, :], ktA[:, :], teA[:, :]), inc="dv")
            bk.release("P01a", "dv", dvk)
            em.wait("vector", "pe", pe_v)
            em.emit("vector", lambda e, s: e.tensor_copy(
                pA(vtA, 0, 0, [128, [65, 8], [1, 64]]),
                pA(P23, 0, 0, [128, [64, 8], [1, 64]])), inc="dv")
            dvv = em.emit("vector", lambda e, s: e.memset(
                pA(vtA, 0, 64, [128, [65, 8], [1, 1]]), 1.0), inc="dv")
            bk.release("P23a", "dv", dvv)
            if not APH_STATE:
                continue
            em.wait("tensor", "dv", max(dvk, dvv, dv_prev_state))
            bk.acquire("tensor", "P45a")
            for hl in range(8):
                MM(pA(ktA, 0, (hl // 2) * 128 + 64 * (hl % 2), [128, [1, 64]]),
                   pA(vtA, 0, hl * 65, [128, [1, 65]]),
                   pA(P45, 64 * (hl % 2), 128 * (hl // 2), [64, [1, 65]]),
                   True, True)
            pe_s = em.cnt["pe"]
            em.wait("vector", "pe", pe_s)
            dv_prev_state = em.emit("vector", lambda e, s, eg=eg: e.tensor_add(
                pA(stS, 0, 65 * 4 * eg, [128, [65, 4], [1, 65]]),
                pA(stS, 0, 65 * 4 * eg, [128, [65, 4], [1, 65]]),
                pA(P45, 0, 0, [128, [128, 4], [1, 65]])), inc="dv")
            bk.release("P45a", "dv", dv_prev_state)
        slot_used("A")
        slot_used("B")
    em.wait("vector", "dv", dv_prev_state)
    dv_state = em.emit("vector", lambda e, s: e.tensor_scalar_mul(
        stS[:, :], stS[:, :], maskS[:, 0:1]), inc="dv")

    if limit <= 2:
        return finish()
    em.wait("vector", "e", dx)
    em.wait("tensor", "e", dx)
    em.wait("scalar", "e", dx)
    em.wait("scalar", "pe", em.cnt["pe"])
    d_all = em.cnt["d"]
    dv_stats = emit_stats(stsB, sqB, tmpB,
                          lambda g, fc: pA(acc, 0, fc * T + g * 512, [128, [1, 512]]),
                          pre=lambda g: d_all)
    em.wait("tensor", "dv", dv_stats)
    for g in range(NG):
        B = P01 if g % 2 == 0 else P23
        ka, kb = ("P01a", "P01b") if g % 2 == 0 else ("P23a", "P23b")
        bk.acquire("tensor", ka, kb)
        MM(consts[0:1, :], pA(stsB, 0, g * 512, [1, [1, 512]]),
           pA(B, 0, 0, [128, [1, 512]]), True, True)
        bmr = MM(consts[32:33, :], pA(stsB, 32, g * 512, [1, [1, 512]]),
                 pA(B, 0, 512, [128, [1, 512]]), True, True)
        em.wait("vector", "pe", bmr)
        em.emit("vector", lambda e, s, g=g, B=B: e.tensor_sub(
            pA(nrmB, 0, 0, [128, [512, 8], [1, 512]]),
            pA(acc, 0, g * 512, [128, [T, 8], [1, 512]]),
            pA(B, 0, 0, [128, [0, 8], [1, 512]])), inc="dv")
        dvn = em.emit("vector", lambda e, s, g=g, B=B: e.tensor_mul(
            pA(xn, 0, g * 512, [128, [T, 8], [1, 512]]),
            pA(nrmB, 0, 0, [128, [512, 8], [1, 512]]),
            pA(B, 0, 512, [128, [0, 8], [1, 512]])), inc="dv")
        bk.release(ka, "dv", dvn)
        bk.release(kb, "dv", dvn)
    dv_xn = em.cnt["dv"]
    for mc in range(8):
        em.emit("vector", lambda e, s, mc=mc: e.tensor_scalar_add(
            pA(acc, 0, mc * T, [128, [1, T]]), pA(acc, 0, mc * T, [128, [1, T]]),
            sb2s[:, mc:mc + 1]), inc="dv")
    for r in range(8):
        for mc in range(8):
            em.emit("vector", lambda e, s, r=r, mc=mc: e.tensor_scalar_add(
                pA(acc, 0, mc * T + r, [128, [8, 256]]),
                pA(acc, 0, mc * T + r, [128, [8, 256]]),
                b2cl[:, r * 8 + mc:r * 8 + mc + 1]), inc="dv")
    dv_accinit = em.cnt["dv"]

    if limit <= 3:
        return finish()
    def proj_fm(wsrc, spill):
        h1 = wload(slotA[:, :], whalf(wsrc, 0), slot="A")
        h2 = wload(slotB[:, :], whalf(wsrc, 1), slot="B")
        wwait("tensor", h1, h2)
        em.wait("tensor", "dv", dv_xn)
        em.wait("scalar", "d", em.cnt["d"])
        em.wait("vector", "d", em.cnt["d"])
        banks = [(P01, 0, "P01a"), (P01, 512, "P01b"),
                 (P23, 0, "P23a"), (P23, 512, "P23b")]
        spdmas = []
        hg = 0
        for g in range(NG):
            for ech in range(2):
                stgo = 2048 * (hg % 2)
                if hg >= 2:
                    em.wait("vector", "d", spdmas[hg - 1])
                    em.wait("scalar", "d", spdmas[hg - 1])
                for el in range(4):
                    ec = ech * 4 + el
                    slot = slotA if ec < 4 else slotB
                    b, o, key = banks[ec % 4]
                    bk.acquire("tensor", key)
                    for fc in range(FC):
                        MM(pA(slot, 0, fc * 512 + (ec % 4) * 128, [128, [1, 128]]),
                           pA(xn, 0, fc * T + g * 512, [128, [1, 512]]),
                           pA(b, 0, o, [128, [1, 512]]),
                           start=(fc == 0), stop=(fc == FC - 1))
                    pe_mm = em.cnt["pe"]
                    em.wait("scalar", "pe", pe_mm)
                    a1 = em.emit("scalar", lambda e, s, b=b, o=o, stgo=stgo, el=el: e.activation(
                        pA(stg, 0, stgo + el * 512, [128, [1, 512]]),
                        pA(b, 0, o, [128, [1, 512]]), AF.Exp), inc="ac")
                    em.wait("vector", "ac", a1)
                    em.emit("vector", lambda e, s, stgo=stgo, el=el: e.tensor_scalar_min(
                        pA(stg, 0, stgo + el * 512, [128, [1, 512]]),
                        pA(stg, 0, stgo + el * 512, [128, [1, 512]]), 1.0), inc="dv")
                    em.emit("vector", lambda e, s, b=b, o=o: e.tensor_scalar_max(
                        trS[:, :], pA(b, 0, o, [128, [1, 512]]), 0.0), inc="dv")
                    dve = em.emit("vector", lambda e, s, stgo=stgo, el=el: e.tensor_add(
                        pA(stg, 0, stgo + el * 512, [128, [1, 512]]),
                        pA(stg, 0, stgo + el * 512, [128, [1, 512]]),
                        trS[:, :]), inc="dv")
                    bk.release(key, "dv", dve)
                dv_g = em.cnt["dv"]
                em.wait("sync", "dv", dv_g)
                spdmas.append(dma(
                    "sync",
                    bass.AP(spill.tensor, g * 524288 + ech * 4 * 65536,
                            [[512, 128], [65536, 4], [1, 512]]),
                    pA(stg, 0, stgo, [128, [512, 4], [1, 512]])))
                hg += 1
        slot_used("A")
        slot_used("B")
        return spdmas[-1]

    proj_fm(wkT, k_sp)

    if limit <= 5:
        return finish()
    h1 = wload(slotA[:, :], whalf(wvT, 0), slot="A")
    h2 = wload(slotB[:, :], whalf(wvT, 1), slot="B")
    wwait("tensor", h1, h2)
    em.wait("vector", "d", em.cnt["d"])
    vdmas = []
    for sc in range(NSC):
        stgo = 1040 * (sc % 2)
        if sc >= 2:
            em.wait("vector", "d", vdmas[sc - 1])
        PB = P01 if sc % 2 == 0 else P23
        ka, kb = ("P01a", "P01b") if sc % 2 == 0 else ("P23a", "P23b")
        bk.acquire("tensor", ka, kb)
        for eg in range(2):
            slot = slotA if eg == 0 else slotB
            for fc in range(FC):
                MM(pA(xn, 0, fc * T + sc * 128, [128, [1, 128]]),
                   pA(slot, 0, fc * 512, [128, [1, 512]]),
                   pA(PB, 0, eg * 512, [128, [1, 512]]),
                   start=(fc == 0), stop=(fc == FC - 1))
        pe_v = em.cnt["pe"]
        em.wait("vector", "pe", pe_v)
        em.emit("vector", lambda e, s, stgo=stgo, PB=PB: e.tensor_copy(
            pA(vstg, 0, stgo, [128, [65, 16], [1, 64]]),
            pA(PB, 0, 0, [128, [64, 16], [1, 64]])), inc="dv")
        dvv = em.emit("vector", lambda e, s, stgo=stgo: e.memset(
            pA(vstg, 0, stgo + 64, [128, [65, 16], [1, 1]]), 1.0), inc="dv")
        bk.release(ka, "dv", dvv)
        bk.release(kb, "dv", dvv)
        em.wait("sync", "dv", dvv)
        vdmas.append(dma("sync",
                         bass.AP(v_sp.tensor, sc * (128 * 1040), [[1040, 128], [1, 1040]]),
                         pA(vstg, 0, stgo, [128, [1, 1040]])))
    slot_used("A")
    slot_used("B")

    if limit <= 6:
        return finish()
    proj_fm(wqT, q_sp)

    if limit <= 7:
        return finish()
    wo1 = wload(slotA[:, :], whalf(woT, 0), slot="A")
    wo2 = wload(slotB[:, :], whalf(woT, 1), slot="B")
    em.wait("sync", "d", em.cnt["d"])
    wwait("tensor", wo1, wo2)

    use_q = use_k = use_v = 0
    for sc in range(SCAN_N):
        g, t0 = sc // 4, (sc % 4) * 128
        em.wait("sync", "pe", use_q)
        dq = dma("sync", bass.AP(qS, 0, [[1024, 128], [128, 8], [1, 128]]),
                 bass.AP(q_sp.tensor, g * 524288 + t0, [[512, 128], [65536, 8], [1, 128]]))
        em.wait("sync", "pe", use_k)
        dk = dma("sync", bass.AP(kfS, 0, [[1024, 128], [128, 8], [1, 128]]),
                 bass.AP(k_sp.tensor, g * 524288 + t0, [[512, 128], [65536, 8], [1, 128]]))
        em.wait("sync", "pe", use_v)
        dv1 = dma("sync", bass.AP(vtS, 0, [[1040, 128], [1, 520]]),
                  bass.AP(v_sp.tensor, sc * (128 * 1040), [[1040, 128], [1, 520]]))
        dv2 = dma("sync", bass.AP(vtS, 520, [[1040, 128], [1, 520]]),
                  bass.AP(v_sp.tensor, sc * (128 * 1040) + 520, [[1040, 128], [1, 520]]))
        em.wait("tensor", "d", dv2)
        dv_ktok = 0
        if SCAN_KTR:
            bk.acquire("tensor", "P45a", "P45b")
            for ec in range(FC):
                em.emit("tensor", lambda e, s, ec=ec: e.transpose(
                    pA(P45, 0, ec * 128, [128, [1, 128]]),
                    pA(kfS, 0, ec * 128, [128, [1, 128]]),
                    eyeS[:, :]), inc="pe")
            bk.acquire("tensor", "P01a")
            pe_ktr = em.emit("tensor", lambda e, s: e.transpose(
                pA(P01, 0, 0, [128, [1, 128]]), eyeS[:, :], eyeS[:, :]), inc="pe")
            em.wait("vector", "pe", pe_ktr)
            dv_ktok = em.emit("vector", lambda e, s: e.tensor_copy(
                ktS[:, :], pA(P45, 0, 0, [128, [1, 1024]])), inc="dv")
            bk.release("P45a", "dv", dv_ktok)
            bk.release("P45b", "dv", dv_ktok)
        for w in range(2):
            heads = list(range(w * 8, w * 8 + 8))
            dv_am = 0
            if SCAN_AT:
                bk.acquire("tensor", "P01a", "P01b")
                for h in heads:
                    l = h % 8
                    MM(pA(kfS, 64 * (h % 2), (h // 2) * 128, [64, [1, 128]]),
                       pA(qS, 64 * (h % 2), (h // 2) * 128, [64, [1, 128]]),
                       pA(P01, 0, (l % 2) * 512 + (l // 2) * 128, [128, [1, 128]]),
                       True, True)
                pe_at = em.cnt["pe"]
                em.wait("vector", "pe", pe_at)
                dv_am = em.emit("vector", lambda e, s: e.tensor_mul(
                    pA(amS, 0, 0, [128, [128, 8], [1, 128]]),
                    pA(P01, 0, 0, [128, [128, 8], [1, 128]]),
                    pA(tmask, 0, 0, [128, [0, 8], [1, 128]])), inc="dv")
                bk.release("P01a", "dv", dv_am)
                bk.release("P01b", "dv", dv_am)
            em.wait("tensor", "dv", max(dv_am, dv_ktok, dv_state))
            em.wait("tensor", "d", dv2)
            if SCAN_OT:
                bk.acquire("tensor", "P23a", "P23b")
                for h in heads:
                    l = h % 8
                    MM(pA(amS, 0, (l % 2) * 512 + (l // 2) * 128, [128, [1, 128]]),
                       pA(vtS, 0, h * 65, [128, [1, 65]]),
                       pA(P23, 0, 128 * (h % 8), [128, [1, 65]]),
                       True, False)
                    MM(pA(qS, 64 * (h % 2), (h // 2) * 128, [64, [1, 128]]),
                       pA(stS, 64 * (h % 2), 65 * (h // 2), [64, [1, 65]]),
                       pA(P23, 0, 128 * (h % 8), [128, [1, 65]]),
                       False, True)
                pe_o = em.cnt["pe"]
                em.wait("vector", "pe", pe_o)
                if SCAN_DEN:
                    em.emit("vector", lambda e, s, w=w: e.tensor_scalar_add(
                        pA(denS, 0, 8 * w, [128, [1, 8], [1, 1]]),
                        pA(P23, 0, 64, [128, [128, 8], [1, 1]]), EPS_ATT), inc="dv")
                    em.emit("vector", lambda e, s: e.drain())
                    em.emit("vector", lambda e, s, w=w: e.reciprocal(
                        pA(denS, 0, 16 + 8 * w, [128, [1, 8]]),
                        pA(denS, 0, 8 * w, [128, [1, 8]])), inc="dv")
                    em.emit("vector", lambda e, s: e.drain())
                dv_div = em.emit("vector", lambda e, s, w=w: e.tensor_mul(
                    pA(atkS, 0, 512 * w, [128, [64, 8], [1, 64]]),
                    pA(P23, 0, 0, [128, [128, 8], [1, 64]]),
                    pA(denS, 0, 16 + 8 * w, [128, [1, 8], [0, 64]])), inc="dv")
                bk.release("P23a", "dv", dv_div)
                bk.release("P23b", "dv", dv_div)
            if SCAN_DELTA:
                bk.acquire("tensor", "P6")
                for h in heads:
                    MM(pA(ktS, 0, (h // 2) * 128 + 64 * (h % 2), [128, [1, 64]]),
                       pA(vtS, 0, h * 65, [128, [1, 65]]),
                       pA(P6, 64 * (h % 2), 128 * ((h // 2) % 4), [64, [1, 65]]),
                       True, True)
                pe_sd = em.cnt["pe"]
                if w == 1:
                    use_q = pe_o
                    use_k = pe_at
                    use_v = pe_sd
                em.wait("vector", "pe", pe_sd)
                dv_state = em.emit("vector", lambda e, s, w=w: e.tensor_add(
                    pA(stS, 0, 65 * 4 * w, [128, [65, 4], [1, 65]]),
                    pA(stS, 0, 65 * 4 * w, [128, [65, 4], [1, 65]]),
                    pA(P6, 0, 0, [128, [128, 4], [1, 65]])), inc="dv")
                bk.release("P6", "dv", dv_state)
        em.wait("tensor", "dv", em.cnt["dv"])
        if SCAN_ATR:
            bk.acquire("tensor", "P45a", "P45b")
            for ec in range(FC):
                em.emit("tensor", lambda e, s, ec=ec: e.transpose(
                    pA(P45, 0, ec * 128, [128, [1, 128]]),
                    pA(atkS, 0, ec * 128, [128, [1, 128]]),
                    eyeS[:, :]), inc="pe")
            bk.acquire("tensor", "P01a")
            pe_atr = em.emit("tensor", lambda e, s: e.transpose(
                pA(P01, 0, 0, [128, [1, 128]]), eyeS[:, :], eyeS[:, :]), inc="pe")
            em.wait("vector", "pe", pe_atr)
            dv_afm = em.emit("vector", lambda e, s, sc=sc: e.tensor_copy(
                pA(afmS, 0, (sc % 2) * 128, [128, [256, 8], [1, 128]]),
                pA(P45, 0, 0, [128, [128, 8], [1, 128]])), inc="dv")
            bk.release("P45a", "dv", dv_afm)
            bk.release("P45b", "dv", dv_afm)
        if SCAN_WO and SCAN_ATR and sc % 2 == 1:
            tg0 = (sc - 1) * 128
            em.wait("tensor", "dv", dv_afm)
            for mc in range(8):
                slot = slotA if mc < 4 else slotB
                key = "P7" if mc % 2 == 0 else "P6"
                B = P7 if mc % 2 == 0 else P6
                bk.acquire("tensor", key)
                for ec in range(FC):
                    MM(pA(slot, 0, ec * 512 + (mc % 4) * 128, [128, [1, 128]]),
                       pA(afmS, 0, ec * 256, [128, [1, 256]]),
                       pA(B, 0, 0, [128, [1, 256]]),
                       start=(ec == 0), stop=(ec == FC - 1))
                pe_wo = em.cnt["pe"]
                em.wait("vector", "pe", pe_wo)
                dv_wo = em.emit("vector", lambda e, s, mc=mc, tg0=tg0, B=B: e.tensor_add(
                    pA(acc, 0, mc * T + tg0, [128, [1, 256]]),
                    pA(acc, 0, mc * T + tg0, [128, [1, 256]]),
                    pA(B, 0, 0, [128, [1, 256]])), inc="dv")
                bk.release(key, "dv", dv_wo)
    slot_used("A")
    slot_used("B")
    dv_attn = em.cnt["dv"]
    if "accA" in tap_aps:
        em.wait("sync", "dv", dv_attn)
        dma("sync", tap_aps["accA"][:, :], bass.AP(acc, 0, [[FC * T, 128], [1, FC * T]]))

    if limit <= 8:
        return finish()
    prev_smul = 0
    for blk in range(4):
        l1 = wload(slotA[:, :],
                   bass.AP(sw1.tensor, blk * 512, [[2048, 128], [262144, 8], [1, 512]]),
                   slot="A")
        l3 = wload(slotB[:, :],
                   bass.AP(sw3.tensor, blk * 512, [[2048, 128], [262144, 8], [1, 512]]),
                   slot="B")
        l2 = wload(sw2S[:, :],
                   bass.AP(sw2.tensor, blk * 512 * 1024, [[1024, 128], [131072, 4], [1, 1024]]),
                   wait_pe=em.cnt["pe"], slot="S")
        wwait("tensor", l1, l3)
        for tg in range(NG):
            for hcl in range(4):
                hc = blk * 4 + hcl
                bk.acquire("tensor", "P01a")
                for fc in range(FC):
                    MM(pA(slotA, 0, fc * 512 + hcl * 128, [128, [1, 128]]),
                       pA(xn, 0, fc * T + tg * 512, [128, [1, 512]]),
                       pA(P01, 0, 0, [128, [1, 512]]),
                       start=(fc == 0), stop=(fc == FC - 1))
                pe_g = em.cnt["pe"]
                bk.acquire("tensor", "P01b")
                for fc in range(FC):
                    MM(pA(slotB, 0, fc * 512 + hcl * 128, [128, [1, 128]]),
                       pA(xn, 0, fc * T + tg * 512, [128, [1, 512]]),
                       pA(P01, 0, 512, [128, [1, 512]]),
                       start=(fc == 0), stop=(fc == FC - 1))
                pe_u = em.cnt["pe"]
                if tg == NG - 1 and hcl == 3:
                    slot_free["A"] = slot_free["B"] = pe_u
                em.wait("scalar", "pe", pe_g)
                em.wait("scalar", "dv", prev_smul)
                ag = em.emit("scalar", lambda e, s, hc=hc: e.activation(
                    s1S[:, :], pA(P01, 0, 0, [128, [1, 512]]), AF.Silu,
                    bias=sb1s[:, hc:hc + 1]), inc="ac")
                bk.release("P01a", "ac", ag)
                em.wait("scalar", "pe", pe_u)
                au = em.emit("scalar", lambda e, s, hc=hc: e.activation(
                    u2S[:, :], pA(P01, 0, 512, [128, [1, 512]]), AF.Identity,
                    bias=sb3s[:, hc:hc + 1], scale=0.5), inc="ac")
                bk.release("P01b", "ac", au)
                em.wait("vector", "ac", au)
                prev_smul = em.emit("vector", lambda e, s, hcl=hcl: e.tensor_mul(
                    pA(sbkS, 0, hcl * 512, [128, [1, 512]]),
                    s1S[:, :], u2S[:, :]), inc="dv")
            dv_s = em.cnt["dv"]
            em.wait("tensor", "dv", dv_s)
            if tg == 0:
                wwait("tensor", l2)
            for mc in range(8):
                key = "P23a" if mc % 2 == 0 else "P23b"
                bk.acquire("tensor", key)
                for hcl in range(4):
                    MM(pA(sw2S, 0, hcl * 1024 + mc * 128, [128, [1, 128]]),
                       pA(sbkS, 0, hcl * 512, [128, [1, 512]]),
                       pA(P23, 0, 512 * (mc % 2), [128, [1, 512]]),
                       start=(hcl == 0), stop=(hcl == 3))
                pe_y = em.cnt["pe"]
                em.wait("vector", "pe", pe_y)
                dv_y = em.emit("vector", lambda e, s, mc=mc, tg=tg: e.tensor_add(
                    pA(acc, 0, mc * T + tg * 512, [128, [1, 512]]),
                    pA(acc, 0, mc * T + tg * 512, [128, [1, 512]]),
                    pA(P23, 0, 512 * (mc % 2), [128, [1, 512]])), inc="dv")
                bk.release(key, "dv", dv_y)
    dv_shared = em.cnt["dv"]
    if "accS" in tap_aps:
        em.wait("sync", "dv", dv_shared)
        dma("sync", tap_aps["accS"][:, :], bass.AP(acc, 0, [[FC * T, 128], [1, FC * T]]))

    if limit <= 9:
        return finish()
    def moe_slot_full(key):
        return {"A": slotA, "B": slotB, "X": wxS}[key][:, :]

    def moe_tile(key, coloff, n):
        t = {"A": slotA, "B": slotB, "X": wxS}[key]
        return pA(t, 0, coloff, [128, [1, n]])

    def wsrc_half(wt, e_i, half):
        return bass.AP(wt.tensor, e_i * (D * D) + half * 524288,
                       [[1024, 128], [131072, 4], [1, 1024]])

    halves = []
    for e_i in range(NE):
        halves += [(e_i, "w1", 0), (e_i, "w1", 1), (e_i, "w2", 0), (e_i, "w2", 1)]
    ROT = ["A", "B", "X"]
    slot_of = {hv: ROT[i % 3] for i, hv in enumerate(halves)}
    loaded = {}
    free_pe = {"A": em.cnt["pe"], "B": em.cnt["pe"], "X": em.cnt["pe"]}
    next_load = 0

    def emit_loads(upto):
        nonlocal next_load
        while next_load < upto:
            hv = halves[next_load]
            key = slot_of[hv]
            wt = w1 if hv[1] == "w1" else w2
            loaded[hv] = wload(moe_slot_full(key), wsrc_half(wt, hv[0], hv[2]),
                               wait_pe=free_pe[key], slot=key)
            next_load += 1

    emit_loads(2)
    hbanks = [(P01, 0, "P01a"), (P01, 512, "P01b"), (P23, 0, "P23a"), (P23, 512, "P23b")]
    for e_i in range(NE):
        r0, r1 = (3 * e_i) % 8, (3 * e_i + 3) % 8
        for ci, r in enumerate((r0, r1)):
            wwait("tensor", loaded[(e_i, "w1", 0)], loaded[(e_i, "w1", 1)])
            for hc in range(FC):
                b, o, key = hbanks[(ci * FC + hc) % 4]
                bk.acquire("tensor", key)
                for fc in range(FC):
                    half = fc // 4
                    MM(moe_tile(slot_of[(e_i, "w1", half)], (fc % 4) * 1024 + hc * 128, 128),
                       bass.AP(xn, fc * T + r, [[FC * T, 128], [8, 256]]),
                       pA(b, 0, o, [128, [1, 256]]),
                       start=(fc == 0), stop=(fc == FC - 1))
                pe_h = em.cnt["pe"]
                em.wait("scalar", "pe", pe_h)
                ah = em.emit("scalar", lambda e, s, b=b, o=o, hc=hc, ci=ci, e_i=e_i: e.activation(
                    pA(heS, 0, hc * 512 + ci * 256, [128, [1, 256]]),
                    pA(b, 0, o, [128, [1, 256]]), AF.Gelu,
                    bias=b1s[:, e_i * 8 + hc:e_i * 8 + hc + 1]), inc="ac")
                bk.release(key, "ac", ah)
        free_pe[slot_of[(e_i, "w1", 0)]] = em.cnt["pe"]
        free_pe[slot_of[(e_i, "w1", 1)]] = em.cnt["pe"]
        emit_loads(4 * e_i + 4)
        ac_h = em.cnt["ac"]
        em.wait("tensor", "ac", ac_h)
        for ci, r in enumerate((r0, r1)):
            wwait("tensor", loaded[(e_i, "w2", 0)], loaded[(e_i, "w2", 1)])
            for mc in range(8):
                key = "P45a" if mc % 2 == 0 else "P45b"
                bk.acquire("tensor", key)
                for hc in range(FC):
                    half = hc // 4
                    MM(moe_tile(slot_of[(e_i, "w2", half)], (hc % 4) * 1024 + mc * 128, 128),
                       pA(heS, 0, hc * 512 + ci * 256, [128, [1, 256]]),
                       pA(P45, 0, 512 * (mc % 2), [128, [1, 256]]),
                       start=(hc == 0), stop=(hc == FC - 1))
                pe_y = em.cnt["pe"]
                em.wait("vector", "pe", pe_y)
                em.emit("vector", lambda e, s, mc=mc: e.tensor_scalar_mul(
                    tmpM[:, :], pA(P45, 0, 512 * (mc % 2), [128, [1, 256]]), 0.25), inc="dv")
                dv_y = em.emit("vector", lambda e, s, mc=mc, r=r: e.tensor_add(
                    pA(acc, 0, mc * T + r, [128, [8, 256]]),
                    pA(acc, 0, mc * T + r, [128, [8, 256]]),
                    tmpM[:, :]), inc="dv")
                bk.release(key, "dv", dv_y)
        free_pe[slot_of[(e_i, "w2", 0)]] = em.cnt["pe"]
        free_pe[slot_of[(e_i, "w2", 1)]] = em.cnt["pe"]
        emit_loads(min(4 * e_i + 6, len(halves)))

    dv_all = em.cnt["dv"]
    em.wait("sync", "dv", dv_all)
    dfinal = dma("sync", bass.AP(out.tensor, 0, [[T, 128], [128 * T, 8], [1, T]]),
                 bass.AP(acc, 0, [[FC * T, 128], [T, 8], [1, T]]))
    if "back" in tap_aps:
        em.wait("sync", "pe", em.cnt["pe"])
        dfinal = dma("sync", tap_aps["back"][:, :],
                     bass.AP(backing, 0, [[BACK, 128], [1, BACK]]))
    if "xn" in tap_aps:
        for q in range(4):
            em.wait("vector", "pe", em.cnt["pe"])
            dvq = em.emit("vector", lambda e, s, q=q: e.tensor_copy(
                bass.AP(stg, 0, [[4096, 128], [1, 4096]]),
                bass.AP(xn, q * 4096, [[FC * T, 128], [1, 4096]])), inc="dv")
            em.wait("sync", "dv", dvq)
            dt = dma("sync", bass.AP(tap_aps["xn"].tensor, q * 4096,
                                     [[FC * T, 128], [1, 4096]]),
                     bass.AP(stg, 0, [[4096, 128], [1, 4096]]))
            em.wait("vector", "d", dt)
    em.wait("sync", "d", dfinal)

    semctxs, sems = [], {}
    for ch in em.cnt:
        cm = nc.semaphore("sem_" + ch)
        semctxs.append(cm)
        sems[ch] = cm.__enter__()
    blk_cm = nc.Block()
    block = blk_cm.__enter__()
    for eng in EM.ENGS:
        stream = em.streams[eng]
        getattr(block, eng)(lambda e, stream=stream: [f(e, sems) for f in stream])
    blk_cm.__exit__(None, None, None)
    for cm in semctxs:
        cm.__exit__(None, None, None)
    for cm in reversed(ctxs):
        cm.__exit__(None, None, None)
    return nc


_CACHE = {}


def _get_nc(taps=()):
    key = tuple(taps)
    if key not in _CACHE:
        _CACHE[key] = build_nc(taps=key)
    return _CACHE[key]


def host_inputs(full):
    f32 = np.float32
    x = np.asarray(full["L0_attn_x"], f32)
    shared = {
        "wqT": np.ascontiguousarray(np.asarray(full["L0_attn_Wq"], f32).T),
        "wkT": np.ascontiguousarray(np.asarray(full["L0_attn_Wk"], f32).T),
        "wvT": np.ascontiguousarray(np.asarray(full["L0_attn_Wv"], f32).T),
        "woT": np.ascontiguousarray(np.asarray(full["L0_attn_Wo"], f32).T),
        "sw1": np.ascontiguousarray(np.asarray(full["L0_ffn_sw1"], f32)),
        "sw3": np.ascontiguousarray(np.asarray(full["L0_ffn_sw3"], f32)),
        "sw2": np.ascontiguousarray(np.asarray(full["L0_ffn_sw2"], f32)),
        "w1": np.ascontiguousarray(np.asarray(full["L0_ffn_w1"], f32)),
        "w2": np.ascontiguousarray(np.asarray(full["L0_ffn_w2"], f32)),
        "sb1c": np.ascontiguousarray(np.asarray(full["L0_ffn_sb1"], f32).reshape(16, 128).T),
        "sb3c": np.ascontiguousarray(np.asarray(full["L0_ffn_sb3"], f32).reshape(16, 128).T),
        "sb2c": np.ascontiguousarray(np.asarray(full["L0_ffn_sb2"], f32).reshape(8, 128).T),
        "b1c": np.ascontiguousarray(np.asarray(full["L0_ffn_b1"], f32).reshape(NE, 8, 128).transpose(0, 2, 1)),
        "b2c": np.ascontiguousarray(np.asarray(full["L0_ffn_b2"], f32).reshape(NE, 8, 128).transpose(0, 2, 1)),
        "tmaskD": np.ascontiguousarray(np.triu(np.ones((128, 128), f32))),
        "eyeD": np.eye(128, dtype=f32),
        "onesD": np.ones((128, 128), f32),
    }
    zeros_prev = np.zeros((D, T), f32)
    in_maps = []
    for c in range(8):
        b, half = c // 2, c % 2
        m = dict(shared)
        m["xin"] = np.ascontiguousarray(x[b, half * T:(half + 1) * T].T)
        m["xprev"] = (np.ascontiguousarray(x[b, 0:T].T) if half == 1 else zeros_prev)
        m["maskA"] = np.full((128, 1), float(half), f32)
        in_maps.append(m)
    return in_maps


def kernel(**inputs):
    nc = _get_nc(())
    in_maps = host_inputs(inputs)
    res = run_bass_kernel_spmd(nc, in_maps, core_ids=list(range(8)))
    outp = np.empty((4, 4096, 1024), np.float32)
    for c in range(8):
        b, half = c // 2, c % 2
        outp[b, half * T:(half + 1) * T] = res.results[c]["out"].T
    return outp


# version 12
# speedup vs baseline: 1.0110x; 1.0110x over previous
import numpy as np
import concourse.bass as bass
import concourse.mybir as mybir
from concourse.bass_utils import run_bass_kernel_spmd

F32 = mybir.dt.float32
F32R = mybir.dt.float32r
AF = mybir.ActivationFunctionType

D = 1024
T = 2048
H = 16
FC = 8
NSC = 16
NG = 4
HS = 2048
NE = 8
EPS_ATT = 1e-8
EPS_LN = 1e-5
BACK = 11016
PS = 2568


class EM:
    ENGS = ("sync", "scalar", "vector", "tensor", "gpsimd")

    def __init__(self):
        self.streams = {e: [] for e in self.ENGS}
        self.cnt = {ch: 0 for ch in ("w", "wA", "wB", "wX", "wS", "d", "e", "pe", "dv", "ac")}
        self.lastwait = {e: {ch: 0 for ch in self.cnt} for e in self.ENGS}

    def emit(self, eng, fn, inc=None, k=1):
        if inc is not None:
            if eng in ("vector", "scalar"):
                self.wait(eng, inc, self.cnt[inc])
            self.cnt[inc] += k
            def wrapped(e, sems, fn=fn, inc=inc, k=k):
                fn(e, sems).then_inc(sems[inc], k)
            self.streams[eng].append(wrapped)
            return self.cnt[inc]
        self.streams[eng].append(lambda e, sems, fn=fn: fn(e, sems))
        return None

    def wait(self, eng, ch, val):
        if val is None or val <= 0:
            return
        assert val <= self.cnt[ch], (eng, ch, val, self.cnt[ch])
        if self.lastwait[eng][ch] >= val:
            return
        self.lastwait[eng][ch] = val
        self.streams[eng].append(
            lambda e, sems, ch=ch, val=val: e.wait_ge(sems[ch], val))


class Banks:
    def __init__(self, em):
        self.em = em
        self.free_at = {}

    def acquire(self, eng, *keys):
        for key in keys:
            ent = self.free_at.get(key)
            if ent is not None:
                self.em.wait(eng, ent[0], ent[1])

    def release(self, key, ch, val):
        self.free_at[key] = (ch, val)


SCAN_N = NSC
APH_BC = True
APH_NRM = True
APH_PROJ = True
APH_STATE = True
SCAN_WO = True
SCAN_ATR = True
SCAN_AT = True
SCAN_OT = True
SCAN_DEN = True
SCAN_DELTA = True
SCAN_KTR = True


def build_nc(taps=(), limit=99):
    nc = bass.Bass("TRN2", target_bir_lowering=False)

    xin = nc.dram_tensor("xin", [D, T], F32, kind="ExternalInput").ap()
    xprev = nc.dram_tensor("xprev", [D, T], F32, kind="ExternalInput").ap()
    maskA = nc.dram_tensor("maskA", [128, 1], F32, kind="ExternalInput").ap()
    wqT = nc.dram_tensor("wqT", [D, D], F32, kind="ExternalInput").ap()
    wkT = nc.dram_tensor("wkT", [D, D], F32, kind="ExternalInput").ap()
    wvT = nc.dram_tensor("wvT", [D, D], F32, kind="ExternalInput").ap()
    woT = nc.dram_tensor("woT", [D, D], F32, kind="ExternalInput").ap()
    sw1 = nc.dram_tensor("sw1", [D, HS], F32, kind="ExternalInput").ap()
    sw3 = nc.dram_tensor("sw3", [D, HS], F32, kind="ExternalInput").ap()
    sw2 = nc.dram_tensor("sw2", [HS, D], F32, kind="ExternalInput").ap()
    w1 = nc.dram_tensor("w1", [NE, D, D], F32, kind="ExternalInput").ap()
    w2 = nc.dram_tensor("w2", [NE, D, D], F32, kind="ExternalInput").ap()
    sb1c = nc.dram_tensor("sb1c", [128, 16], F32, kind="ExternalInput").ap()
    sb3c = nc.dram_tensor("sb3c", [128, 16], F32, kind="ExternalInput").ap()
    sb2c = nc.dram_tensor("sb2c", [128, 8], F32, kind="ExternalInput").ap()
    b1c = nc.dram_tensor("b1c", [NE, 128, 8], F32, kind="ExternalInput").ap()
    b2c = nc.dram_tensor("b2c", [NE, 128, 8], F32, kind="ExternalInput").ap()
    tmaskD = nc.dram_tensor("tmaskD", [128, 128], F32, kind="ExternalInput").ap()
    eyeD = nc.dram_tensor("eyeD", [128, 128], F32, kind="ExternalInput").ap()
    onesD = nc.dram_tensor("onesD", [128, 128], F32, kind="ExternalInput").ap()
    out = nc.dram_tensor("out", [D, T], F32, kind="ExternalOutput").ap()

    spk = "ExternalOutput" if "spills" in taps else "Internal"
    q_sp = nc.dram_tensor("q_sp", [NG, FC, 128, 512], F32, kind=spk).ap()
    k_sp = nc.dram_tensor("k_sp", [NG, FC, 128, 512], F32, kind=spk).ap()
    v_sp = nc.dram_tensor("v_sp", [NSC, 128, 1040], F32, kind=spk).ap()

    tap_aps = {}
    for tname in taps:
        if tname == "spills":
            continue
        cols = BACK if tname == "back" else FC * T
        tap_aps[tname] = nc.dram_tensor(
            "tap_" + tname, [128, cols], F32, kind="ExternalOutput").ap()

    em = EM()
    bk = Banks(em)
    ctxs = []

    def sbuf(name, shape, dtype):
        cm = nc.sbuf_tensor(name, shape, dtype)
        ctxs.append(cm)
        return cm.__enter__()

    def psum(name, shape):
        cm = nc.psum_tensor(name, shape, F32)
        ctxs.append(cm)
        return cm.__enter__()

    xn = sbuf("xn", [128, FC * T], F32R)
    acc = sbuf("acc", [128, FC * T], F32)
    slotA = sbuf("slotA", [128, 4096], F32R)
    slotB = sbuf("slotB", [128, 4096], F32R)
    consts = sbuf("consts", [128, 128], F32)
    tmask = sbuf("tmask", [128, 128], F32)
    eyeS = sbuf("eyeS", [128, 128], F32)
    maskS = sbuf("maskS", [128, 1], F32)
    sb1s = sbuf("sb1s", [128, 16], F32)
    sb3s = sbuf("sb3s", [128, 16], F32)
    sb2s = sbuf("sb2s", [128, 8], F32)
    b1s = sbuf("b1s", [128, NE * 8], F32)
    b2s = sbuf("b2s", [128, NE * 8], F32)
    b2cl = sbuf("b2cl", [128, NE * 8], F32)
    epss = sbuf("epss", [128, 1], F32)
    onesR = sbuf("onesR", [128, 2], F32R)
    backing = sbuf("backing", [128, BACK], F32)
    base = nc.lookup_mloc(backing).addr

    def overlay(name, cols, dtype, coloff):
        return nc.alloc_sbuf_tensor_at(name, [128, cols], dtype,
                                       offset=base + coloff * 4)

    stS = overlay("stS", 520, F32, 0)
    stsA = overlay("stsA", 2048, F32, 520)
    stsB = overlay("stsB", 2048, F32, 520)
    xaS = overlay("xaS", 4096, F32, PS)
    sqA = overlay("sqA", 1024, F32R, PS + 4096)
    tmpA = overlay("tmpA", 1024, F32, PS + 5120)
    xaF = overlay("xaF", 1024, F32, PS)
    xnA = overlay("xnA", 1024, F32R, PS + 1024)
    teA = overlay("teA", 512, F32, PS + 2048)
    ktA = overlay("ktA", 512, F32, PS + 2560)
    vtA = overlay("vtA", 520, F32, PS + 3072)
    sqB = overlay("sqB", 1024, F32R, PS)
    tmpB = overlay("tmpB", 1024, F32, PS + 1024)
    nrmB = overlay("nrmB", 4096, F32, PS + 2048)
    stg = overlay("stg", 4096, F32, PS)
    trS = overlay("trS", 512, F32, PS + 4096)
    vstg = overlay("vstg", 2080, F32, PS + 4608)
    qS = overlay("qS", 1024, F32, PS)
    kfS = overlay("kfS", 1024, F32, PS + 1024)
    vtS = overlay("vtS", 1040, F32, PS + 2048)
    ktS = overlay("ktS", 1024, F32, PS + 3088)
    amS = overlay("amS", 1024, F32, PS + 4112)
    atkS = overlay("atkS", 1024, F32, PS + 5136)
    denS = overlay("denS", 32, F32, PS + 6160)
    afmS = overlay("afmS", 2048, F32R, PS + 6192)
    sbkS = overlay("sbkS", 2048, F32R, PS)
    s1S = overlay("s1S", 512, F32, PS + 2048)
    u2S = overlay("u2S", 512, F32, PS + 2560)
    sw2S = overlay("sw2S", 4096, F32R, PS + 3072)
    wxS = overlay("wxS", 4096, F32R, PS)
    heS = overlay("heS", 4096, F32R, PS + 4096)
    tmpM = overlay("tmpM", 256, F32, PS + 8192)

    P01 = psum("P01", [128, 1024])
    P23 = psum("P23", [128, 1024])
    P45 = psum("P45", [128, 1024])
    P6 = psum("P6", [128, 512])
    P7 = psum("P7", [128, 512])

    def pA(t, p0, c0, dims):
        stride = t.shape[1]
        return bass.AP(t, p0 * stride + c0, [[stride, dims[0]]] + dims[1:])

    def dma(eng, dst, src, ch=None):
        if ch is None:
            ch = "w" if eng == "gpsimd" else "d"
        return em.emit(eng, lambda e, s, d=dst, sr=src: e.dma_start(out=d, in_=sr),
                       inc=ch, k=16)

    def MM(lhsT, rhs, dst, start, stop):
        return em.emit("tensor", lambda e, s, l=lhsT, r=rhs, d=dst, st=start, sp=stop:
                       e.matmul(d, l, r, start=st, stop=sp), inc="pe")

    slot_free = {"A": 0, "B": 0}

    def wload(dst_ap, src_ap, wait_pe=None, slot=None):
        if wait_pe is None:
            wait_pe = slot_free.get(slot, em.cnt["pe"]) if slot else em.cnt["pe"]
        em.wait("gpsimd", "pe", wait_pe)
        ch = ("w" + slot) if slot in ("A", "B", "X", "S") else "w"
        return (ch, dma("gpsimd", dst_ap, src_ap, ch=ch))

    def slot_used(slot):
        slot_free[slot] = em.cnt["pe"]

    def wwait(eng, *tokens):
        for ch, val in tokens:
            em.wait(eng, ch, val)

    def whalf(wt, eg):
        return bass.AP(wt.tensor, eg * 512, [[1024, 128], [131072, 8], [1, 512]])

    def finish():
        dv_all = em.cnt["dv"]
        em.wait("sync", "dv", dv_all)
        em.wait("sync", "pe", em.cnt["pe"])
        em.wait("sync", "ac", em.cnt["ac"])
        dfin = dma("sync", bass.AP(out.tensor, 0, [[T, 128], [128 * T, 8], [1, T]]),
                   bass.AP(acc, 0, [[FC * T, 128], [T, 8], [1, T]]))
        if "back" in tap_aps:
            dfin = dma("sync", tap_aps["back"][:, :],
                       bass.AP(backing, 0, [[BACK, 128], [1, BACK]]))
        em.wait("sync", "d", dfin)
        semctxs, sems = [], {}
        for ch in em.cnt:
            cm = nc.semaphore("sem_" + ch)
            semctxs.append(cm)
            sems[ch] = cm.__enter__()
        blk_cm = nc.Block()
        block = blk_cm.__enter__()
        for eng in EM.ENGS:
            stream = em.streams[eng]
            getattr(block, eng)(lambda e, stream=stream: [f(e, sems) for f in stream])
        blk_cm.__exit__(None, None, None)
        for cm in semctxs:
            cm.__exit__(None, None, None)
        for cm in reversed(ctxs):
            cm.__exit__(None, None, None)
        return nc

    dma("gpsimd", consts[:, :], onesD)
    dma("gpsimd", tmask[:, :], tmaskD)
    dma("gpsimd", eyeS[:, :], eyeD)
    dma("gpsimd", maskS[:, :], maskA)
    dma("gpsimd", sb1s[:, :], sb1c)
    dma("gpsimd", sb3s[:, :], sb3c)
    dma("gpsimd", sb2s[:, :], sb2c)
    dma("gpsimd", b1s[:, :], bass.AP(b1c.tensor, 0, [[8, 128], [1024, 8], [1, 8]]))
    dma("gpsimd", onesR[:, :], bass.AP(onesD.tensor, 0, [[128, 128], [1, 2]]))
    wconst = dma("gpsimd", b2s[:, :], bass.AP(b2c.tensor, 0, [[8, 128], [1024, 8], [1, 8]]))

    dx = dma("sync", bass.AP(acc, 0, [[FC * T, 128], [T, 8], [1, T]]),
             bass.AP(xin.tensor, 0, [[T, 128], [128 * T, 8], [1, T]]), ch="e")

    em.wait("vector", "w", wconst)
    em.emit("vector", lambda e, s: e.memset(epss[:, :], EPS_LN), inc="dv")
    em.emit("vector", lambda e, s: e.tensor_scalar_mul(sb3s[:, :], sb3s[:, :], 0.5), inc="dv")
    em.emit("vector", lambda e, s: e.tensor_scalar_mul(sb2s[:, :], sb2s[:, :], 0.5), inc="dv")
    for r in range(8):
        e0, e1 = (3 * r) % 8, (3 * r - 1) % 8
        em.emit("vector", lambda e, s, r=r, e0=e0, e1=e1: e.tensor_add(
            b2cl[:, r * 8:(r + 1) * 8], b2s[:, e0 * 8:(e0 + 1) * 8],
            b2s[:, e1 * 8:(e1 + 1) * 8]), inc="dv")
    em.emit("vector", lambda e, s: e.tensor_scalar_mul(b2cl[:, :], b2cl[:, :], 0.25), inc="dv")
    em.emit("vector", lambda e, s: e.memset(stS[:, :], 0.0), inc="dv")
    dv_sinit = em.cnt["dv"]

    BPAIR = [((P01, 0, "P01a"), (P01, 512, "P01b")),
             ((P23, 0, "P23a"), (P23, 512, "P23b")),
             ((P45, 0, "P45a"), (P45, 512, "P45b")),
             ((P6, 0, "P6"), (P7, 0, "P7"))]

    def emit_stats(st, sq, tmp, src_chunk, pre=None):
        sq_mm = [0, 0]
        for g in range(NG):
            (bS, oS, kS), (bQ, oQ, kQ) = BPAIR[g]
            if pre is not None:
                dn = pre(g)
                em.wait("tensor", "d", dn)
                em.wait("scalar", "d", dn)
            bk.acquire("tensor", kS, kQ)
            for fc in range(FC):
                par = fc % 2
                src = src_chunk(g, fc)
                MM(consts[:, 0:1], src, pA(bS, 0, oS, [1, [1, 512]]),
                   start=(fc == 0), stop=(fc == FC - 1))
                em.wait("scalar", "pe", sq_mm[par])
                an = em.emit("scalar", lambda e, s, src=src, par=par: e.activation(
                    pA(sq, 0, par * 512, [128, [1, 512]]), src, AF.Square), inc="ac")
                em.wait("tensor", "ac", an)
                sq_mm[par] = MM(onesR[:, :], pA(sq, 0, par * 512, [128, [1, 512]]),
                                pA(bQ, 0, oQ, [2, [1, 512]]),
                                start=(fc == 0), stop=(fc == FC - 1))
        pe_done = em.cnt["pe"]
        em.wait("vector", "pe", pe_done)
        for g in range(NG):
            (bS, oS, kS), (bQ, oQ, kQ) = BPAIR[g]
            em.emit("vector", lambda e, s, bS=bS, oS=oS, g=g: e.tensor_scalar_mul(
                pA(st, 0, g * 512, [1, [1, 512]]),
                pA(bS, 0, oS, [1, [1, 512]]), 1.0 / D), inc="dv")
            dvq = em.emit("vector", lambda e, s, bQ=bQ, oQ=oQ: e.tensor_scalar_mul(
                pA(tmp, 0, 0, [1, [1, 512]]),
                pA(bQ, 0, oQ, [1, [1, 512]]), 1.0 / D), inc="dv")
            bk.release(kS, "dv", dvq)
            bk.release(kQ, "dv", dvq)
            em.emit("vector", lambda e, s, g=g: e.tensor_mul(
                pA(tmp, 0, 512, [1, [1, 512]]),
                pA(st, 0, g * 512, [1, [1, 512]]),
                pA(st, 0, g * 512, [1, [1, 512]])), inc="dv")
            dvv = em.emit("vector", lambda e, s: e.tensor_sub(
                pA(tmp, 0, 0, [1, [1, 512]]),
                pA(tmp, 0, 0, [1, [1, 512]]),
                pA(tmp, 0, 512, [1, [1, 512]])), inc="dv")
            em.wait("scalar", "dv", dvv)
            an = em.emit("scalar", lambda e, s: e.activation(
                pA(tmp, 0, 512, [1, [1, 512]]),
                pA(tmp, 0, 0, [1, [1, 512]]), AF.Sqrt, bias=epss[0:1, 0:1]), inc="ac")
            em.wait("vector", "ac", an)
            em.emit("vector", lambda e, s, g=g: e.reciprocal(
                pA(st, 32, g * 512, [1, [1, 512]]),
                pA(tmp, 0, 512, [1, [1, 512]])), inc="dv")
            em.emit("vector", lambda e, s: e.drain())
        return em.cnt["dv"]

    if limit <= 0:
        return finish()
    def a1_pre(g):
        if g > 0:
            em.wait("sync", "pe", em.cnt["pe"])
            em.wait("sync", "ac", em.cnt["ac"])
        return dma("sync", bass.AP(xaS, 0, [[4096, 128], [512, 8], [1, 512]]),
                   bass.AP(xprev.tensor, g * 512, [[T, 128], [128 * T, 8], [1, 512]]))

    em.wait("tensor", "w", wconst)
    dvA = emit_stats(stsA, sqA, tmpA,
                     lambda g, fc: pA(xaS, 0, fc * 512, [128, [1, 512]]),
                     pre=a1_pre)

    if limit <= 1:
        return finish()
    em.wait("gpsimd", "dv", dvA)
    em.wait("tensor", "dv", dvA)
    dv_prev_state = dv_sinit
    pe_prev_proj = em.cnt["pe"]
    em.wait("sync", "ac", em.cnt["ac"])
    for eg in range(2):
        wk = wload(slotA[:, :], whalf(wkT, eg), slot="A")
        wv = wload(slotB[:, :], whalf(wvT, eg), slot="B")
        wwait("tensor", wk, wv)
        for sc in range(NSC):
            t0 = sc * 128
            em.wait("sync", "pe", pe_prev_proj)
            dn = dma("sync", bass.AP(xaF, 0, [[1024, 128], [128, 8], [1, 128]]),
                     bass.AP(xprev.tensor, t0, [[T, 128], [128 * T, 8], [1, 128]]))
            if not APH_BC:
                continue
            bk.acquire("tensor", "P6", "P7")
            MM(consts[0:1, :], pA(stsA, 0, t0, [1, [1, 128]]),
               pA(P6, 0, 0, [128, [1, 128]]), True, True)
            bmr = MM(consts[32:33, :], pA(stsA, 32, t0, [1, [1, 128]]),
                     pA(P7, 0, 0, [128, [1, 128]]), True, True)
            if not APH_NRM:
                bk.release("P6", "pe", bmr)
                bk.release("P7", "pe", bmr)
                continue
            em.wait("vector", "d", dn)
            em.wait("vector", "pe", bmr)
            em.emit("vector", lambda e, s: e.tensor_sub(
                pA(xaF, 0, 0, [128, [128, 8], [1, 128]]),
                pA(xaF, 0, 0, [128, [128, 8], [1, 128]]),
                pA(P6, 0, 0, [128, [0, 8], [1, 128]])), inc="dv")
            dvn = em.emit("vector", lambda e, s: e.tensor_mul(
                pA(xnA, 0, 0, [128, [128, 8], [1, 128]]),
                pA(xaF, 0, 0, [128, [128, 8], [1, 128]]),
                pA(P7, 0, 0, [128, [0, 8], [1, 128]])), inc="dv")
            bk.release("P6", "dv", dvn)
            bk.release("P7", "dv", dvn)
            em.wait("tensor", "dv", dvn)
            if not APH_PROJ:
                continue
            bk.acquire("tensor", "P01a")
            for fc in range(FC):
                MM(pA(xnA, 0, fc * 128, [128, [1, 128]]),
                   pA(slotA, 0, fc * 512, [128, [1, 512]]),
                   pA(P01, 0, 0, [128, [1, 512]]),
                   start=(fc == 0), stop=(fc == FC - 1))
            pe_k = em.cnt["pe"]
            bk.acquire("tensor", "P23a")
            for fc in range(FC):
                MM(pA(xnA, 0, fc * 128, [128, [1, 128]]),
                   pA(slotB, 0, fc * 512, [128, [1, 512]]),
                   pA(P23, 0, 0, [128, [1, 512]]),
                   start=(fc == 0), stop=(fc == FC - 1))
            pe_v = em.cnt["pe"]
            pe_prev_proj = pe_v
            em.wait("scalar", "pe", pe_k)
            a1 = em.emit("scalar", lambda e, s: e.activation(
                teA[:, :], pA(P01, 0, 0, [128, [1, 512]]), AF.Exp), inc="ac")
            em.wait("vector", "ac", a1)
            em.emit("vector", lambda e, s: e.tensor_scalar_min(
                teA[:, :], teA[:, :], 1.0), inc="dv")
            em.emit("vector", lambda e, s: e.tensor_scalar_max(
                ktA[:, :], pA(P01, 0, 0, [128, [1, 512]]), 0.0), inc="dv")
            dvk = em.emit("vector", lambda e, s: e.tensor_add(
                ktA[:, :], ktA[:, :], teA[:, :]), inc="dv")
            bk.release("P01a", "dv", dvk)
            em.wait("vector", "pe", pe_v)
            em.emit("vector", lambda e, s: e.tensor_copy(
                pA(vtA, 0, 0, [128, [65, 8], [1, 64]]),
                pA(P23, 0, 0, [128, [64, 8], [1, 64]])), inc="dv")
            dvv = em.emit("vector", lambda e, s: e.memset(
                pA(vtA, 0, 64, [128, [65, 8], [1, 1]]), 1.0), inc="dv")
            bk.release("P23a", "dv", dvv)
            if not APH_STATE:
                continue
            em.wait("tensor", "dv", max(dvk, dvv, dv_prev_state))
            bk.acquire("tensor", "P45a")
            for hl in range(8):
                MM(pA(ktA, 0, (hl // 2) * 128 + 64 * (hl % 2), [128, [1, 64]]),
                   pA(vtA, 0, hl * 65, [128, [1, 65]]),
                   pA(P45, 64 * (hl % 2), 128 * (hl // 2), [64, [1, 65]]),
                   True, True)
            pe_s = em.cnt["pe"]
            em.wait("vector", "pe", pe_s)
            dv_prev_state = em.emit("vector", lambda e, s, eg=eg: e.tensor_add(
                pA(stS, 0, 65 * 4 * eg, [128, [65, 4], [1, 65]]),
                pA(stS, 0, 65 * 4 * eg, [128, [65, 4], [1, 65]]),
                pA(P45, 0, 0, [128, [128, 4], [1, 65]])), inc="dv")
            bk.release("P45a", "dv", dv_prev_state)
        slot_used("A")
        slot_used("B")
    em.wait("vector", "dv", dv_prev_state)
    dv_state = em.emit("vector", lambda e, s: e.tensor_scalar_mul(
        stS[:, :], stS[:, :], maskS[:, 0:1]), inc="dv")

    if limit <= 2:
        return finish()
    em.wait("vector", "e", dx)
    em.wait("tensor", "e", dx)
    em.wait("scalar", "e", dx)
    em.wait("scalar", "pe", em.cnt["pe"])
    d_all = em.cnt["d"]
    dv_stats = emit_stats(stsB, sqB, tmpB,
                          lambda g, fc: pA(acc, 0, fc * T + g * 512, [128, [1, 512]]),
                          pre=lambda g: d_all)
    em.wait("tensor", "dv", dv_stats)
    for g in range(NG):
        B = P01 if g % 2 == 0 else P23
        ka, kb = ("P01a", "P01b") if g % 2 == 0 else ("P23a", "P23b")
        bk.acquire("tensor", ka, kb)
        MM(consts[0:1, :], pA(stsB, 0, g * 512, [1, [1, 512]]),
           pA(B, 0, 0, [128, [1, 512]]), True, True)
        bmr = MM(consts[32:33, :], pA(stsB, 32, g * 512, [1, [1, 512]]),
                 pA(B, 0, 512, [128, [1, 512]]), True, True)
        em.wait("vector", "pe", bmr)
        em.emit("vector", lambda e, s, g=g, B=B: e.tensor_sub(
            pA(nrmB, 0, 0, [128, [512, 8], [1, 512]]),
            pA(acc, 0, g * 512, [128, [T, 8], [1, 512]]),
            pA(B, 0, 0, [128, [0, 8], [1, 512]])), inc="dv")
        dvn = em.emit("vector", lambda e, s, g=g, B=B: e.tensor_mul(
            pA(xn, 0, g * 512, [128, [T, 8], [1, 512]]),
            pA(nrmB, 0, 0, [128, [512, 8], [1, 512]]),
            pA(B, 0, 512, [128, [0, 8], [1, 512]])), inc="dv")
        bk.release(ka, "dv", dvn)
        bk.release(kb, "dv", dvn)
    dv_xn = em.cnt["dv"]
    for mc in range(8):
        em.emit("vector", lambda e, s, mc=mc: e.tensor_scalar_add(
            pA(acc, 0, mc * T, [128, [1, T]]), pA(acc, 0, mc * T, [128, [1, T]]),
            sb2s[:, mc:mc + 1]), inc="dv")
    for r in range(8):
        for mc in range(8):
            em.emit("vector", lambda e, s, r=r, mc=mc: e.tensor_scalar_add(
                pA(acc, 0, mc * T + r, [128, [8, 256]]),
                pA(acc, 0, mc * T + r, [128, [8, 256]]),
                b2cl[:, r * 8 + mc:r * 8 + mc + 1]), inc="dv")
    dv_accinit = em.cnt["dv"]

    if limit <= 3:
        return finish()
    def proj_fm(wsrc, spill):
        h1 = wload(slotA[:, :], whalf(wsrc, 0), slot="A")
        h2 = wload(slotB[:, :], whalf(wsrc, 1), slot="B")
        wwait("tensor", h1, h2)
        em.wait("tensor", "dv", dv_xn)
        em.wait("scalar", "d", em.cnt["d"])
        em.wait("vector", "d", em.cnt["d"])
        banks = [(P01, 0, "P01a"), (P01, 512, "P01b"),
                 (P23, 0, "P23a"), (P23, 512, "P23b")]
        spdmas = []
        hg = 0
        for g in range(NG):
            for ech in range(2):
                stgo = 2048 * (hg % 2)
                if hg >= 2:
                    em.wait("vector", "d", spdmas[hg - 1])
                    em.wait("scalar", "d", spdmas[hg - 1])
                for el in range(4):
                    ec = ech * 4 + el
                    slot = slotA if ec < 4 else slotB
                    b, o, key = banks[ec % 4]
                    bk.acquire("tensor", key)
                    for fc in range(FC):
                        MM(pA(slot, 0, fc * 512 + (ec % 4) * 128, [128, [1, 128]]),
                           pA(xn, 0, fc * T + g * 512, [128, [1, 512]]),
                           pA(b, 0, o, [128, [1, 512]]),
                           start=(fc == 0), stop=(fc == FC - 1))
                    pe_mm = em.cnt["pe"]
                    em.wait("scalar", "pe", pe_mm)
                    a1 = em.emit("scalar", lambda e, s, b=b, o=o, stgo=stgo, el=el: e.activation(
                        pA(stg, 0, stgo + el * 512, [128, [1, 512]]),
                        pA(b, 0, o, [128, [1, 512]]), AF.Exp), inc="ac")
                    em.wait("vector", "ac", a1)
                    em.emit("vector", lambda e, s, stgo=stgo, el=el: e.tensor_scalar_min(
                        pA(stg, 0, stgo + el * 512, [128, [1, 512]]),
                        pA(stg, 0, stgo + el * 512, [128, [1, 512]]), 1.0), inc="dv")
                    em.emit("vector", lambda e, s, b=b, o=o: e.tensor_scalar_max(
                        trS[:, :], pA(b, 0, o, [128, [1, 512]]), 0.0), inc="dv")
                    dve = em.emit("vector", lambda e, s, stgo=stgo, el=el: e.tensor_add(
                        pA(stg, 0, stgo + el * 512, [128, [1, 512]]),
                        pA(stg, 0, stgo + el * 512, [128, [1, 512]]),
                        trS[:, :]), inc="dv")
                    bk.release(key, "dv", dve)
                dv_g = em.cnt["dv"]
                em.wait("sync", "dv", dv_g)
                spdmas.append(dma(
                    "sync",
                    bass.AP(spill.tensor, g * 524288 + ech * 4 * 65536,
                            [[512, 128], [65536, 4], [1, 512]]),
                    pA(stg, 0, stgo, [128, [512, 4], [1, 512]])))
                hg += 1
        slot_used("A")
        slot_used("B")
        return spdmas[-1]

    proj_fm(wkT, k_sp)

    if limit <= 5:
        return finish()
    h1 = wload(slotA[:, :], whalf(wvT, 0), slot="A")
    h2 = wload(slotB[:, :], whalf(wvT, 1), slot="B")
    wwait("tensor", h1, h2)
    em.wait("vector", "d", em.cnt["d"])
    vdmas = []
    for sc in range(NSC):
        stgo = 1040 * (sc % 2)
        if sc >= 2:
            em.wait("vector", "d", vdmas[sc - 1])
        PB = P01 if sc % 2 == 0 else P23
        ka, kb = ("P01a", "P01b") if sc % 2 == 0 else ("P23a", "P23b")
        bk.acquire("tensor", ka, kb)
        for eg in range(2):
            slot = slotA if eg == 0 else slotB
            for fc in range(FC):
                MM(pA(xn, 0, fc * T + sc * 128, [128, [1, 128]]),
                   pA(slot, 0, fc * 512, [128, [1, 512]]),
                   pA(PB, 0, eg * 512, [128, [1, 512]]),
                   start=(fc == 0), stop=(fc == FC - 1))
        pe_v = em.cnt["pe"]
        em.wait("vector", "pe", pe_v)
        em.emit("vector", lambda e, s, stgo=stgo, PB=PB: e.tensor_copy(
            pA(vstg, 0, stgo, [128, [65, 16], [1, 64]]),
            pA(PB, 0, 0, [128, [64, 16], [1, 64]])), inc="dv")
        dvv = em.emit("vector", lambda e, s, stgo=stgo: e.memset(
            pA(vstg, 0, stgo + 64, [128, [65, 16], [1, 1]]), 1.0), inc="dv")
        bk.release(ka, "dv", dvv)
        bk.release(kb, "dv", dvv)
        em.wait("sync", "dv", dvv)
        vdmas.append(dma("sync",
                         bass.AP(v_sp.tensor, sc * (128 * 1040), [[1040, 128], [1, 1040]]),
                         pA(vstg, 0, stgo, [128, [1, 1040]])))
    slot_used("A")
    slot_used("B")

    if limit <= 6:
        return finish()
    proj_fm(wqT, q_sp)

    if limit <= 7:
        return finish()
    wo1 = wload(slotA[:, :], whalf(woT, 0), slot="A")
    wo2 = wload(slotB[:, :], whalf(woT, 1), slot="B")
    em.wait("sync", "d", em.cnt["d"])
    wwait("tensor", wo1, wo2)

    use_q = use_k = use_v = 0
    for sc in range(SCAN_N):
        g, t0 = sc // 4, (sc % 4) * 128
        em.wait("sync", "pe", use_q)
        dq = dma("sync", bass.AP(qS, 0, [[1024, 128], [128, 8], [1, 128]]),
                 bass.AP(q_sp.tensor, g * 524288 + t0, [[512, 128], [65536, 8], [1, 128]]))
        em.wait("sync", "pe", use_k)
        dk = dma("sync", bass.AP(kfS, 0, [[1024, 128], [128, 8], [1, 128]]),
                 bass.AP(k_sp.tensor, g * 524288 + t0, [[512, 128], [65536, 8], [1, 128]]))
        em.wait("sync", "pe", use_v)
        dv1 = dma("sync", bass.AP(vtS, 0, [[1040, 128], [1, 520]]),
                  bass.AP(v_sp.tensor, sc * (128 * 1040), [[1040, 128], [1, 520]]))
        dv2 = dma("sync", bass.AP(vtS, 520, [[1040, 128], [1, 520]]),
                  bass.AP(v_sp.tensor, sc * (128 * 1040) + 520, [[1040, 128], [1, 520]]))
        em.wait("tensor", "d", dv2)
        dv_ktok = 0
        if SCAN_KTR:
            bk.acquire("tensor", "P45a", "P45b")
            for ec in range(FC):
                em.emit("tensor", lambda e, s, ec=ec: e.transpose(
                    pA(P45, 0, ec * 128, [128, [1, 128]]),
                    pA(kfS, 0, ec * 128, [128, [1, 128]]),
                    eyeS[:, :]), inc="pe")
            bk.acquire("tensor", "P01a")
            pe_ktr = em.emit("tensor", lambda e, s: e.transpose(
                pA(P01, 0, 0, [128, [1, 128]]), eyeS[:, :], eyeS[:, :]), inc="pe")
            em.wait("vector", "pe", pe_ktr)
            dv_ktok = em.emit("vector", lambda e, s: e.tensor_copy(
                ktS[:, :], pA(P45, 0, 0, [128, [1, 1024]])), inc="dv")
            bk.release("P45a", "dv", dv_ktok)
            bk.release("P45b", "dv", dv_ktok)
        for w in range(2):
            heads = list(range(w * 8, w * 8 + 8))
            dv_am = 0
            if SCAN_AT:
                bk.acquire("tensor", "P01a", "P01b")
                for h in heads:
                    l = h % 8
                    MM(pA(kfS, 64 * (h % 2), (h // 2) * 128, [64, [1, 128]]),
                       pA(qS, 64 * (h % 2), (h // 2) * 128, [64, [1, 128]]),
                       pA(P01, 0, (l % 2) * 512 + (l // 2) * 128, [128, [1, 128]]),
                       True, True)
                pe_at = em.cnt["pe"]
                em.wait("vector", "pe", pe_at)
                dv_am = em.emit("vector", lambda e, s: e.tensor_mul(
                    pA(amS, 0, 0, [128, [128, 8], [1, 128]]),
                    pA(P01, 0, 0, [128, [128, 8], [1, 128]]),
                    pA(tmask, 0, 0, [128, [0, 8], [1, 128]])), inc="dv")
                bk.release("P01a", "dv", dv_am)
                bk.release("P01b", "dv", dv_am)
            em.wait("tensor", "dv", max(dv_am, dv_ktok, dv_state))
            em.wait("tensor", "d", dv2)
            if SCAN_OT:
                bk.acquire("tensor", "P23a", "P23b")
                for h in heads:
                    l = h % 8
                    MM(pA(amS, 0, (l % 2) * 512 + (l // 2) * 128, [128, [1, 128]]),
                       pA(vtS, 0, h * 65, [128, [1, 65]]),
                       pA(P23, 0, 128 * (h % 8), [128, [1, 65]]),
                       True, False)
                    MM(pA(qS, 64 * (h % 2), (h // 2) * 128, [64, [1, 128]]),
                       pA(stS, 64 * (h % 2), 65 * (h // 2), [64, [1, 65]]),
                       pA(P23, 0, 128 * (h % 8), [128, [1, 65]]),
                       False, True)
                pe_o = em.cnt["pe"]
                em.wait("vector", "pe", pe_o)
                if SCAN_DEN:
                    em.emit("vector", lambda e, s, w=w: e.tensor_scalar_add(
                        pA(denS, 0, 8 * w, [128, [1, 8], [1, 1]]),
                        pA(P23, 0, 64, [128, [128, 8], [1, 1]]), EPS_ATT), inc="dv")
                    em.emit("vector", lambda e, s: e.drain())
                    em.emit("vector", lambda e, s, w=w: e.reciprocal(
                        pA(denS, 0, 16 + 8 * w, [128, [1, 8]]),
                        pA(denS, 0, 8 * w, [128, [1, 8]])), inc="dv")
                    em.emit("vector", lambda e, s: e.drain())
                dv_div = em.emit("vector", lambda e, s, w=w: e.tensor_mul(
                    pA(atkS, 0, 512 * w, [128, [64, 8], [1, 64]]),
                    pA(P23, 0, 0, [128, [128, 8], [1, 64]]),
                    pA(denS, 0, 16 + 8 * w, [128, [1, 8], [0, 64]])), inc="dv")
                bk.release("P23a", "dv", dv_div)
                bk.release("P23b", "dv", dv_div)
            if SCAN_DELTA:
                bk.acquire("tensor", "P6")
                for h in heads:
                    MM(pA(ktS, 0, (h // 2) * 128 + 64 * (h % 2), [128, [1, 64]]),
                       pA(vtS, 0, h * 65, [128, [1, 65]]),
                       pA(P6, 64 * (h % 2), 128 * ((h // 2) % 4), [64, [1, 65]]),
                       True, True)
                pe_sd = em.cnt["pe"]
                if w == 1:
                    use_q = pe_o
                    use_k = pe_at
                    use_v = pe_sd
                em.wait("vector", "pe", pe_sd)
                dv_state = em.emit("vector", lambda e, s, w=w: e.tensor_add(
                    pA(stS, 0, 65 * 4 * w, [128, [65, 4], [1, 65]]),
                    pA(stS, 0, 65 * 4 * w, [128, [65, 4], [1, 65]]),
                    pA(P6, 0, 0, [128, [128, 4], [1, 65]])), inc="dv")
                bk.release("P6", "dv", dv_state)
        em.wait("tensor", "dv", em.cnt["dv"])
        if SCAN_ATR:
            bk.acquire("tensor", "P45a", "P45b")
            for ec in range(FC):
                em.emit("tensor", lambda e, s, ec=ec: e.transpose(
                    pA(P45, 0, ec * 128, [128, [1, 128]]),
                    pA(atkS, 0, ec * 128, [128, [1, 128]]),
                    eyeS[:, :]), inc="pe")
            bk.acquire("tensor", "P01a")
            pe_atr = em.emit("tensor", lambda e, s: e.transpose(
                pA(P01, 0, 0, [128, [1, 128]]), eyeS[:, :], eyeS[:, :]), inc="pe")
            em.wait("vector", "pe", pe_atr)
            dv_afm = em.emit("vector", lambda e, s, sc=sc: e.tensor_copy(
                pA(afmS, 0, (sc % 2) * 128, [128, [256, 8], [1, 128]]),
                pA(P45, 0, 0, [128, [128, 8], [1, 128]])), inc="dv")
            bk.release("P45a", "dv", dv_afm)
            bk.release("P45b", "dv", dv_afm)
        if SCAN_WO and SCAN_ATR and sc % 2 == 1:
            tg0 = (sc - 1) * 128
            em.wait("tensor", "dv", dv_afm)
            for mc in range(8):
                slot = slotA if mc < 4 else slotB
                key = "P7" if mc % 2 == 0 else "P6"
                B = P7 if mc % 2 == 0 else P6
                bk.acquire("tensor", key)
                for ec in range(FC):
                    MM(pA(slot, 0, ec * 512 + (mc % 4) * 128, [128, [1, 128]]),
                       pA(afmS, 0, ec * 256, [128, [1, 256]]),
                       pA(B, 0, 0, [128, [1, 256]]),
                       start=(ec == 0), stop=(ec == FC - 1))
                pe_wo = em.cnt["pe"]
                em.wait("vector", "pe", pe_wo)
                dv_wo = em.emit("vector", lambda e, s, mc=mc, tg0=tg0, B=B: e.tensor_add(
                    pA(acc, 0, mc * T + tg0, [128, [1, 256]]),
                    pA(acc, 0, mc * T + tg0, [128, [1, 256]]),
                    pA(B, 0, 0, [128, [1, 256]])), inc="dv")
                bk.release(key, "dv", dv_wo)
    slot_used("A")
    slot_used("B")
    dv_attn = em.cnt["dv"]
    if "accA" in tap_aps:
        em.wait("sync", "dv", dv_attn)
        dma("sync", tap_aps["accA"][:, :], bass.AP(acc, 0, [[FC * T, 128], [1, FC * T]]))

    if limit <= 8:
        return finish()
    prev_smul = 0
    for blk in range(4):
        l1 = wload(slotA[:, :],
                   bass.AP(sw1.tensor, blk * 512, [[2048, 128], [262144, 8], [1, 512]]),
                   slot="A")
        l3 = wload(slotB[:, :],
                   bass.AP(sw3.tensor, blk * 512, [[2048, 128], [262144, 8], [1, 512]]),
                   slot="B")
        l2 = wload(sw2S[:, :],
                   bass.AP(sw2.tensor, blk * 512 * 1024, [[1024, 128], [131072, 4], [1, 1024]]),
                   wait_pe=em.cnt["pe"], slot="S")
        wwait("tensor", l1, l3)
        for tg in range(NG):
            for hcl in range(4):
                hc = blk * 4 + hcl
                bk.acquire("tensor", "P01a")
                for fc in range(FC):
                    MM(pA(slotA, 0, fc * 512 + hcl * 128, [128, [1, 128]]),
                       pA(xn, 0, fc * T + tg * 512, [128, [1, 512]]),
                       pA(P01, 0, 0, [128, [1, 512]]),
                       start=(fc == 0), stop=(fc == FC - 1))
                pe_g = em.cnt["pe"]
                bk.acquire("tensor", "P01b")
                for fc in range(FC):
                    MM(pA(slotB, 0, fc * 512 + hcl * 128, [128, [1, 128]]),
                       pA(xn, 0, fc * T + tg * 512, [128, [1, 512]]),
                       pA(P01, 0, 512, [128, [1, 512]]),
                       start=(fc == 0), stop=(fc == FC - 1))
                pe_u = em.cnt["pe"]
                if tg == NG - 1 and hcl == 3:
                    slot_free["A"] = slot_free["B"] = pe_u
                em.wait("scalar", "pe", pe_g)
                em.wait("scalar", "dv", prev_smul)
                ag = em.emit("scalar", lambda e, s, hc=hc: e.activation(
                    s1S[:, :], pA(P01, 0, 0, [128, [1, 512]]), AF.Silu,
                    bias=sb1s[:, hc:hc + 1]), inc="ac")
                bk.release("P01a", "ac", ag)
                em.wait("scalar", "pe", pe_u)
                au = em.emit("scalar", lambda e, s, hc=hc: e.activation(
                    u2S[:, :], pA(P01, 0, 512, [128, [1, 512]]), AF.Identity,
                    bias=sb3s[:, hc:hc + 1], scale=0.5), inc="ac")
                bk.release("P01b", "ac", au)
                em.wait("vector", "ac", au)
                prev_smul = em.emit("vector", lambda e, s, hcl=hcl: e.tensor_mul(
                    pA(sbkS, 0, hcl * 512, [128, [1, 512]]),
                    s1S[:, :], u2S[:, :]), inc="dv")
            dv_s = em.cnt["dv"]
            em.wait("tensor", "dv", dv_s)
            if tg == 0:
                wwait("tensor", l2)
            for mc in range(8):
                key = "P23a" if mc % 2 == 0 else "P23b"
                bk.acquire("tensor", key)
                for hcl in range(4):
                    MM(pA(sw2S, 0, hcl * 1024 + mc * 128, [128, [1, 128]]),
                       pA(sbkS, 0, hcl * 512, [128, [1, 512]]),
                       pA(P23, 0, 512 * (mc % 2), [128, [1, 512]]),
                       start=(hcl == 0), stop=(hcl == 3))
                pe_y = em.cnt["pe"]
                em.wait("vector", "pe", pe_y)
                dv_y = em.emit("vector", lambda e, s, mc=mc, tg=tg: e.tensor_add(
                    pA(acc, 0, mc * T + tg * 512, [128, [1, 512]]),
                    pA(acc, 0, mc * T + tg * 512, [128, [1, 512]]),
                    pA(P23, 0, 512 * (mc % 2), [128, [1, 512]])), inc="dv")
                bk.release(key, "dv", dv_y)
    dv_shared = em.cnt["dv"]
    if "accS" in tap_aps:
        em.wait("sync", "dv", dv_shared)
        dma("sync", tap_aps["accS"][:, :], bass.AP(acc, 0, [[FC * T, 128], [1, FC * T]]))

    if limit <= 9:
        return finish()
    def moe_slot_full(key):
        return {"A": slotA, "B": slotB, "X": wxS}[key][:, :]

    def moe_tile(key, coloff, n):
        t = {"A": slotA, "B": slotB, "X": wxS}[key]
        return pA(t, 0, coloff, [128, [1, n]])

    def wsrc_half(wt, e_i, half):
        return bass.AP(wt.tensor, e_i * (D * D) + half * 524288,
                       [[1024, 128], [131072, 4], [1, 1024]])

    halves = []
    for e_i in range(NE):
        halves += [(e_i, "w1", 0), (e_i, "w1", 1), (e_i, "w2", 0), (e_i, "w2", 1)]
    ROT = ["A", "B", "X"]
    slot_of = {hv: ROT[i % 3] for i, hv in enumerate(halves)}
    loaded = {}
    free_pe = {"A": em.cnt["pe"], "B": em.cnt["pe"], "X": em.cnt["pe"]}
    next_load = 0

    def emit_loads(upto):
        nonlocal next_load
        while next_load < upto:
            hv = halves[next_load]
            key = slot_of[hv]
            wt = w1 if hv[1] == "w1" else w2
            loaded[hv] = wload(moe_slot_full(key), wsrc_half(wt, hv[0], hv[2]),
                               wait_pe=free_pe[key], slot=key)
            next_load += 1

    emit_loads(2)
    hbanks = [(P01, 0, "P01a"), (P01, 512, "P01b"), (P23, 0, "P23a"), (P23, 512, "P23b")]
    for e_i in range(NE):
        r0, r1 = (3 * e_i) % 8, (3 * e_i + 3) % 8
        for ci, r in enumerate((r0, r1)):
            wwait("tensor", loaded[(e_i, "w1", 0)], loaded[(e_i, "w1", 1)])
            for hc in range(FC):
                b, o, key = hbanks[(ci * FC + hc) % 4]
                bk.acquire("tensor", key)
                for fc in range(FC):
                    half = fc // 4
                    MM(moe_tile(slot_of[(e_i, "w1", half)], (fc % 4) * 1024 + hc * 128, 128),
                       bass.AP(xn, fc * T + r, [[FC * T, 128], [8, 256]]),
                       pA(b, 0, o, [128, [1, 256]]),
                       start=(fc == 0), stop=(fc == FC - 1))
                pe_h = em.cnt["pe"]
                em.wait("scalar", "pe", pe_h)
                ah = em.emit("scalar", lambda e, s, b=b, o=o, hc=hc, ci=ci, e_i=e_i: e.activation(
                    pA(heS, 0, hc * 512 + ci * 256, [128, [1, 256]]),
                    pA(b, 0, o, [128, [1, 256]]), AF.Gelu,
                    bias=b1s[:, e_i * 8 + hc:e_i * 8 + hc + 1]), inc="ac")
                bk.release(key, "ac", ah)
        free_pe[slot_of[(e_i, "w1", 0)]] = em.cnt["pe"]
        free_pe[slot_of[(e_i, "w1", 1)]] = em.cnt["pe"]
        emit_loads(4 * e_i + 4)
        ac_h = em.cnt["ac"]
        em.wait("tensor", "ac", ac_h)
        for ci, r in enumerate((r0, r1)):
            wwait("tensor", loaded[(e_i, "w2", 0)], loaded[(e_i, "w2", 1)])
            for mc in range(8):
                key = "P45a" if mc % 2 == 0 else "P45b"
                bk.acquire("tensor", key)
                for hc in range(FC):
                    half = hc // 4
                    MM(moe_tile(slot_of[(e_i, "w2", half)], (hc % 4) * 1024 + mc * 128, 128),
                       pA(heS, 0, hc * 512 + ci * 256, [128, [1, 256]]),
                       pA(P45, 0, 512 * (mc % 2), [128, [1, 256]]),
                       start=(hc == 0), stop=(hc == FC - 1))
                pe_y = em.cnt["pe"]
                em.wait("vector", "pe", pe_y)
                em.emit("vector", lambda e, s, mc=mc: e.tensor_scalar_mul(
                    tmpM[:, :], pA(P45, 0, 512 * (mc % 2), [128, [1, 256]]), 0.25), inc="dv")
                dv_y = em.emit("vector", lambda e, s, mc=mc, r=r: e.tensor_add(
                    pA(acc, 0, mc * T + r, [128, [8, 256]]),
                    pA(acc, 0, mc * T + r, [128, [8, 256]]),
                    tmpM[:, :]), inc="dv")
                bk.release(key, "dv", dv_y)
        free_pe[slot_of[(e_i, "w2", 0)]] = em.cnt["pe"]
        free_pe[slot_of[(e_i, "w2", 1)]] = em.cnt["pe"]
        emit_loads(min(4 * e_i + 6, len(halves)))

    dv_all = em.cnt["dv"]
    em.wait("sync", "dv", dv_all)
    dfinal = dma("sync", bass.AP(out.tensor, 0, [[T, 128], [128 * T, 8], [1, T]]),
                 bass.AP(acc, 0, [[FC * T, 128], [T, 8], [1, T]]))
    if "back" in tap_aps:
        em.wait("sync", "pe", em.cnt["pe"])
        dfinal = dma("sync", tap_aps["back"][:, :],
                     bass.AP(backing, 0, [[BACK, 128], [1, BACK]]))
    if "xn" in tap_aps:
        for q in range(4):
            em.wait("vector", "pe", em.cnt["pe"])
            dvq = em.emit("vector", lambda e, s, q=q: e.tensor_copy(
                bass.AP(stg, 0, [[4096, 128], [1, 4096]]),
                bass.AP(xn, q * 4096, [[FC * T, 128], [1, 4096]])), inc="dv")
            em.wait("sync", "dv", dvq)
            dt = dma("sync", bass.AP(tap_aps["xn"].tensor, q * 4096,
                                     [[FC * T, 128], [1, 4096]]),
                     bass.AP(stg, 0, [[4096, 128], [1, 4096]]))
            em.wait("vector", "d", dt)
    em.wait("sync", "d", dfinal)

    semctxs, sems = [], {}
    for ch in em.cnt:
        cm = nc.semaphore("sem_" + ch)
        semctxs.append(cm)
        sems[ch] = cm.__enter__()
    blk_cm = nc.Block()
    block = blk_cm.__enter__()
    for eng in EM.ENGS:
        stream = em.streams[eng]
        getattr(block, eng)(lambda e, stream=stream: [f(e, sems) for f in stream])
    blk_cm.__exit__(None, None, None)
    for cm in semctxs:
        cm.__exit__(None, None, None)
    for cm in reversed(ctxs):
        cm.__exit__(None, None, None)
    return nc


_CACHE = {}


def _get_nc(taps=()):
    key = tuple(taps)
    if key not in _CACHE:
        _CACHE[key] = build_nc(taps=key)
    return _CACHE[key]


def host_inputs(full):
    f32 = np.float32
    x = np.asarray(full["L0_attn_x"], f32)
    shared = {
        "wqT": np.ascontiguousarray(np.asarray(full["L0_attn_Wq"], f32).T),
        "wkT": np.ascontiguousarray(np.asarray(full["L0_attn_Wk"], f32).T),
        "wvT": np.ascontiguousarray(np.asarray(full["L0_attn_Wv"], f32).T),
        "woT": np.ascontiguousarray(np.asarray(full["L0_attn_Wo"], f32).T),
        "sw1": np.ascontiguousarray(np.asarray(full["L0_ffn_sw1"], f32)),
        "sw3": np.ascontiguousarray(np.asarray(full["L0_ffn_sw3"], f32)),
        "sw2": np.ascontiguousarray(np.asarray(full["L0_ffn_sw2"], f32)),
        "w1": np.ascontiguousarray(np.asarray(full["L0_ffn_w1"], f32)),
        "w2": np.ascontiguousarray(np.asarray(full["L0_ffn_w2"], f32)),
        "sb1c": np.ascontiguousarray(np.asarray(full["L0_ffn_sb1"], f32).reshape(16, 128).T),
        "sb3c": np.ascontiguousarray(np.asarray(full["L0_ffn_sb3"], f32).reshape(16, 128).T),
        "sb2c": np.ascontiguousarray(np.asarray(full["L0_ffn_sb2"], f32).reshape(8, 128).T),
        "b1c": np.ascontiguousarray(np.asarray(full["L0_ffn_b1"], f32).reshape(NE, 8, 128).transpose(0, 2, 1)),
        "b2c": np.ascontiguousarray(np.asarray(full["L0_ffn_b2"], f32).reshape(NE, 8, 128).transpose(0, 2, 1)),
        "tmaskD": np.ascontiguousarray(np.triu(np.ones((128, 128), f32))),
        "eyeD": np.eye(128, dtype=f32),
        "onesD": np.ones((128, 128), f32),
    }
    zeros_prev = np.zeros((D, T), f32)
    in_maps = []
    for c in range(8):
        b, half = c // 2, c % 2
        m = dict(shared)
        m["xin"] = np.ascontiguousarray(x[b, half * T:(half + 1) * T].T)
        m["xprev"] = (np.ascontiguousarray(x[b, 0:T].T) if half == 1 else zeros_prev)
        m["maskA"] = np.full((128, 1), float(half), f32)
        in_maps.append(m)
    return in_maps


def kernel(**inputs):
    nc = _get_nc(())
    in_maps = host_inputs(inputs)
    res = run_bass_kernel_spmd(nc, in_maps, core_ids=list(range(8)))
    outp = np.empty((4, 4096, 1024), np.float32)
    for c in range(8):
        b, half = c // 2, c % 2
        outp[b, half * T:(half + 1) * T] = res.results[c]["out"].T
    return outp


# version 14
# speedup vs baseline: 1.0138x; 1.0028x over previous
import numpy as np
import concourse.bass as bass
import concourse.mybir as mybir
from concourse.bass_utils import run_bass_kernel_spmd

F32 = mybir.dt.float32
F32R = mybir.dt.float32r
AF = mybir.ActivationFunctionType

D = 1024
T = 2048
H = 16
FC = 8
NSC = 16
NG = 4
HS = 2048
NE = 8
EPS_ATT = 1e-8
EPS_LN = 1e-5
BACK = 11016
PS = 2568


class EM:
    ENGS = ("sync", "scalar", "vector", "tensor", "gpsimd")

    def __init__(self):
        self.streams = {e: [] for e in self.ENGS}
        self.cnt = {ch: 0 for ch in ("w", "wA", "wB", "wX", "wS", "d", "e", "pe", "dv", "ac")}
        self.lastwait = {e: {ch: 0 for ch in self.cnt} for e in self.ENGS}

    def emit(self, eng, fn, inc=None, k=1):
        if inc is not None:
            if eng in ("vector", "scalar"):
                self.wait(eng, inc, self.cnt[inc])
            self.cnt[inc] += k
            def wrapped(e, sems, fn=fn, inc=inc, k=k):
                fn(e, sems).then_inc(sems[inc], k)
            self.streams[eng].append(wrapped)
            return self.cnt[inc]
        self.streams[eng].append(lambda e, sems, fn=fn: fn(e, sems))
        return None

    def wait(self, eng, ch, val):
        if val is None or val <= 0:
            return
        assert val <= self.cnt[ch], (eng, ch, val, self.cnt[ch])
        if self.lastwait[eng][ch] >= val:
            return
        self.lastwait[eng][ch] = val
        self.streams[eng].append(
            lambda e, sems, ch=ch, val=val: e.wait_ge(sems[ch], val))


class Banks:
    def __init__(self, em):
        self.em = em
        self.free_at = {}

    def acquire(self, eng, *keys):
        for key in keys:
            ent = self.free_at.get(key)
            if ent is not None:
                self.em.wait(eng, ent[0], ent[1])

    def release(self, key, ch, val):
        self.free_at[key] = (ch, val)


SCAN_N = NSC
APH_BC = True
APH_NRM = True
APH_PROJ = True
APH_STATE = True
SCAN_WO = True
SCAN_ATR = True
SCAN_AT = True
SCAN_OT = True
SCAN_DEN = True
SCAN_DELTA = True
SCAN_KTR = True


def build_nc(taps=(), limit=99):
    nc = bass.Bass("TRN2", target_bir_lowering=False)

    xin = nc.dram_tensor("xin", [D, T], F32, kind="ExternalInput").ap()
    xprev = nc.dram_tensor("xprev", [D, T], F32, kind="ExternalInput").ap()
    maskA = nc.dram_tensor("maskA", [128, 1], F32, kind="ExternalInput").ap()
    wqT = nc.dram_tensor("wqT", [D, D], F32, kind="ExternalInput").ap()
    wkT = nc.dram_tensor("wkT", [D, D], F32, kind="ExternalInput").ap()
    wvT = nc.dram_tensor("wvT", [D, D], F32, kind="ExternalInput").ap()
    woT = nc.dram_tensor("woT", [D, D], F32, kind="ExternalInput").ap()
    sw1 = nc.dram_tensor("sw1", [D, HS], F32, kind="ExternalInput").ap()
    sw3 = nc.dram_tensor("sw3", [D, HS], F32, kind="ExternalInput").ap()
    sw2 = nc.dram_tensor("sw2", [HS, D], F32, kind="ExternalInput").ap()
    w1 = nc.dram_tensor("w1", [NE, D, D], F32, kind="ExternalInput").ap()
    w2 = nc.dram_tensor("w2", [NE, D, D], F32, kind="ExternalInput").ap()
    sb1c = nc.dram_tensor("sb1c", [128, 16], F32, kind="ExternalInput").ap()
    sb3c = nc.dram_tensor("sb3c", [128, 16], F32, kind="ExternalInput").ap()
    sb2c = nc.dram_tensor("sb2c", [128, 8], F32, kind="ExternalInput").ap()
    b1c = nc.dram_tensor("b1c", [NE, 128, 8], F32, kind="ExternalInput").ap()
    b2c = nc.dram_tensor("b2c", [NE, 128, 8], F32, kind="ExternalInput").ap()
    tmaskD = nc.dram_tensor("tmaskD", [128, 128], F32, kind="ExternalInput").ap()
    eyeD = nc.dram_tensor("eyeD", [128, 128], F32, kind="ExternalInput").ap()
    onesD = nc.dram_tensor("onesD", [128, 128], F32, kind="ExternalInput").ap()
    out = nc.dram_tensor("out", [D, T], F32, kind="ExternalOutput").ap()

    spk = "ExternalOutput" if "spills" in taps else "Internal"
    q_sp = nc.dram_tensor("q_sp", [NG, FC, 128, 512], F32, kind=spk).ap()
    k_sp = nc.dram_tensor("k_sp", [NG, FC, 128, 512], F32, kind=spk).ap()
    v_sp = nc.dram_tensor("v_sp", [NSC, 128, 1040], F32, kind=spk).ap()

    tap_aps = {}
    for tname in taps:
        if tname == "spills":
            continue
        cols = BACK if tname == "back" else FC * T
        tap_aps[tname] = nc.dram_tensor(
            "tap_" + tname, [128, cols], F32, kind="ExternalOutput").ap()

    em = EM()
    bk = Banks(em)
    ctxs = []

    def sbuf(name, shape, dtype):
        cm = nc.sbuf_tensor(name, shape, dtype)
        ctxs.append(cm)
        return cm.__enter__()

    def psum(name, shape):
        cm = nc.psum_tensor(name, shape, F32)
        ctxs.append(cm)
        return cm.__enter__()

    xn = sbuf("xn", [128, FC * T], F32R)
    acc = sbuf("acc", [128, FC * T], F32)
    slotA = sbuf("slotA", [128, 4096], F32R)
    slotB = sbuf("slotB", [128, 4096], F32R)
    consts = sbuf("consts", [128, 128], F32)
    tmask = sbuf("tmask", [128, 128], F32)
    eyeS = sbuf("eyeS", [128, 128], F32)
    maskS = sbuf("maskS", [128, 1], F32)
    sb1s = sbuf("sb1s", [128, 16], F32)
    sb3s = sbuf("sb3s", [128, 16], F32)
    sb2s = sbuf("sb2s", [128, 8], F32)
    b1s = sbuf("b1s", [128, NE * 8], F32)
    b2s = sbuf("b2s", [128, NE * 8], F32)
    b2cl = sbuf("b2cl", [128, NE * 8], F32)
    epss = sbuf("epss", [128, 1], F32)
    onesR = sbuf("onesR", [128, 2], F32R)
    backing = sbuf("backing", [128, BACK], F32)
    base = nc.lookup_mloc(backing).addr

    def overlay(name, cols, dtype, coloff):
        return nc.alloc_sbuf_tensor_at(name, [128, cols], dtype,
                                       offset=base + coloff * 4)

    stS = overlay("stS", 520, F32, 0)
    stsA = overlay("stsA", 2048, F32, 520)
    stsB = overlay("stsB", 2048, F32, 520)
    xaS = overlay("xaS", 4096, F32, PS)
    sqA = overlay("sqA", 1024, F32R, PS + 4096)
    tmpA = overlay("tmpA", 1024, F32, PS + 5120)
    xaF = overlay("xaF", 1024, F32, PS)
    xnA = overlay("xnA", 1024, F32R, PS + 1024)
    teA = overlay("teA", 512, F32, PS + 2048)
    ktA = overlay("ktA", 512, F32, PS + 2560)
    vtA = overlay("vtA", 520, F32, PS + 3072)
    sqB = overlay("sqB", 1024, F32R, PS)
    tmpB = overlay("tmpB", 1024, F32, PS + 1024)
    nrmB = overlay("nrmB", 4096, F32, PS + 2048)
    stg = overlay("stg", 4096, F32, PS)
    trS = overlay("trS", 512, F32, PS + 4096)
    vstg = overlay("vstg", 2080, F32, PS + 4608)
    qS = overlay("qS", 1024, F32, PS)
    kfS = overlay("kfS", 1024, F32, PS + 1024)
    vtS = overlay("vtS", 1040, F32, PS + 2048)
    ktS = overlay("ktS", 1024, F32, PS + 3088)
    amS = overlay("amS", 1024, F32, PS + 4112)
    atkS = overlay("atkS", 1024, F32, PS + 5136)
    denS = overlay("denS", 32, F32, PS + 6160)
    afmS = overlay("afmS", 2048, F32R, PS + 6192)
    sbkS = overlay("sbkS", 2048, F32R, PS)
    s1S = overlay("s1S", 512, F32, PS + 2048)
    u2S = overlay("u2S", 512, F32, PS + 2560)
    sw2S = overlay("sw2S", 4096, F32R, PS + 3072)
    wxS = overlay("wxS", 4096, F32R, PS)
    heS = overlay("heS", 4096, F32R, PS + 4096)
    tmpM = overlay("tmpM", 256, F32, PS + 8192)

    P01 = psum("P01", [128, 1024])
    P23 = psum("P23", [128, 1024])
    P45 = psum("P45", [128, 1024])
    P6 = psum("P6", [128, 512])
    P7 = psum("P7", [128, 512])

    def pA(t, p0, c0, dims):
        stride = t.shape[1]
        return bass.AP(t, p0 * stride + c0, [[stride, dims[0]]] + dims[1:])

    def dma(eng, dst, src, ch=None):
        if ch is None:
            ch = "w" if eng == "gpsimd" else "d"
        return em.emit(eng, lambda e, s, d=dst, sr=src: e.dma_start(out=d, in_=sr),
                       inc=ch, k=16)

    def MM(lhsT, rhs, dst, start, stop):
        return em.emit("tensor", lambda e, s, l=lhsT, r=rhs, d=dst, st=start, sp=stop:
                       e.matmul(d, l, r, start=st, stop=sp), inc="pe")

    slot_free = {"A": 0, "B": 0}

    def wload(dst_ap, src_ap, wait_pe=None, slot=None):
        if wait_pe is None:
            wait_pe = slot_free.get(slot, em.cnt["pe"]) if slot else em.cnt["pe"]
        em.wait("gpsimd", "pe", wait_pe)
        ch = ("w" + slot) if slot in ("A", "B", "X", "S") else "w"
        return (ch, dma("gpsimd", dst_ap, src_ap, ch=ch))

    def slot_used(slot):
        slot_free[slot] = em.cnt["pe"]

    def wwait(eng, *tokens):
        for ch, val in tokens:
            em.wait(eng, ch, val)

    def whalf(wt, eg):
        return bass.AP(wt.tensor, eg * 512, [[1024, 128], [131072, 8], [1, 512]])

    def finish():
        dv_all = em.cnt["dv"]
        em.wait("sync", "dv", dv_all)
        em.wait("sync", "pe", em.cnt["pe"])
        em.wait("sync", "ac", em.cnt["ac"])
        dfin = dma("sync", bass.AP(out.tensor, 0, [[T, 128], [128 * T, 8], [1, T]]),
                   bass.AP(acc, 0, [[FC * T, 128], [T, 8], [1, T]]))
        if "back" in tap_aps:
            dfin = dma("sync", tap_aps["back"][:, :],
                       bass.AP(backing, 0, [[BACK, 128], [1, BACK]]))
        em.wait("sync", "d", dfin)
        semctxs, sems = [], {}
        for ch in em.cnt:
            cm = nc.semaphore("sem_" + ch)
            semctxs.append(cm)
            sems[ch] = cm.__enter__()
        blk_cm = nc.Block()
        block = blk_cm.__enter__()
        for eng in EM.ENGS:
            stream = em.streams[eng]
            getattr(block, eng)(lambda e, stream=stream: [f(e, sems) for f in stream])
        blk_cm.__exit__(None, None, None)
        for cm in semctxs:
            cm.__exit__(None, None, None)
        for cm in reversed(ctxs):
            cm.__exit__(None, None, None)
        return nc

    dma("gpsimd", consts[:, :], onesD)
    dma("gpsimd", tmask[:, :], tmaskD)
    dma("gpsimd", eyeS[:, :], eyeD)
    dma("gpsimd", maskS[:, :], maskA)
    dma("gpsimd", sb1s[:, :], sb1c)
    dma("gpsimd", sb3s[:, :], sb3c)
    dma("gpsimd", sb2s[:, :], sb2c)
    dma("gpsimd", b1s[:, :], bass.AP(b1c.tensor, 0, [[8, 128], [1024, 8], [1, 8]]))
    dma("gpsimd", onesR[:, :], bass.AP(onesD.tensor, 0, [[128, 128], [1, 2]]))
    wconst = dma("gpsimd", b2s[:, :], bass.AP(b2c.tensor, 0, [[8, 128], [1024, 8], [1, 8]]))

    dx = dma("sync", bass.AP(acc, 0, [[FC * T, 128], [T, 8], [1, T]]),
             bass.AP(xin.tensor, 0, [[T, 128], [128 * T, 8], [1, T]]), ch="e")

    em.wait("vector", "w", wconst)
    em.emit("vector", lambda e, s: e.memset(epss[:, :], EPS_LN), inc="dv")
    em.emit("vector", lambda e, s: e.tensor_scalar_mul(sb3s[:, :], sb3s[:, :], 0.5), inc="dv")
    em.emit("vector", lambda e, s: e.tensor_scalar_mul(sb2s[:, :], sb2s[:, :], 0.5), inc="dv")
    for r in range(8):
        e0, e1 = (3 * r) % 8, (3 * r - 1) % 8
        em.emit("vector", lambda e, s, r=r, e0=e0, e1=e1: e.tensor_add(
            b2cl[:, r * 8:(r + 1) * 8], b2s[:, e0 * 8:(e0 + 1) * 8],
            b2s[:, e1 * 8:(e1 + 1) * 8]), inc="dv")
    em.emit("vector", lambda e, s: e.tensor_scalar_mul(b2cl[:, :], b2cl[:, :], 0.25), inc="dv")
    em.emit("vector", lambda e, s: e.memset(stS[:, :], 0.0), inc="dv")
    dv_sinit = em.cnt["dv"]

    BPAIR = [((P01, 0, "P01a"), (P01, 512, "P01b")),
             ((P23, 0, "P23a"), (P23, 512, "P23b")),
             ((P45, 0, "P45a"), (P45, 512, "P45b")),
             ((P6, 0, "P6"), (P7, 0, "P7"))]

    def emit_stats(st, sq, tmp, src_chunk, pre=None):
        sq_mm = [0, 0]
        for g in range(NG):
            (bS, oS, kS), (bQ, oQ, kQ) = BPAIR[g]
            if pre is not None:
                dn = pre(g)
                em.wait("tensor", "d", dn)
                em.wait("scalar", "d", dn)
            bk.acquire("tensor", kS, kQ)
            for fc in range(FC):
                par = fc % 2
                src = src_chunk(g, fc)
                MM(consts[:, 0:1], src, pA(bS, 0, oS, [1, [1, 512]]),
                   start=(fc == 0), stop=(fc == FC - 1))
                em.wait("scalar", "pe", sq_mm[par])
                an = em.emit("scalar", lambda e, s, src=src, par=par: e.activation(
                    pA(sq, 0, par * 512, [128, [1, 512]]), src, AF.Square), inc="ac")
                em.wait("tensor", "ac", an)
                sq_mm[par] = MM(onesR[:, :], pA(sq, 0, par * 512, [128, [1, 512]]),
                                pA(bQ, 0, oQ, [2, [1, 512]]),
                                start=(fc == 0), stop=(fc == FC - 1))
        pe_done = em.cnt["pe"]
        em.wait("vector", "pe", pe_done)
        for g in range(NG):
            (bS, oS, kS), (bQ, oQ, kQ) = BPAIR[g]
            em.emit("vector", lambda e, s, bS=bS, oS=oS, g=g: e.tensor_scalar_mul(
                pA(st, 0, g * 512, [1, [1, 512]]),
                pA(bS, 0, oS, [1, [1, 512]]), 1.0 / D), inc="dv")
            dvq = em.emit("vector", lambda e, s, bQ=bQ, oQ=oQ: e.tensor_scalar_mul(
                pA(tmp, 0, 0, [1, [1, 512]]),
                pA(bQ, 0, oQ, [1, [1, 512]]), 1.0 / D), inc="dv")
            bk.release(kS, "dv", dvq)
            bk.release(kQ, "dv", dvq)
            em.emit("vector", lambda e, s, g=g: e.tensor_mul(
                pA(tmp, 0, 512, [1, [1, 512]]),
                pA(st, 0, g * 512, [1, [1, 512]]),
                pA(st, 0, g * 512, [1, [1, 512]])), inc="dv")
            dvv = em.emit("vector", lambda e, s: e.tensor_sub(
                pA(tmp, 0, 0, [1, [1, 512]]),
                pA(tmp, 0, 0, [1, [1, 512]]),
                pA(tmp, 0, 512, [1, [1, 512]])), inc="dv")
            em.wait("scalar", "dv", dvv)
            an = em.emit("scalar", lambda e, s: e.activation(
                pA(tmp, 0, 512, [1, [1, 512]]),
                pA(tmp, 0, 0, [1, [1, 512]]), AF.Sqrt, bias=epss[0:1, 0:1]), inc="ac")
            em.wait("vector", "ac", an)
            em.emit("vector", lambda e, s, g=g: e.reciprocal(
                pA(st, 32, g * 512, [1, [1, 512]]),
                pA(tmp, 0, 512, [1, [1, 512]])), inc="dv")
            em.emit("vector", lambda e, s: e.drain())
        return em.cnt["dv"]

    if limit <= 0:
        return finish()
    def a1_pre(g):
        if g > 0:
            em.wait("sync", "pe", em.cnt["pe"])
            em.wait("sync", "ac", em.cnt["ac"])
        return dma("sync", bass.AP(xaS, 0, [[4096, 128], [512, 8], [1, 512]]),
                   bass.AP(xprev.tensor, g * 512, [[T, 128], [128 * T, 8], [1, 512]]))

    em.wait("tensor", "w", wconst)
    dvA = emit_stats(stsA, sqA, tmpA,
                     lambda g, fc: pA(xaS, 0, fc * 512, [128, [1, 512]]),
                     pre=a1_pre)

    if limit <= 1:
        return finish()
    em.wait("gpsimd", "dv", dvA)
    em.wait("tensor", "dv", dvA)
    dv_prev_state = dv_sinit
    pe_prev_proj = em.cnt["pe"]
    em.wait("sync", "ac", em.cnt["ac"])
    for eg in range(2):
        wk = wload(slotA[:, :], whalf(wkT, eg), slot="A")
        wv = wload(slotB[:, :], whalf(wvT, eg), slot="B")
        wwait("tensor", wk, wv)
        for sc in range(NSC):
            t0 = sc * 128
            em.wait("sync", "pe", pe_prev_proj)
            dn = dma("sync", bass.AP(xaF, 0, [[1024, 128], [128, 8], [1, 128]]),
                     bass.AP(xprev.tensor, t0, [[T, 128], [128 * T, 8], [1, 128]]))
            if not APH_BC:
                continue
            bk.acquire("tensor", "P6", "P7")
            MM(consts[0:1, :], pA(stsA, 0, t0, [1, [1, 128]]),
               pA(P6, 0, 0, [128, [1, 128]]), True, True)
            bmr = MM(consts[32:33, :], pA(stsA, 32, t0, [1, [1, 128]]),
                     pA(P7, 0, 0, [128, [1, 128]]), True, True)
            if not APH_NRM:
                bk.release("P6", "pe", bmr)
                bk.release("P7", "pe", bmr)
                continue
            em.wait("vector", "d", dn)
            em.wait("vector", "pe", bmr)
            em.emit("vector", lambda e, s: e.tensor_sub(
                pA(xaF, 0, 0, [128, [128, 8], [1, 128]]),
                pA(xaF, 0, 0, [128, [128, 8], [1, 128]]),
                pA(P6, 0, 0, [128, [0, 8], [1, 128]])), inc="dv")
            dvn = em.emit("vector", lambda e, s: e.tensor_mul(
                pA(xnA, 0, 0, [128, [128, 8], [1, 128]]),
                pA(xaF, 0, 0, [128, [128, 8], [1, 128]]),
                pA(P7, 0, 0, [128, [0, 8], [1, 128]])), inc="dv")
            bk.release("P6", "dv", dvn)
            bk.release("P7", "dv", dvn)
            em.wait("tensor", "dv", dvn)
            if not APH_PROJ:
                continue
            bk.acquire("tensor", "P01a")
            for fc in range(FC):
                MM(pA(xnA, 0, fc * 128, [128, [1, 128]]),
                   pA(slotA, 0, fc * 512, [128, [1, 512]]),
                   pA(P01, 0, 0, [128, [1, 512]]),
                   start=(fc == 0), stop=(fc == FC - 1))
            pe_k = em.cnt["pe"]
            bk.acquire("tensor", "P23a")
            for fc in range(FC):
                MM(pA(xnA, 0, fc * 128, [128, [1, 128]]),
                   pA(slotB, 0, fc * 512, [128, [1, 512]]),
                   pA(P23, 0, 0, [128, [1, 512]]),
                   start=(fc == 0), stop=(fc == FC - 1))
            pe_v = em.cnt["pe"]
            pe_prev_proj = pe_v
            em.wait("scalar", "pe", pe_k)
            a1 = em.emit("scalar", lambda e, s: e.activation(
                teA[:, :], pA(P01, 0, 0, [128, [1, 512]]), AF.Exp), inc="ac")
            em.wait("vector", "ac", a1)
            em.emit("vector", lambda e, s: e.tensor_scalar_min(
                teA[:, :], teA[:, :], 1.0), inc="dv")
            em.emit("vector", lambda e, s: e.tensor_scalar_max(
                ktA[:, :], pA(P01, 0, 0, [128, [1, 512]]), 0.0), inc="dv")
            dvk = em.emit("vector", lambda e, s: e.tensor_add(
                ktA[:, :], ktA[:, :], teA[:, :]), inc="dv")
            bk.release("P01a", "dv", dvk)
            em.wait("vector", "pe", pe_v)
            em.emit("vector", lambda e, s: e.tensor_copy(
                pA(vtA, 0, 0, [128, [65, 8], [1, 64]]),
                pA(P23, 0, 0, [128, [64, 8], [1, 64]])), inc="dv")
            dvv = em.emit("vector", lambda e, s: e.memset(
                pA(vtA, 0, 64, [128, [65, 8], [1, 1]]), 1.0), inc="dv")
            bk.release("P23a", "dv", dvv)
            if not APH_STATE:
                continue
            em.wait("tensor", "dv", max(dvk, dvv, dv_prev_state))
            bk.acquire("tensor", "P45a")
            for hl in range(8):
                MM(pA(ktA, 0, (hl // 2) * 128 + 64 * (hl % 2), [128, [1, 64]]),
                   pA(vtA, 0, hl * 65, [128, [1, 65]]),
                   pA(P45, 64 * (hl % 2), 128 * (hl // 2), [64, [1, 65]]),
                   True, True)
            pe_s = em.cnt["pe"]
            em.wait("vector", "pe", pe_s)
            dv_prev_state = em.emit("vector", lambda e, s, eg=eg: e.tensor_add(
                pA(stS, 0, 65 * 4 * eg, [128, [65, 4], [1, 65]]),
                pA(stS, 0, 65 * 4 * eg, [128, [65, 4], [1, 65]]),
                pA(P45, 0, 0, [128, [128, 4], [1, 65]])), inc="dv")
            bk.release("P45a", "dv", dv_prev_state)
        slot_used("A")
        slot_used("B")
    em.wait("vector", "dv", dv_prev_state)
    dv_state = em.emit("vector", lambda e, s: e.tensor_scalar_mul(
        stS[:, :], stS[:, :], maskS[:, 0:1]), inc="dv")

    if limit <= 2:
        return finish()
    em.wait("vector", "e", dx)
    em.wait("tensor", "e", dx)
    em.wait("scalar", "e", dx)
    em.wait("scalar", "pe", em.cnt["pe"])
    d_all = em.cnt["d"]
    dv_stats = emit_stats(stsB, sqB, tmpB,
                          lambda g, fc: pA(acc, 0, fc * T + g * 512, [128, [1, 512]]),
                          pre=lambda g: d_all)
    em.wait("tensor", "dv", dv_stats)
    for g in range(NG):
        B = P01 if g % 2 == 0 else P23
        ka, kb = ("P01a", "P01b") if g % 2 == 0 else ("P23a", "P23b")
        bk.acquire("tensor", ka, kb)
        MM(consts[0:1, :], pA(stsB, 0, g * 512, [1, [1, 512]]),
           pA(B, 0, 0, [128, [1, 512]]), True, True)
        bmr = MM(consts[32:33, :], pA(stsB, 32, g * 512, [1, [1, 512]]),
                 pA(B, 0, 512, [128, [1, 512]]), True, True)
        em.wait("vector", "pe", bmr)
        em.emit("vector", lambda e, s, g=g, B=B: e.tensor_sub(
            pA(nrmB, 0, 0, [128, [512, 8], [1, 512]]),
            pA(acc, 0, g * 512, [128, [T, 8], [1, 512]]),
            pA(B, 0, 0, [128, [0, 8], [1, 512]])), inc="dv")
        dvn = em.emit("vector", lambda e, s, g=g, B=B: e.tensor_mul(
            pA(xn, 0, g * 512, [128, [T, 8], [1, 512]]),
            pA(nrmB, 0, 0, [128, [512, 8], [1, 512]]),
            pA(B, 0, 512, [128, [0, 8], [1, 512]])), inc="dv")
        bk.release(ka, "dv", dvn)
        bk.release(kb, "dv", dvn)
    dv_xn = em.cnt["dv"]
    accinit_jobs = []
    for mc in range(8):
        accinit_jobs.append(lambda mc=mc: em.emit("vector", lambda e, s, mc=mc: e.tensor_scalar_add(
            pA(acc, 0, mc * T, [128, [1, T]]), pA(acc, 0, mc * T, [128, [1, T]]),
            sb2s[:, mc:mc + 1]), inc="dv"))
    for r in range(8):
        for mc in range(8):
            accinit_jobs.append(lambda r=r, mc=mc: em.emit("vector", lambda e, s, r=r, mc=mc: e.tensor_scalar_add(
                pA(acc, 0, mc * T + r, [128, [8, 256]]),
                pA(acc, 0, mc * T + r, [128, [8, 256]]),
                b2cl[:, r * 8 + mc:r * 8 + mc + 1]), inc="dv"))

    if limit <= 3:
        return finish()
    def proj_fm(wsrc, spill):
        h1 = wload(slotA[:, :], whalf(wsrc, 0), slot="A")
        h2 = wload(slotB[:, :], whalf(wsrc, 1), slot="B")
        wwait("tensor", h1, h2)
        em.wait("tensor", "dv", dv_xn)
        em.wait("scalar", "d", em.cnt["d"])
        em.wait("vector", "d", em.cnt["d"])
        banks = [(P01, 0, "P01a"), (P01, 512, "P01b"),
                 (P23, 0, "P23a"), (P23, 512, "P23b")]
        spdmas = []
        hg = 0
        for g in range(NG):
            for ech in range(2):
                stgo = 2048 * (hg % 2)
                if hg >= 2:
                    em.wait("vector", "d", spdmas[hg - 1])
                    em.wait("scalar", "d", spdmas[hg - 1])
                for el in range(4):
                    ec = ech * 4 + el
                    slot = slotA if ec < 4 else slotB
                    b, o, key = banks[ec % 4]
                    bk.acquire("tensor", key)
                    for fc in range(FC):
                        MM(pA(slot, 0, fc * 512 + (ec % 4) * 128, [128, [1, 128]]),
                           pA(xn, 0, fc * T + g * 512, [128, [1, 512]]),
                           pA(b, 0, o, [128, [1, 512]]),
                           start=(fc == 0), stop=(fc == FC - 1))
                    pe_mm = em.cnt["pe"]
                    em.wait("scalar", "pe", pe_mm)
                    a1 = em.emit("scalar", lambda e, s, b=b, o=o, stgo=stgo, el=el: e.activation(
                        pA(stg, 0, stgo + el * 512, [128, [1, 512]]),
                        pA(b, 0, o, [128, [1, 512]]), AF.Exp), inc="ac")
                    em.wait("vector", "ac", a1)
                    em.emit("vector", lambda e, s, stgo=stgo, el=el: e.tensor_scalar_min(
                        pA(stg, 0, stgo + el * 512, [128, [1, 512]]),
                        pA(stg, 0, stgo + el * 512, [128, [1, 512]]), 1.0), inc="dv")
                    em.emit("vector", lambda e, s, b=b, o=o: e.tensor_scalar_max(
                        trS[:, :], pA(b, 0, o, [128, [1, 512]]), 0.0), inc="dv")
                    dve = em.emit("vector", lambda e, s, stgo=stgo, el=el: e.tensor_add(
                        pA(stg, 0, stgo + el * 512, [128, [1, 512]]),
                        pA(stg, 0, stgo + el * 512, [128, [1, 512]]),
                        trS[:, :]), inc="dv")
                    bk.release(key, "dv", dve)
                dv_g = em.cnt["dv"]
                em.wait("sync", "dv", dv_g)
                spdmas.append(dma(
                    "sync",
                    bass.AP(spill.tensor, g * 524288 + ech * 4 * 65536,
                            [[512, 128], [65536, 4], [1, 512]]),
                    pA(stg, 0, stgo, [128, [512, 4], [1, 512]])))
                hg += 1
        slot_used("A")
        slot_used("B")
        return spdmas[-1]

    proj_fm(wkT, k_sp)

    if limit <= 5:
        return finish()
    h1 = wload(slotA[:, :], whalf(wvT, 0), slot="A")
    h2 = wload(slotB[:, :], whalf(wvT, 1), slot="B")
    wwait("tensor", h1, h2)
    em.wait("vector", "d", em.cnt["d"])
    vdmas = []
    for sc in range(NSC):
        stgo = 1040 * (sc % 2)
        if sc >= 2:
            em.wait("vector", "d", vdmas[sc - 1])
        PB = P01 if sc % 2 == 0 else P23
        ka, kb = ("P01a", "P01b") if sc % 2 == 0 else ("P23a", "P23b")
        bk.acquire("tensor", ka, kb)
        for eg in range(2):
            slot = slotA if eg == 0 else slotB
            for fc in range(FC):
                MM(pA(xn, 0, fc * T + sc * 128, [128, [1, 128]]),
                   pA(slot, 0, fc * 512, [128, [1, 512]]),
                   pA(PB, 0, eg * 512, [128, [1, 512]]),
                   start=(fc == 0), stop=(fc == FC - 1))
        pe_v = em.cnt["pe"]
        em.wait("vector", "pe", pe_v)
        em.emit("vector", lambda e, s, stgo=stgo, PB=PB: e.tensor_copy(
            pA(vstg, 0, stgo, [128, [65, 16], [1, 64]]),
            pA(PB, 0, 0, [128, [64, 16], [1, 64]])), inc="dv")
        dvv = em.emit("vector", lambda e, s, stgo=stgo: e.memset(
            pA(vstg, 0, stgo + 64, [128, [65, 16], [1, 1]]), 1.0), inc="dv")
        bk.release(ka, "dv", dvv)
        bk.release(kb, "dv", dvv)
        em.wait("sync", "dv", dvv)
        vdmas.append(dma("sync",
                         bass.AP(v_sp.tensor, sc * (128 * 1040), [[1040, 128], [1, 1040]]),
                         pA(vstg, 0, stgo, [128, [1, 1040]])))
        for _ in range(5):
            if accinit_jobs:
                accinit_jobs.pop(0)()
    while accinit_jobs:
        accinit_jobs.pop(0)()
    dv_accinit = em.cnt["dv"]
    slot_used("A")
    slot_used("B")

    if limit <= 6:
        return finish()
    proj_fm(wqT, q_sp)

    if limit <= 7:
        return finish()
    wo1 = wload(slotA[:, :], whalf(woT, 0), slot="A")
    wo2 = wload(slotB[:, :], whalf(woT, 1), slot="B")
    em.wait("sync", "d", em.cnt["d"])
    wwait("tensor", wo1, wo2)

    use_q = use_k = use_v = 0
    for sc in range(SCAN_N):
        g, t0 = sc // 4, (sc % 4) * 128
        em.wait("sync", "pe", use_q)
        dq = dma("sync", bass.AP(qS, 0, [[1024, 128], [128, 8], [1, 128]]),
                 bass.AP(q_sp.tensor, g * 524288 + t0, [[512, 128], [65536, 8], [1, 128]]))
        em.wait("sync", "pe", use_k)
        dk = dma("sync", bass.AP(kfS, 0, [[1024, 128], [128, 8], [1, 128]]),
                 bass.AP(k_sp.tensor, g * 524288 + t0, [[512, 128], [65536, 8], [1, 128]]))
        em.wait("sync", "pe", use_v)
        dv1 = dma("sync", bass.AP(vtS, 0, [[1040, 128], [1, 520]]),
                  bass.AP(v_sp.tensor, sc * (128 * 1040), [[1040, 128], [1, 520]]))
        dv2 = dma("sync", bass.AP(vtS, 520, [[1040, 128], [1, 520]]),
                  bass.AP(v_sp.tensor, sc * (128 * 1040) + 520, [[1040, 128], [1, 520]]))
        em.wait("tensor", "d", dv2)
        dv_ktok = 0
        if SCAN_KTR:
            bk.acquire("tensor", "P45a", "P45b")
            for ec in range(FC):
                em.emit("tensor", lambda e, s, ec=ec: e.transpose(
                    pA(P45, 0, ec * 128, [128, [1, 128]]),
                    pA(kfS, 0, ec * 128, [128, [1, 128]]),
                    eyeS[:, :]), inc="pe")
            bk.acquire("tensor", "P01a")
            pe_ktr = em.emit("tensor", lambda e, s: e.transpose(
                pA(P01, 0, 0, [128, [1, 128]]), eyeS[:, :], eyeS[:, :]), inc="pe")
            em.wait("vector", "pe", pe_ktr)
            dv_ktok = em.emit("vector", lambda e, s: e.tensor_copy(
                ktS[:, :], pA(P45, 0, 0, [128, [1, 1024]])), inc="dv")
            bk.release("P45a", "dv", dv_ktok)
            bk.release("P45b", "dv", dv_ktok)
        for w in range(2):
            heads = list(range(w * 8, w * 8 + 8))
            dv_am = 0
            if SCAN_AT:
                bk.acquire("tensor", "P01a", "P01b")
                for h in heads:
                    l = h % 8
                    MM(pA(kfS, 64 * (h % 2), (h // 2) * 128, [64, [1, 128]]),
                       pA(qS, 64 * (h % 2), (h // 2) * 128, [64, [1, 128]]),
                       pA(P01, 0, (l % 2) * 512 + (l // 2) * 128, [128, [1, 128]]),
                       True, True)
                pe_at = em.cnt["pe"]
                em.wait("vector", "pe", pe_at)
                dv_am = em.emit("vector", lambda e, s: e.tensor_mul(
                    pA(amS, 0, 0, [128, [128, 8], [1, 128]]),
                    pA(P01, 0, 0, [128, [128, 8], [1, 128]]),
                    pA(tmask, 0, 0, [128, [0, 8], [1, 128]])), inc="dv")
                bk.release("P01a", "dv", dv_am)
                bk.release("P01b", "dv", dv_am)
            em.wait("tensor", "dv", max(dv_ktok, dv_state))
            em.wait("tensor", "d", dv2)
            if SCAN_DELTA:
                bk.acquire("tensor", "P6")
                for h in heads:
                    MM(pA(ktS, 0, (h // 2) * 128 + 64 * (h % 2), [128, [1, 64]]),
                       pA(vtS, 0, h * 65, [128, [1, 65]]),
                       pA(P6, 64 * (h % 2), 128 * ((h // 2) % 4), [64, [1, 65]]),
                       True, True)
                pe_sd = em.cnt["pe"]
            if SCAN_OT:
                em.wait("tensor", "dv", dv_am)
                bk.acquire("tensor", "P23a", "P23b")
                for h in heads:
                    l = h % 8
                    MM(pA(amS, 0, (l % 2) * 512 + (l // 2) * 128, [128, [1, 128]]),
                       pA(vtS, 0, h * 65, [128, [1, 65]]),
                       pA(P23, 0, 128 * (h % 8), [128, [1, 65]]),
                       True, False)
                    MM(pA(qS, 64 * (h % 2), (h // 2) * 128, [64, [1, 128]]),
                       pA(stS, 64 * (h % 2), 65 * (h // 2), [64, [1, 65]]),
                       pA(P23, 0, 128 * (h % 8), [128, [1, 65]]),
                       False, True)
                pe_o = em.cnt["pe"]
                em.wait("vector", "pe", pe_o)
                if SCAN_DEN:
                    em.emit("vector", lambda e, s, w=w: e.tensor_scalar_add(
                        pA(denS, 0, 8 * w, [128, [1, 8], [1, 1]]),
                        pA(P23, 0, 64, [128, [128, 8], [1, 1]]), EPS_ATT), inc="dv")
                    em.emit("vector", lambda e, s: e.drain())
                    em.emit("vector", lambda e, s, w=w: e.reciprocal(
                        pA(denS, 0, 16 + 8 * w, [128, [1, 8]]),
                        pA(denS, 0, 8 * w, [128, [1, 8]])), inc="dv")
                    em.emit("vector", lambda e, s: e.drain())
                dv_div = em.emit("vector", lambda e, s, w=w: e.tensor_mul(
                    pA(atkS, 0, 512 * w, [128, [64, 8], [1, 64]]),
                    pA(P23, 0, 0, [128, [128, 8], [1, 64]]),
                    pA(denS, 0, 16 + 8 * w, [128, [1, 8], [0, 64]])), inc="dv")
                bk.release("P23a", "dv", dv_div)
                bk.release("P23b", "dv", dv_div)
            if SCAN_DELTA:
                if w == 1:
                    use_q = pe_o
                    use_k = pe_at
                    use_v = pe_o
                em.wait("vector", "pe", max(pe_sd, pe_o))
                dv_state = em.emit("vector", lambda e, s, w=w: e.tensor_add(
                    pA(stS, 0, 65 * 4 * w, [128, [65, 4], [1, 65]]),
                    pA(stS, 0, 65 * 4 * w, [128, [65, 4], [1, 65]]),
                    pA(P6, 0, 0, [128, [128, 4], [1, 65]])), inc="dv")
                bk.release("P6", "dv", dv_state)
        em.wait("tensor", "dv", dv_div)
        if SCAN_ATR:
            bk.acquire("tensor", "P45a", "P45b")
            for ec in range(FC):
                em.emit("tensor", lambda e, s, ec=ec: e.transpose(
                    pA(P45, 0, ec * 128, [128, [1, 128]]),
                    pA(atkS, 0, ec * 128, [128, [1, 128]]),
                    eyeS[:, :]), inc="pe")
            bk.acquire("tensor", "P01a")
            pe_atr = em.emit("tensor", lambda e, s: e.transpose(
                pA(P01, 0, 0, [128, [1, 128]]), eyeS[:, :], eyeS[:, :]), inc="pe")
            em.wait("vector", "pe", pe_atr)
            dv_afm = em.emit("vector", lambda e, s, sc=sc: e.tensor_copy(
                pA(afmS, 0, (sc % 2) * 128, [128, [256, 8], [1, 128]]),
                pA(P45, 0, 0, [128, [128, 8], [1, 128]])), inc="dv")
            bk.release("P45a", "dv", dv_afm)
            bk.release("P45b", "dv", dv_afm)
        if SCAN_WO and SCAN_ATR and sc % 2 == 1:
            tg0 = (sc - 1) * 128
            em.wait("tensor", "dv", dv_afm)
            for mc in range(8):
                slot = slotA if mc < 4 else slotB
                key = "P7" if mc % 2 == 0 else "P6"
                B = P7 if mc % 2 == 0 else P6
                bk.acquire("tensor", key)
                for ec in range(FC):
                    MM(pA(slot, 0, ec * 512 + (mc % 4) * 128, [128, [1, 128]]),
                       pA(afmS, 0, ec * 256, [128, [1, 256]]),
                       pA(B, 0, 0, [128, [1, 256]]),
                       start=(ec == 0), stop=(ec == FC - 1))
                pe_wo = em.cnt["pe"]
                em.wait("vector", "pe", pe_wo)
                dv_wo = em.emit("vector", lambda e, s, mc=mc, tg0=tg0, B=B: e.tensor_add(
                    pA(acc, 0, mc * T + tg0, [128, [1, 256]]),
                    pA(acc, 0, mc * T + tg0, [128, [1, 256]]),
                    pA(B, 0, 0, [128, [1, 256]])), inc="dv")
                bk.release(key, "dv", dv_wo)
    slot_used("A")
    slot_used("B")
    dv_attn = em.cnt["dv"]
    if "accA" in tap_aps:
        em.wait("sync", "dv", dv_attn)
        dma("sync", tap_aps["accA"][:, :], bass.AP(acc, 0, [[FC * T, 128], [1, FC * T]]))

    if limit <= 8:
        return finish()
    prev_smul = 0
    for blk in range(4):
        l1 = wload(slotA[:, :],
                   bass.AP(sw1.tensor, blk * 512, [[2048, 128], [262144, 8], [1, 512]]),
                   slot="A")
        l3 = wload(slotB[:, :],
                   bass.AP(sw3.tensor, blk * 512, [[2048, 128], [262144, 8], [1, 512]]),
                   slot="B")
        l2 = wload(sw2S[:, :],
                   bass.AP(sw2.tensor, blk * 512 * 1024, [[1024, 128], [131072, 4], [1, 1024]]),
                   wait_pe=em.cnt["pe"], slot="S")
        wwait("tensor", l1, l3)
        for tg in range(NG):
            for hcl in range(4):
                hc = blk * 4 + hcl
                bk.acquire("tensor", "P01a")
                for fc in range(FC):
                    MM(pA(slotA, 0, fc * 512 + hcl * 128, [128, [1, 128]]),
                       pA(xn, 0, fc * T + tg * 512, [128, [1, 512]]),
                       pA(P01, 0, 0, [128, [1, 512]]),
                       start=(fc == 0), stop=(fc == FC - 1))
                pe_g = em.cnt["pe"]
                bk.acquire("tensor", "P01b")
                for fc in range(FC):
                    MM(pA(slotB, 0, fc * 512 + hcl * 128, [128, [1, 128]]),
                       pA(xn, 0, fc * T + tg * 512, [128, [1, 512]]),
                       pA(P01, 0, 512, [128, [1, 512]]),
                       start=(fc == 0), stop=(fc == FC - 1))
                pe_u = em.cnt["pe"]
                if tg == NG - 1 and hcl == 3:
                    slot_free["A"] = slot_free["B"] = pe_u
                em.wait("scalar", "pe", pe_g)
                em.wait("scalar", "dv", prev_smul)
                ag = em.emit("scalar", lambda e, s, hc=hc: e.activation(
                    s1S[:, :], pA(P01, 0, 0, [128, [1, 512]]), AF.Silu,
                    bias=sb1s[:, hc:hc + 1]), inc="ac")
                bk.release("P01a", "ac", ag)
                em.wait("scalar", "pe", pe_u)
                au = em.emit("scalar", lambda e, s, hc=hc: e.activation(
                    u2S[:, :], pA(P01, 0, 512, [128, [1, 512]]), AF.Identity,
                    bias=sb3s[:, hc:hc + 1], scale=0.5), inc="ac")
                bk.release("P01b", "ac", au)
                em.wait("vector", "ac", au)
                prev_smul = em.emit("vector", lambda e, s, hcl=hcl: e.tensor_mul(
                    pA(sbkS, 0, hcl * 512, [128, [1, 512]]),
                    s1S[:, :], u2S[:, :]), inc="dv")
            dv_s = em.cnt["dv"]
            em.wait("tensor", "dv", dv_s)
            if tg == 0:
                wwait("tensor", l2)
            for mc in range(8):
                key = "P23a" if mc % 2 == 0 else "P23b"
                bk.acquire("tensor", key)
                for hcl in range(4):
                    MM(pA(sw2S, 0, hcl * 1024 + mc * 128, [128, [1, 128]]),
                       pA(sbkS, 0, hcl * 512, [128, [1, 512]]),
                       pA(P23, 0, 512 * (mc % 2), [128, [1, 512]]),
                       start=(hcl == 0), stop=(hcl == 3))
                pe_y = em.cnt["pe"]
                em.wait("vector", "pe", pe_y)
                dv_y = em.emit("vector", lambda e, s, mc=mc, tg=tg: e.tensor_add(
                    pA(acc, 0, mc * T + tg * 512, [128, [1, 512]]),
                    pA(acc, 0, mc * T + tg * 512, [128, [1, 512]]),
                    pA(P23, 0, 512 * (mc % 2), [128, [1, 512]])), inc="dv")
                bk.release(key, "dv", dv_y)
    dv_shared = em.cnt["dv"]
    if "accS" in tap_aps:
        em.wait("sync", "dv", dv_shared)
        dma("sync", tap_aps["accS"][:, :], bass.AP(acc, 0, [[FC * T, 128], [1, FC * T]]))

    if limit <= 9:
        return finish()
    def moe_slot_full(key):
        return {"A": slotA, "B": slotB, "X": wxS}[key][:, :]

    def moe_tile(key, coloff, n):
        t = {"A": slotA, "B": slotB, "X": wxS}[key]
        return pA(t, 0, coloff, [128, [1, n]])

    def wsrc_half(wt, e_i, half):
        return bass.AP(wt.tensor, e_i * (D * D) + half * 524288,
                       [[1024, 128], [131072, 4], [1, 1024]])

    halves = []
    for e_i in range(NE):
        halves += [(e_i, "w1", 0), (e_i, "w1", 1), (e_i, "w2", 0), (e_i, "w2", 1)]
    ROT = ["A", "B", "X"]
    slot_of = {hv: ROT[i % 3] for i, hv in enumerate(halves)}
    loaded = {}
    free_pe = {"A": em.cnt["pe"], "B": em.cnt["pe"], "X": em.cnt["pe"]}
    next_load = 0

    def emit_loads(upto):
        nonlocal next_load
        while next_load < upto:
            hv = halves[next_load]
            key = slot_of[hv]
            wt = w1 if hv[1] == "w1" else w2
            loaded[hv] = wload(moe_slot_full(key), wsrc_half(wt, hv[0], hv[2]),
                               wait_pe=free_pe[key], slot=key)
            next_load += 1

    emit_loads(2)
    hbanks = [(P01, 0, "P01a"), (P01, 512, "P01b"), (P23, 0, "P23a"), (P23, 512, "P23b")]
    for e_i in range(NE):
        r0, r1 = (3 * e_i) % 8, (3 * e_i + 3) % 8
        for ci, r in enumerate((r0, r1)):
            wwait("tensor", loaded[(e_i, "w1", 0)], loaded[(e_i, "w1", 1)])
            for hc in range(FC):
                b, o, key = hbanks[(ci * FC + hc) % 4]
                bk.acquire("tensor", key)
                for fc in range(FC):
                    half = fc // 4
                    MM(moe_tile(slot_of[(e_i, "w1", half)], (fc % 4) * 1024 + hc * 128, 128),
                       bass.AP(xn, fc * T + r, [[FC * T, 128], [8, 256]]),
                       pA(b, 0, o, [128, [1, 256]]),
                       start=(fc == 0), stop=(fc == FC - 1))
                pe_h = em.cnt["pe"]
                em.wait("scalar", "pe", pe_h)
                ah = em.emit("scalar", lambda e, s, b=b, o=o, hc=hc, ci=ci, e_i=e_i: e.activation(
                    pA(heS, 0, hc * 512 + ci * 256, [128, [1, 256]]),
                    pA(b, 0, o, [128, [1, 256]]), AF.Gelu,
                    bias=b1s[:, e_i * 8 + hc:e_i * 8 + hc + 1]), inc="ac")
                bk.release(key, "ac", ah)
        free_pe[slot_of[(e_i, "w1", 0)]] = em.cnt["pe"]
        free_pe[slot_of[(e_i, "w1", 1)]] = em.cnt["pe"]
        emit_loads(4 * e_i + 4)
        ac_h = em.cnt["ac"]
        em.wait("tensor", "ac", ac_h)
        for ci, r in enumerate((r0, r1)):
            wwait("tensor", loaded[(e_i, "w2", 0)], loaded[(e_i, "w2", 1)])
            for mc in range(8):
                key = "P45a" if mc % 2 == 0 else "P45b"
                bk.acquire("tensor", key)
                for hc in range(FC):
                    half = hc // 4
                    MM(moe_tile(slot_of[(e_i, "w2", half)], (hc % 4) * 1024 + mc * 128, 128),
                       pA(heS, 0, hc * 512 + ci * 256, [128, [1, 256]]),
                       pA(P45, 0, 512 * (mc % 2), [128, [1, 256]]),
                       start=(hc == 0), stop=(hc == FC - 1))
                pe_y = em.cnt["pe"]
                em.wait("vector", "pe", pe_y)
                em.emit("vector", lambda e, s, mc=mc: e.tensor_scalar_mul(
                    tmpM[:, :], pA(P45, 0, 512 * (mc % 2), [128, [1, 256]]), 0.25), inc="dv")
                dv_y = em.emit("vector", lambda e, s, mc=mc, r=r: e.tensor_add(
                    pA(acc, 0, mc * T + r, [128, [8, 256]]),
                    pA(acc, 0, mc * T + r, [128, [8, 256]]),
                    tmpM[:, :]), inc="dv")
                bk.release(key, "dv", dv_y)
        free_pe[slot_of[(e_i, "w2", 0)]] = em.cnt["pe"]
        free_pe[slot_of[(e_i, "w2", 1)]] = em.cnt["pe"]
        emit_loads(min(4 * e_i + 6, len(halves)))

    dv_all = em.cnt["dv"]
    em.wait("sync", "dv", dv_all)
    dfinal = dma("sync", bass.AP(out.tensor, 0, [[T, 128], [128 * T, 8], [1, T]]),
                 bass.AP(acc, 0, [[FC * T, 128], [T, 8], [1, T]]))
    if "back" in tap_aps:
        em.wait("sync", "pe", em.cnt["pe"])
        dfinal = dma("sync", tap_aps["back"][:, :],
                     bass.AP(backing, 0, [[BACK, 128], [1, BACK]]))
    if "xn" in tap_aps:
        for q in range(4):
            em.wait("vector", "pe", em.cnt["pe"])
            dvq = em.emit("vector", lambda e, s, q=q: e.tensor_copy(
                bass.AP(stg, 0, [[4096, 128], [1, 4096]]),
                bass.AP(xn, q * 4096, [[FC * T, 128], [1, 4096]])), inc="dv")
            em.wait("sync", "dv", dvq)
            dt = dma("sync", bass.AP(tap_aps["xn"].tensor, q * 4096,
                                     [[FC * T, 128], [1, 4096]]),
                     bass.AP(stg, 0, [[4096, 128], [1, 4096]]))
            em.wait("vector", "d", dt)
    em.wait("sync", "d", dfinal)

    semctxs, sems = [], {}
    for ch in em.cnt:
        cm = nc.semaphore("sem_" + ch)
        semctxs.append(cm)
        sems[ch] = cm.__enter__()
    blk_cm = nc.Block()
    block = blk_cm.__enter__()
    for eng in EM.ENGS:
        stream = em.streams[eng]
        getattr(block, eng)(lambda e, stream=stream: [f(e, sems) for f in stream])
    blk_cm.__exit__(None, None, None)
    for cm in semctxs:
        cm.__exit__(None, None, None)
    for cm in reversed(ctxs):
        cm.__exit__(None, None, None)
    return nc


_CACHE = {}


def _get_nc(taps=()):
    key = tuple(taps)
    if key not in _CACHE:
        _CACHE[key] = build_nc(taps=key)
    return _CACHE[key]


def host_inputs(full):
    f32 = np.float32
    x = np.asarray(full["L0_attn_x"], f32)
    shared = {
        "wqT": np.ascontiguousarray(np.asarray(full["L0_attn_Wq"], f32).T),
        "wkT": np.ascontiguousarray(np.asarray(full["L0_attn_Wk"], f32).T),
        "wvT": np.ascontiguousarray(np.asarray(full["L0_attn_Wv"], f32).T),
        "woT": np.ascontiguousarray(np.asarray(full["L0_attn_Wo"], f32).T),
        "sw1": np.ascontiguousarray(np.asarray(full["L0_ffn_sw1"], f32)),
        "sw3": np.ascontiguousarray(np.asarray(full["L0_ffn_sw3"], f32)),
        "sw2": np.ascontiguousarray(np.asarray(full["L0_ffn_sw2"], f32)),
        "w1": np.ascontiguousarray(np.asarray(full["L0_ffn_w1"], f32)),
        "w2": np.ascontiguousarray(np.asarray(full["L0_ffn_w2"], f32)),
        "sb1c": np.ascontiguousarray(np.asarray(full["L0_ffn_sb1"], f32).reshape(16, 128).T),
        "sb3c": np.ascontiguousarray(np.asarray(full["L0_ffn_sb3"], f32).reshape(16, 128).T),
        "sb2c": np.ascontiguousarray(np.asarray(full["L0_ffn_sb2"], f32).reshape(8, 128).T),
        "b1c": np.ascontiguousarray(np.asarray(full["L0_ffn_b1"], f32).reshape(NE, 8, 128).transpose(0, 2, 1)),
        "b2c": np.ascontiguousarray(np.asarray(full["L0_ffn_b2"], f32).reshape(NE, 8, 128).transpose(0, 2, 1)),
        "tmaskD": np.ascontiguousarray(np.triu(np.ones((128, 128), f32))),
        "eyeD": np.eye(128, dtype=f32),
        "onesD": np.ones((128, 128), f32),
    }
    zeros_prev = np.zeros((D, T), f32)
    in_maps = []
    for c in range(8):
        b, half = c // 2, c % 2
        m = dict(shared)
        m["xin"] = np.ascontiguousarray(x[b, half * T:(half + 1) * T].T)
        m["xprev"] = (np.ascontiguousarray(x[b, 0:T].T) if half == 1 else zeros_prev)
        m["maskA"] = np.full((128, 1), float(half), f32)
        in_maps.append(m)
    return in_maps


def kernel(**inputs):
    nc = _get_nc(())
    in_maps = host_inputs(inputs)
    res = run_bass_kernel_spmd(nc, in_maps, core_ids=list(range(8)))
    outp = np.empty((4, 4096, 1024), np.float32)
    for c in range(8):
        b, half = c // 2, c % 2
        outp[b, half * T:(half + 1) * T] = res.results[c]["out"].T
    return outp


# version 15
# speedup vs baseline: 1.0962x; 1.0813x over previous
import numpy as np
import concourse.bass as bass
import concourse.mybir as mybir
from concourse.bass_utils import run_bass_kernel_spmd

F32 = mybir.dt.float32
F32R = mybir.dt.float32r
AF = mybir.ActivationFunctionType

D = 1024
T = 2048
H = 16
FC = 8
NSC = 16
NG = 4
HS = 2048
NE = 8
EPS_ATT = 1e-8
EPS_LN = 1e-5
BACK = 11016
PS = 2568


class EM:
    ENGS = ("sync", "scalar", "vector", "tensor", "gpsimd")

    def __init__(self):
        self.streams = {e: [] for e in self.ENGS}
        self.cnt = {ch: 0 for ch in ("w", "wA", "wB", "wX", "wS", "d", "e", "pe", "dv", "ac")}
        self.lastwait = {e: {ch: 0 for ch in self.cnt} for e in self.ENGS}

    def emit(self, eng, fn, inc=None, k=1):
        if inc is not None:
            if eng in ("vector", "scalar"):
                self.wait(eng, inc, self.cnt[inc])
            self.cnt[inc] += k
            def wrapped(e, sems, fn=fn, inc=inc, k=k):
                fn(e, sems).then_inc(sems[inc], k)
            self.streams[eng].append(wrapped)
            return self.cnt[inc]
        self.streams[eng].append(lambda e, sems, fn=fn: fn(e, sems))
        return None

    def wait(self, eng, ch, val):
        if val is None or val <= 0:
            return
        assert val <= self.cnt[ch], (eng, ch, val, self.cnt[ch])
        if self.lastwait[eng][ch] >= val:
            return
        self.lastwait[eng][ch] = val
        self.streams[eng].append(
            lambda e, sems, ch=ch, val=val: e.wait_ge(sems[ch], val))


class Banks:
    def __init__(self, em):
        self.em = em
        self.free_at = {}

    def acquire(self, eng, *keys):
        for key in keys:
            ent = self.free_at.get(key)
            if ent is not None:
                self.em.wait(eng, ent[0], ent[1])

    def release(self, key, ch, val):
        self.free_at[key] = (ch, val)


SCAN_N = NSC
APH_BC = True
APH_NRM = True
APH_PROJ = True
APH_STATE = True
SCAN_WO = True
SCAN_ATR = True
SCAN_AT = True
SCAN_OT = True
SCAN_DEN = True
SCAN_DELTA = True
SCAN_KTR = True


def build_nc(taps=(), limit=99):
    nc = bass.Bass("TRN2", target_bir_lowering=False)

    xin = nc.dram_tensor("xin", [D, T], F32, kind="ExternalInput").ap()
    xprev = nc.dram_tensor("xprev", [D, T], F32, kind="ExternalInput").ap()
    maskA = nc.dram_tensor("maskA", [128, 1], F32, kind="ExternalInput").ap()
    wqT = nc.dram_tensor("wqT", [D, D], F32, kind="ExternalInput").ap()
    wkT = nc.dram_tensor("wkT", [D, D], F32, kind="ExternalInput").ap()
    wvT = nc.dram_tensor("wvT", [D, D], F32, kind="ExternalInput").ap()
    woT = nc.dram_tensor("woT", [D, D], F32, kind="ExternalInput").ap()
    sw1 = nc.dram_tensor("sw1", [D, HS], F32, kind="ExternalInput").ap()
    sw3 = nc.dram_tensor("sw3", [D, HS], F32, kind="ExternalInput").ap()
    sw2 = nc.dram_tensor("sw2", [HS, D], F32, kind="ExternalInput").ap()
    w1 = nc.dram_tensor("w1", [NE, D, D], F32, kind="ExternalInput").ap()
    w2 = nc.dram_tensor("w2", [NE, D, D], F32, kind="ExternalInput").ap()
    sb1c = nc.dram_tensor("sb1c", [128, 16], F32, kind="ExternalInput").ap()
    sb3c = nc.dram_tensor("sb3c", [128, 16], F32, kind="ExternalInput").ap()
    sb2c = nc.dram_tensor("sb2c", [128, 8], F32, kind="ExternalInput").ap()
    b1c = nc.dram_tensor("b1c", [NE, 128, 8], F32, kind="ExternalInput").ap()
    b2c = nc.dram_tensor("b2c", [NE, 128, 8], F32, kind="ExternalInput").ap()
    tmaskD = nc.dram_tensor("tmaskD", [128, 128], F32, kind="ExternalInput").ap()
    eyeD = nc.dram_tensor("eyeD", [128, 128], F32, kind="ExternalInput").ap()
    onesD = nc.dram_tensor("onesD", [128, 128], F32, kind="ExternalInput").ap()
    out = nc.dram_tensor("out", [D, T], F32, kind="ExternalOutput").ap()

    spk = "ExternalOutput" if "spills" in taps else "Internal"
    q_sp = nc.dram_tensor("q_sp", [NG, FC, 128, 512], F32, kind=spk).ap()
    k_sp = nc.dram_tensor("k_sp", [NG, FC, 128, 512], F32, kind=spk).ap()
    v_sp = nc.dram_tensor("v_sp", [NSC, 128, 1040], F32, kind=spk).ap()

    tap_aps = {}
    for tname in taps:
        if tname == "spills":
            continue
        cols = BACK if tname == "back" else FC * T
        tap_aps[tname] = nc.dram_tensor(
            "tap_" + tname, [128, cols], F32, kind="ExternalOutput").ap()

    em = EM()
    bk = Banks(em)
    ctxs = []

    def sbuf(name, shape, dtype):
        cm = nc.sbuf_tensor(name, shape, dtype)
        ctxs.append(cm)
        return cm.__enter__()

    def psum(name, shape):
        cm = nc.psum_tensor(name, shape, F32)
        ctxs.append(cm)
        return cm.__enter__()

    xn = sbuf("xn", [128, FC * T], F32R)
    acc = sbuf("acc", [128, FC * T], F32)
    slotA = sbuf("slotA", [128, 4096], F32R)
    slotB = sbuf("slotB", [128, 4096], F32R)
    consts = sbuf("consts", [128, 128], F32)
    tmask = sbuf("tmask", [128, 128], F32)
    eyeS = sbuf("eyeS", [128, 128], F32)
    maskS = sbuf("maskS", [128, 1], F32)
    sb1s = sbuf("sb1s", [128, 16], F32)
    sb3s = sbuf("sb3s", [128, 16], F32)
    sb2s = sbuf("sb2s", [128, 8], F32)
    b1s = sbuf("b1s", [128, NE * 8], F32)
    b2s = sbuf("b2s", [128, NE * 8], F32)
    b2cl = sbuf("b2cl", [128, NE * 8], F32)
    epss = sbuf("epss", [128, 1], F32)
    onesR = sbuf("onesR", [128, 2], F32R)
    backing = sbuf("backing", [128, BACK], F32)
    base = nc.lookup_mloc(backing).addr

    def overlay(name, cols, dtype, coloff):
        return nc.alloc_sbuf_tensor_at(name, [128, cols], dtype,
                                       offset=base + coloff * 4)

    stS = overlay("stS", 520, F32, 0)
    stsA = overlay("stsA", 2048, F32, 520)
    stsB = overlay("stsB", 2048, F32, 520)
    xaS = overlay("xaS", 4096, F32, PS)
    sqA = overlay("sqA", 1024, F32R, PS + 4096)
    tmpA = overlay("tmpA", 1024, F32, PS + 5120)
    xaF = overlay("xaF", 1024, F32, PS)
    xnA = overlay("xnA", 1024, F32R, PS + 1024)
    teA = overlay("teA", 512, F32, PS + 2048)
    ktA = overlay("ktA", 512, F32, PS + 2560)
    vtA = overlay("vtA", 520, F32, PS + 3072)
    sqB = overlay("sqB", 1024, F32R, PS)
    tmpB = overlay("tmpB", 1024, F32, PS + 1024)
    nrmB = overlay("nrmB", 4096, F32, PS + 2048)
    stg = overlay("stg", 4096, F32, PS)
    trS = overlay("trS", 512, F32, PS + 4096)
    vstg = overlay("vstg", 2080, F32, PS + 4608)
    qS = overlay("qS", 1024, F32, PS)
    kfS = overlay("kfS", 1024, F32, PS + 1024)
    vtS = overlay("vtS", 1040, F32, PS + 2048)
    ktS = overlay("ktS", 1024, F32, PS + 3088)
    amS = overlay("amS", 1024, F32, PS + 4112)
    atkS = overlay("atkS", 1024, F32, PS + 5136)
    denS = overlay("denS", 32, F32, PS + 6160)
    afmS = overlay("afmS", 2048, F32R, PS + 6192)
    sbkS = overlay("sbkS", 2048, F32R, PS)
    s1S = overlay("s1S", 512, F32, PS + 2048)
    u2S = overlay("u2S", 512, F32, PS + 2560)
    sw2S = overlay("sw2S", 4096, F32R, PS + 3072)
    wxS = overlay("wxS", 4096, F32R, PS)
    heS = overlay("heS", 4096, F32R, PS + 4096)
    tmpM = overlay("tmpM", 256, F32, PS + 8192)

    P01 = psum("P01", [128, 1024])
    P23 = psum("P23", [128, 1024])
    P45 = psum("P45", [128, 1024])
    P6 = psum("P6", [128, 512])
    P7 = psum("P7", [128, 512])

    def pA(t, p0, c0, dims):
        stride = t.shape[1]
        return bass.AP(t, p0 * stride + c0, [[stride, dims[0]]] + dims[1:])

    def dma(eng, dst, src, ch=None):
        if ch is None:
            ch = "w" if eng == "gpsimd" else "d"
        return em.emit(eng, lambda e, s, d=dst, sr=src: e.dma_start(out=d, in_=sr),
                       inc=ch, k=16)

    def MM(lhsT, rhs, dst, start, stop):
        return em.emit("tensor", lambda e, s, l=lhsT, r=rhs, d=dst, st=start, sp=stop:
                       e.matmul(d, l, r, start=st, stop=sp), inc="pe")

    slot_free = {"A": 0, "B": 0}

    def wload(dst_ap, src_ap, wait_pe=None, slot=None):
        if wait_pe is None:
            wait_pe = slot_free.get(slot, em.cnt["pe"]) if slot else em.cnt["pe"]
        em.wait("gpsimd", "pe", wait_pe)
        ch = ("w" + slot) if slot in ("A", "B", "X", "S") else "w"
        return (ch, dma("gpsimd", dst_ap, src_ap, ch=ch))

    def slot_used(slot):
        slot_free[slot] = em.cnt["pe"]

    def wwait(eng, *tokens):
        for ch, val in tokens:
            em.wait(eng, ch, val)

    def whalf(wt, eg):
        return bass.AP(wt.tensor, eg * 512, [[1024, 128], [131072, 8], [1, 512]])

    def finish():
        dv_all = em.cnt["dv"]
        em.wait("sync", "dv", dv_all)
        em.wait("sync", "pe", em.cnt["pe"])
        em.wait("sync", "ac", em.cnt["ac"])
        dfin = dma("sync", bass.AP(out.tensor, 0, [[T, 128], [128 * T, 8], [1, T]]),
                   bass.AP(acc, 0, [[FC * T, 128], [T, 8], [1, T]]))
        if "back" in tap_aps:
            dfin = dma("sync", tap_aps["back"][:, :],
                       bass.AP(backing, 0, [[BACK, 128], [1, BACK]]))
        em.wait("sync", "d", dfin)
        semctxs, sems = [], {}
        for ch in em.cnt:
            cm = nc.semaphore("sem_" + ch)
            semctxs.append(cm)
            sems[ch] = cm.__enter__()
        blk_cm = nc.Block()
        block = blk_cm.__enter__()
        for eng in EM.ENGS:
            stream = em.streams[eng]
            getattr(block, eng)(lambda e, stream=stream: [f(e, sems) for f in stream])
        blk_cm.__exit__(None, None, None)
        for cm in semctxs:
            cm.__exit__(None, None, None)
        for cm in reversed(ctxs):
            cm.__exit__(None, None, None)
        return nc

    dma("gpsimd", consts[:, :], onesD)
    dma("gpsimd", tmask[:, :], tmaskD)
    dma("gpsimd", eyeS[:, :], eyeD)
    dma("gpsimd", maskS[:, :], maskA)
    dma("gpsimd", sb1s[:, :], sb1c)
    dma("gpsimd", sb3s[:, :], sb3c)
    dma("gpsimd", sb2s[:, :], sb2c)
    dma("gpsimd", b1s[:, :], bass.AP(b1c.tensor, 0, [[8, 128], [1024, 8], [1, 8]]))
    dma("gpsimd", onesR[:, :], bass.AP(onesD.tensor, 0, [[128, 128], [1, 2]]))
    wconst = dma("gpsimd", b2s[:, :], bass.AP(b2c.tensor, 0, [[8, 128], [1024, 8], [1, 8]]))

    dx = dma("sync", bass.AP(acc, 0, [[FC * T, 128], [T, 8], [1, T]]),
             bass.AP(xin.tensor, 0, [[T, 128], [128 * T, 8], [1, T]]), ch="e")

    em.wait("vector", "w", wconst)
    em.emit("vector", lambda e, s: e.memset(epss[:, :], EPS_LN), inc="dv")
    em.emit("vector", lambda e, s: e.tensor_scalar_mul(sb3s[:, :], sb3s[:, :], 0.5), inc="dv")
    em.emit("vector", lambda e, s: e.tensor_scalar_mul(sb2s[:, :], sb2s[:, :], 0.5), inc="dv")
    for r in range(8):
        e0, e1 = (3 * r) % 8, (3 * r - 1) % 8
        em.emit("vector", lambda e, s, r=r, e0=e0, e1=e1: e.tensor_add(
            b2cl[:, r * 8:(r + 1) * 8], b2s[:, e0 * 8:(e0 + 1) * 8],
            b2s[:, e1 * 8:(e1 + 1) * 8]), inc="dv")
    em.emit("vector", lambda e, s: e.tensor_scalar_mul(b2cl[:, :], b2cl[:, :], 0.25), inc="dv")
    em.emit("vector", lambda e, s: e.memset(stS[:, :], 0.0), inc="dv")
    dv_sinit = em.cnt["dv"]

    BPAIR = [((P01, 0, "P01a"), (P01, 512, "P01b")),
             ((P23, 0, "P23a"), (P23, 512, "P23b")),
             ((P45, 0, "P45a"), (P45, 512, "P45b")),
             ((P6, 0, "P6"), (P7, 0, "P7"))]

    def emit_stats(st, sq, tmp, src_chunk, pre=None):
        sq_mm = [0, 0]
        for g in range(NG):
            (bS, oS, kS), (bQ, oQ, kQ) = BPAIR[g]
            if pre is not None:
                dn = pre(g)
                em.wait("tensor", "d", dn)
                em.wait("scalar", "d", dn)
            bk.acquire("tensor", kS, kQ)
            for fc in range(FC):
                par = fc % 2
                src = src_chunk(g, fc)
                MM(consts[:, 0:1], src, pA(bS, 0, oS, [1, [1, 512]]),
                   start=(fc == 0), stop=(fc == FC - 1))
                em.wait("scalar", "pe", sq_mm[par])
                an = em.emit("scalar", lambda e, s, src=src, par=par: e.activation(
                    pA(sq, 0, par * 512, [128, [1, 512]]), src, AF.Square), inc="ac")
                em.wait("tensor", "ac", an)
                sq_mm[par] = MM(onesR[:, :], pA(sq, 0, par * 512, [128, [1, 512]]),
                                pA(bQ, 0, oQ, [2, [1, 512]]),
                                start=(fc == 0), stop=(fc == FC - 1))
        pe_done = em.cnt["pe"]
        em.wait("vector", "pe", pe_done)
        for g in range(NG):
            (bS, oS, kS), (bQ, oQ, kQ) = BPAIR[g]
            em.emit("vector", lambda e, s, bS=bS, oS=oS, g=g: e.tensor_scalar_mul(
                pA(st, 0, g * 512, [1, [1, 512]]),
                pA(bS, 0, oS, [1, [1, 512]]), 1.0 / D), inc="dv")
            dvq = em.emit("vector", lambda e, s, bQ=bQ, oQ=oQ: e.tensor_scalar_mul(
                pA(tmp, 0, 0, [1, [1, 512]]),
                pA(bQ, 0, oQ, [1, [1, 512]]), 1.0 / D), inc="dv")
            bk.release(kS, "dv", dvq)
            bk.release(kQ, "dv", dvq)
            em.emit("vector", lambda e, s, g=g: e.tensor_mul(
                pA(tmp, 0, 512, [1, [1, 512]]),
                pA(st, 0, g * 512, [1, [1, 512]]),
                pA(st, 0, g * 512, [1, [1, 512]])), inc="dv")
            dvv = em.emit("vector", lambda e, s: e.tensor_sub(
                pA(tmp, 0, 0, [1, [1, 512]]),
                pA(tmp, 0, 0, [1, [1, 512]]),
                pA(tmp, 0, 512, [1, [1, 512]])), inc="dv")
            em.wait("scalar", "dv", dvv)
            an = em.emit("scalar", lambda e, s: e.activation(
                pA(tmp, 0, 512, [1, [1, 512]]),
                pA(tmp, 0, 0, [1, [1, 512]]), AF.Sqrt, bias=epss[0:1, 0:1]), inc="ac")
            em.wait("vector", "ac", an)
            em.emit("vector", lambda e, s, g=g: e.reciprocal(
                pA(st, 32, g * 512, [1, [1, 512]]),
                pA(tmp, 0, 512, [1, [1, 512]])), inc="dv")
            em.emit("vector", lambda e, s: e.drain())
        return em.cnt["dv"]

    if limit <= 0:
        return finish()
    def a1_pre(g):
        if g > 0:
            em.wait("sync", "pe", em.cnt["pe"])
            em.wait("sync", "ac", em.cnt["ac"])
        return dma("sync", bass.AP(xaS, 0, [[4096, 128], [512, 8], [1, 512]]),
                   bass.AP(xprev.tensor, g * 512, [[T, 128], [128 * T, 8], [1, 512]]))

    em.wait("tensor", "w", wconst)
    dvA = emit_stats(stsA, sqA, tmpA,
                     lambda g, fc: pA(xaS, 0, fc * 512, [128, [1, 512]]),
                     pre=a1_pre)

    if limit <= 1:
        return finish()
    em.wait("gpsimd", "dv", dvA)
    em.wait("tensor", "dv", dvA)
    dv_prev_state = dv_sinit
    pe_prev_proj = em.cnt["pe"]
    em.wait("sync", "ac", em.cnt["ac"])
    for eg in range(2):
        wk = wload(slotA[:, :], whalf(wkT, eg), slot="A")
        wv = wload(slotB[:, :], whalf(wvT, eg), slot="B")
        wwait("tensor", wk, wv)
        for sc in range(NSC):
            t0 = sc * 128
            em.wait("sync", "pe", pe_prev_proj)
            dn = dma("sync", bass.AP(xaF, 0, [[1024, 128], [128, 8], [1, 128]]),
                     bass.AP(xprev.tensor, t0, [[T, 128], [128 * T, 8], [1, 128]]))
            if not APH_BC:
                continue
            bk.acquire("tensor", "P6", "P7")
            MM(consts[0:1, :], pA(stsA, 0, t0, [1, [1, 128]]),
               pA(P6, 0, 0, [128, [1, 128]]), True, True)
            bmr = MM(consts[32:33, :], pA(stsA, 32, t0, [1, [1, 128]]),
                     pA(P7, 0, 0, [128, [1, 128]]), True, True)
            if not APH_NRM:
                bk.release("P6", "pe", bmr)
                bk.release("P7", "pe", bmr)
                continue
            em.wait("vector", "d", dn)
            em.wait("vector", "pe", bmr)
            em.emit("vector", lambda e, s: e.tensor_sub(
                pA(xaF, 0, 0, [128, [128, 8], [1, 128]]),
                pA(xaF, 0, 0, [128, [128, 8], [1, 128]]),
                pA(P6, 0, 0, [128, [0, 8], [1, 128]])), inc="dv")
            dvn = em.emit("vector", lambda e, s: e.tensor_mul(
                pA(xnA, 0, 0, [128, [128, 8], [1, 128]]),
                pA(xaF, 0, 0, [128, [128, 8], [1, 128]]),
                pA(P7, 0, 0, [128, [0, 8], [1, 128]])), inc="dv")
            bk.release("P6", "dv", dvn)
            bk.release("P7", "dv", dvn)
            em.wait("tensor", "dv", dvn)
            if not APH_PROJ:
                continue
            bk.acquire("tensor", "P01a")
            for fc in range(FC):
                MM(pA(xnA, 0, fc * 128, [128, [1, 128]]),
                   pA(slotA, 0, fc * 512, [128, [1, 512]]),
                   pA(P01, 0, 0, [128, [1, 512]]),
                   start=(fc == 0), stop=(fc == FC - 1))
            pe_k = em.cnt["pe"]
            bk.acquire("tensor", "P23a")
            for fc in range(FC):
                MM(pA(xnA, 0, fc * 128, [128, [1, 128]]),
                   pA(slotB, 0, fc * 512, [128, [1, 512]]),
                   pA(P23, 0, 0, [128, [1, 512]]),
                   start=(fc == 0), stop=(fc == FC - 1))
            pe_v = em.cnt["pe"]
            pe_prev_proj = pe_v
            em.wait("scalar", "pe", pe_k)
            a1 = em.emit("scalar", lambda e, s: e.activation(
                teA[:, :], pA(P01, 0, 0, [128, [1, 512]]), AF.Exp), inc="ac")
            em.wait("vector", "ac", a1)
            em.emit("vector", lambda e, s: e.tensor_scalar_min(
                teA[:, :], teA[:, :], 1.0), inc="dv")
            em.emit("vector", lambda e, s: e.tensor_scalar_max(
                ktA[:, :], pA(P01, 0, 0, [128, [1, 512]]), 0.0), inc="dv")
            dvk = em.emit("vector", lambda e, s: e.tensor_add(
                ktA[:, :], ktA[:, :], teA[:, :]), inc="dv")
            bk.release("P01a", "dv", dvk)
            em.wait("vector", "pe", pe_v)
            em.emit("vector", lambda e, s: e.tensor_copy(
                pA(vtA, 0, 0, [128, [65, 8], [1, 64]]),
                pA(P23, 0, 0, [128, [64, 8], [1, 64]])), inc="dv")
            dvv = em.emit("vector", lambda e, s: e.memset(
                pA(vtA, 0, 64, [128, [65, 8], [1, 1]]), 1.0), inc="dv")
            bk.release("P23a", "dv", dvv)
            if not APH_STATE:
                continue
            em.wait("tensor", "dv", max(dvk, dvv, dv_prev_state))
            bk.acquire("tensor", "P45a")
            for hl in range(8):
                MM(pA(ktA, 0, (hl // 2) * 128 + 64 * (hl % 2), [128, [1, 64]]),
                   pA(vtA, 0, hl * 65, [128, [1, 65]]),
                   pA(P45, 64 * (hl % 2), 128 * (hl // 2), [64, [1, 65]]),
                   True, True)
            pe_s = em.cnt["pe"]
            em.wait("vector", "pe", pe_s)
            dv_prev_state = em.emit("vector", lambda e, s, eg=eg: e.tensor_add(
                pA(stS, 0, 65 * 4 * eg, [128, [65, 4], [1, 65]]),
                pA(stS, 0, 65 * 4 * eg, [128, [65, 4], [1, 65]]),
                pA(P45, 0, 0, [128, [128, 4], [1, 65]])), inc="dv")
            bk.release("P45a", "dv", dv_prev_state)
        slot_used("A")
        slot_used("B")
    em.wait("vector", "dv", dv_prev_state)
    dv_state = em.emit("vector", lambda e, s: e.tensor_scalar_mul(
        stS[:, :], stS[:, :], maskS[:, 0:1]), inc="dv")

    if limit <= 2:
        return finish()
    em.wait("vector", "e", dx)
    em.wait("tensor", "e", dx)
    em.wait("scalar", "e", dx)
    em.wait("scalar", "pe", em.cnt["pe"])
    d_all = em.cnt["d"]
    dv_stats = emit_stats(stsB, sqB, tmpB,
                          lambda g, fc: pA(acc, 0, fc * T + g * 512, [128, [1, 512]]),
                          pre=lambda g: d_all)
    em.wait("tensor", "dv", dv_stats)
    for g in range(NG):
        B = P01 if g % 2 == 0 else P23
        ka, kb = ("P01a", "P01b") if g % 2 == 0 else ("P23a", "P23b")
        bk.acquire("tensor", ka, kb)
        MM(consts[0:1, :], pA(stsB, 0, g * 512, [1, [1, 512]]),
           pA(B, 0, 0, [128, [1, 512]]), True, True)
        bmr = MM(consts[32:33, :], pA(stsB, 32, g * 512, [1, [1, 512]]),
                 pA(B, 0, 512, [128, [1, 512]]), True, True)
        em.wait("vector", "pe", bmr)
        em.emit("vector", lambda e, s, g=g, B=B: e.tensor_sub(
            pA(nrmB, 0, 0, [128, [512, 8], [1, 512]]),
            pA(acc, 0, g * 512, [128, [T, 8], [1, 512]]),
            pA(B, 0, 0, [128, [0, 8], [1, 512]])), inc="dv")
        dvn = em.emit("vector", lambda e, s, g=g, B=B: e.tensor_mul(
            pA(xn, 0, g * 512, [128, [T, 8], [1, 512]]),
            pA(nrmB, 0, 0, [128, [512, 8], [1, 512]]),
            pA(B, 0, 512, [128, [0, 8], [1, 512]])), inc="dv")
        bk.release(ka, "dv", dvn)
        bk.release(kb, "dv", dvn)
    dv_xn = em.cnt["dv"]
    accinit_jobs = []
    for mc in range(8):
        accinit_jobs.append(lambda mc=mc: em.emit("vector", lambda e, s, mc=mc: e.tensor_scalar_add(
            pA(acc, 0, mc * T, [128, [1, T]]), pA(acc, 0, mc * T, [128, [1, T]]),
            sb2s[:, mc:mc + 1]), inc="dv"))
    for r in range(8):
        for mc in range(8):
            accinit_jobs.append(lambda r=r, mc=mc: em.emit("vector", lambda e, s, r=r, mc=mc: e.tensor_scalar_add(
                pA(acc, 0, mc * T + r, [128, [8, 256]]),
                pA(acc, 0, mc * T + r, [128, [8, 256]]),
                b2cl[:, r * 8 + mc:r * 8 + mc + 1]), inc="dv"))

    if limit <= 3:
        return finish()
    def proj_fm(wsrc, spill):
        h1 = wload(slotA[:, :], whalf(wsrc, 0), slot="A")
        h2 = wload(slotB[:, :], whalf(wsrc, 1), slot="B")
        wwait("tensor", h1, h2)
        em.wait("tensor", "dv", dv_xn)
        em.wait("scalar", "d", em.cnt["d"])
        em.wait("vector", "d", em.cnt["d"])
        banks = [(P01, 0, "P01a"), (P01, 512, "P01b"),
                 (P23, 0, "P23a"), (P23, 512, "P23b")]
        spdmas = []
        hg = 0
        for g in range(NG):
            for ech in range(2):
                stgo = 2048 * (hg % 2)
                if hg >= 2:
                    em.wait("vector", "d", spdmas[hg - 1])
                    em.wait("scalar", "d", spdmas[hg - 1])
                for el in range(4):
                    ec = ech * 4 + el
                    slot = slotA if ec < 4 else slotB
                    b, o, key = banks[ec % 4]
                    bk.acquire("tensor", key)
                    for fc in range(FC):
                        MM(pA(slot, 0, fc * 512 + (ec % 4) * 128, [128, [1, 128]]),
                           pA(xn, 0, fc * T + g * 512, [128, [1, 512]]),
                           pA(b, 0, o, [128, [1, 512]]),
                           start=(fc == 0), stop=(fc == FC - 1))
                    pe_mm = em.cnt["pe"]
                    em.wait("scalar", "pe", pe_mm)
                    a1 = em.emit("scalar", lambda e, s, b=b, o=o, stgo=stgo, el=el: e.activation(
                        pA(stg, 0, stgo + el * 512, [128, [1, 512]]),
                        pA(b, 0, o, [128, [1, 512]]), AF.Exp), inc="ac")
                    em.wait("vector", "ac", a1)
                    em.emit("vector", lambda e, s, stgo=stgo, el=el: e.tensor_scalar_min(
                        pA(stg, 0, stgo + el * 512, [128, [1, 512]]),
                        pA(stg, 0, stgo + el * 512, [128, [1, 512]]), 1.0), inc="dv")
                    em.emit("vector", lambda e, s, b=b, o=o: e.tensor_scalar_max(
                        trS[:, :], pA(b, 0, o, [128, [1, 512]]), 0.0), inc="dv")
                    dve = em.emit("vector", lambda e, s, stgo=stgo, el=el: e.tensor_add(
                        pA(stg, 0, stgo + el * 512, [128, [1, 512]]),
                        pA(stg, 0, stgo + el * 512, [128, [1, 512]]),
                        trS[:, :]), inc="dv")
                    bk.release(key, "dv", dve)
                dv_g = em.cnt["dv"]
                em.wait("sync", "dv", dv_g)
                spdmas.append(dma(
                    "sync",
                    bass.AP(spill.tensor, g * 524288 + ech * 4 * 65536,
                            [[512, 128], [65536, 4], [1, 512]]),
                    pA(stg, 0, stgo, [128, [512, 4], [1, 512]])))
                hg += 1
        slot_used("A")
        slot_used("B")
        return spdmas[-1]

    proj_fm(wkT, k_sp)

    if limit <= 5:
        return finish()
    h1 = wload(slotA[:, :], whalf(wvT, 0), slot="A")
    h2 = wload(slotB[:, :], whalf(wvT, 1), slot="B")
    wwait("tensor", h1, h2)
    em.wait("vector", "d", em.cnt["d"])
    vdmas = []
    for sc in range(NSC):
        stgo = 1040 * (sc % 2)
        if sc >= 2:
            em.wait("vector", "d", vdmas[sc - 1])
        PB = P01 if sc % 2 == 0 else P23
        ka, kb = ("P01a", "P01b") if sc % 2 == 0 else ("P23a", "P23b")
        bk.acquire("tensor", ka, kb)
        for eg in range(2):
            slot = slotA if eg == 0 else slotB
            for fc in range(FC):
                MM(pA(xn, 0, fc * T + sc * 128, [128, [1, 128]]),
                   pA(slot, 0, fc * 512, [128, [1, 512]]),
                   pA(PB, 0, eg * 512, [128, [1, 512]]),
                   start=(fc == 0), stop=(fc == FC - 1))
        pe_v = em.cnt["pe"]
        em.wait("vector", "pe", pe_v)
        em.emit("vector", lambda e, s, stgo=stgo, PB=PB: e.tensor_copy(
            pA(vstg, 0, stgo, [128, [65, 16], [1, 64]]),
            pA(PB, 0, 0, [128, [64, 16], [1, 64]])), inc="dv")
        dvv = em.emit("vector", lambda e, s, stgo=stgo: e.memset(
            pA(vstg, 0, stgo + 64, [128, [65, 16], [1, 1]]), 1.0), inc="dv")
        bk.release(ka, "dv", dvv)
        bk.release(kb, "dv", dvv)
        em.wait("sync", "dv", dvv)
        vdmas.append(dma("sync",
                         bass.AP(v_sp.tensor, sc * (128 * 1040), [[1040, 128], [1, 1040]]),
                         pA(vstg, 0, stgo, [128, [1, 1040]])))
        for _ in range(5):
            if accinit_jobs:
                accinit_jobs.pop(0)()
    while accinit_jobs:
        accinit_jobs.pop(0)()
    dv_accinit = em.cnt["dv"]
    slot_used("A")
    slot_used("B")

    if limit <= 6:
        return finish()
    proj_fm(wqT, q_sp)

    if limit <= 7:
        return finish()
    wo1 = wload(slotA[:, :], whalf(woT, 0), slot="A")
    wo2 = wload(slotB[:, :], whalf(woT, 1), slot="B")
    em.wait("sync", "d", em.cnt["d"])
    wwait("tensor", wo1, wo2)

    use_q = use_k = use_v = 0
    for sc in range(SCAN_N):
        g, t0 = sc // 4, (sc % 4) * 128
        em.wait("sync", "pe", use_q)
        dq = dma("sync", bass.AP(qS, 0, [[1024, 128], [128, 8], [1, 128]]),
                 bass.AP(q_sp.tensor, g * 524288 + t0, [[512, 128], [65536, 8], [1, 128]]))
        em.wait("sync", "pe", use_k)
        dk = dma("sync", bass.AP(kfS, 0, [[1024, 128], [128, 8], [1, 128]]),
                 bass.AP(k_sp.tensor, g * 524288 + t0, [[512, 128], [65536, 8], [1, 128]]))
        em.wait("sync", "pe", use_v)
        dv1 = dma("sync", bass.AP(vtS, 0, [[1040, 128], [1, 520]]),
                  bass.AP(v_sp.tensor, sc * (128 * 1040), [[1040, 128], [1, 520]]))
        dv2 = dma("sync", bass.AP(vtS, 520, [[1040, 128], [1, 520]]),
                  bass.AP(v_sp.tensor, sc * (128 * 1040) + 520, [[1040, 128], [1, 520]]))
        em.wait("tensor", "d", dv2)
        dv_ktok = 0
        if SCAN_KTR:
            bk.acquire("tensor", "P45a", "P45b")
            for ec in range(FC):
                em.emit("tensor", lambda e, s, ec=ec: e.transpose(
                    pA(P45, 0, ec * 128, [128, [1, 128]]),
                    pA(kfS, 0, ec * 128, [128, [1, 128]]),
                    eyeS[:, :]), inc="pe")
            bk.acquire("tensor", "P01a")
            pe_ktr = em.emit("tensor", lambda e, s: e.transpose(
                pA(P01, 0, 0, [128, [1, 128]]), eyeS[:, :], eyeS[:, :]), inc="pe")
            em.wait("vector", "pe", pe_ktr)
            dv_ktok = em.emit("vector", lambda e, s: e.tensor_copy(
                ktS[:, :], pA(P45, 0, 0, [128, [1, 1024]])), inc="dv")
            bk.release("P45a", "dv", dv_ktok)
            bk.release("P45b", "dv", dv_ktok)
        for w in range(2):
            heads = list(range(w * 8, w * 8 + 8))
            dv_am = 0
            if SCAN_AT:
                bk.acquire("tensor", "P01a", "P01b")
                for h in heads:
                    l = h % 8
                    MM(pA(kfS, 64 * (h % 2), (h // 2) * 128, [64, [1, 128]]),
                       pA(qS, 64 * (h % 2), (h // 2) * 128, [64, [1, 128]]),
                       pA(P01, 0, (l % 2) * 512 + (l // 2) * 128, [128, [1, 128]]),
                       True, True)
                pe_at = em.cnt["pe"]
                em.wait("vector", "pe", pe_at)
                dv_am = em.emit("vector", lambda e, s: e.tensor_mul(
                    pA(amS, 0, 0, [128, [128, 8], [1, 128]]),
                    pA(P01, 0, 0, [128, [128, 8], [1, 128]]),
                    pA(tmask, 0, 0, [128, [0, 8], [1, 128]])), inc="dv")
                bk.release("P01a", "dv", dv_am)
                bk.release("P01b", "dv", dv_am)
            em.wait("tensor", "dv", max(dv_ktok, dv_state))
            em.wait("tensor", "d", dv2)
            if SCAN_DELTA:
                bk.acquire("tensor", "P6")
                for h in heads:
                    MM(pA(ktS, 0, (h // 2) * 128 + 64 * (h % 2), [128, [1, 64]]),
                       pA(vtS, 0, h * 65, [128, [1, 65]]),
                       pA(P6, 64 * (h % 2), 128 * ((h // 2) % 4), [64, [1, 65]]),
                       True, True)
                pe_sd = em.cnt["pe"]
            if SCAN_OT:
                em.wait("tensor", "dv", dv_am)
                bk.acquire("tensor", "P23a", "P23b")
                for h in heads:
                    l = h % 8
                    MM(pA(amS, 0, (l % 2) * 512 + (l // 2) * 128, [128, [1, 128]]),
                       pA(vtS, 0, h * 65, [128, [1, 65]]),
                       pA(P23, 0, 128 * (h % 8), [128, [1, 65]]),
                       True, False)
                    MM(pA(qS, 64 * (h % 2), (h // 2) * 128, [64, [1, 128]]),
                       pA(stS, 64 * (h % 2), 65 * (h // 2), [64, [1, 65]]),
                       pA(P23, 0, 128 * (h % 8), [128, [1, 65]]),
                       False, True)
                pe_o = em.cnt["pe"]
                em.wait("vector", "pe", pe_o)
                if SCAN_DEN:
                    em.emit("vector", lambda e, s, w=w: e.tensor_scalar_add(
                        pA(denS, 0, 8 * w, [128, [1, 8], [1, 1]]),
                        pA(P23, 0, 64, [128, [128, 8], [1, 1]]), EPS_ATT), inc="dv")
                    em.emit("vector", lambda e, s: e.drain())
                    em.emit("vector", lambda e, s, w=w: e.reciprocal(
                        pA(denS, 0, 16 + 8 * w, [128, [1, 8]]),
                        pA(denS, 0, 8 * w, [128, [1, 8]])), inc="dv")
                    em.emit("vector", lambda e, s: e.drain())
                dv_div = em.emit("vector", lambda e, s, w=w: e.tensor_mul(
                    pA(atkS, 0, 512 * w, [128, [64, 8], [1, 64]]),
                    pA(P23, 0, 0, [128, [128, 8], [1, 64]]),
                    pA(denS, 0, 16 + 8 * w, [128, [1, 8], [0, 64]])), inc="dv")
                bk.release("P23a", "dv", dv_div)
                bk.release("P23b", "dv", dv_div)
            if SCAN_DELTA:
                if w == 1:
                    use_q = pe_o
                    use_k = pe_at
                    use_v = pe_o
                em.wait("vector", "pe", max(pe_sd, pe_o))
                dv_state = em.emit("vector", lambda e, s, w=w: e.tensor_add(
                    pA(stS, 0, 65 * 4 * w, [128, [65, 4], [1, 65]]),
                    pA(stS, 0, 65 * 4 * w, [128, [65, 4], [1, 65]]),
                    pA(P6, 0, 0, [128, [128, 4], [1, 65]])), inc="dv")
                bk.release("P6", "dv", dv_state)
        em.wait("tensor", "dv", dv_div)
        if SCAN_ATR:
            bk.acquire("tensor", "P45a", "P45b")
            for ec in range(FC):
                em.emit("tensor", lambda e, s, ec=ec: e.transpose(
                    pA(P45, 0, ec * 128, [128, [1, 128]]),
                    pA(atkS, 0, ec * 128, [128, [1, 128]]),
                    eyeS[:, :]), inc="pe")
            bk.acquire("tensor", "P01a")
            pe_atr = em.emit("tensor", lambda e, s: e.transpose(
                pA(P01, 0, 0, [128, [1, 128]]), eyeS[:, :], eyeS[:, :]), inc="pe")
            em.wait("vector", "pe", pe_atr)
            dv_afm = em.emit("vector", lambda e, s, sc=sc: e.tensor_copy(
                pA(afmS, 0, (sc % 2) * 128, [128, [256, 8], [1, 128]]),
                pA(P45, 0, 0, [128, [128, 8], [1, 128]])), inc="dv")
            bk.release("P45a", "dv", dv_afm)
            bk.release("P45b", "dv", dv_afm)
        if SCAN_WO and SCAN_ATR and sc % 2 == 1:
            tg0 = (sc - 1) * 128
            em.wait("tensor", "dv", dv_afm)
            for mc in range(8):
                slot = slotA if mc < 4 else slotB
                key = "P7" if mc % 2 == 0 else "P6"
                B = P7 if mc % 2 == 0 else P6
                bk.acquire("tensor", key)
                for ec in range(FC):
                    MM(pA(slot, 0, ec * 512 + (mc % 4) * 128, [128, [1, 128]]),
                       pA(afmS, 0, ec * 256, [128, [1, 256]]),
                       pA(B, 0, 0, [128, [1, 256]]),
                       start=(ec == 0), stop=(ec == FC - 1))
                pe_wo = em.cnt["pe"]
                em.wait("vector", "pe", pe_wo)
                dv_wo = em.emit("vector", lambda e, s, mc=mc, tg0=tg0, B=B: e.tensor_add(
                    pA(acc, 0, mc * T + tg0, [128, [1, 256]]),
                    pA(acc, 0, mc * T + tg0, [128, [1, 256]]),
                    pA(B, 0, 0, [128, [1, 256]])), inc="dv")
                bk.release(key, "dv", dv_wo)
    slot_used("A")
    slot_used("B")
    dv_attn = em.cnt["dv"]
    if "accA" in tap_aps:
        em.wait("sync", "dv", dv_attn)
        dma("sync", tap_aps["accA"][:, :], bass.AP(acc, 0, [[FC * T, 128], [1, FC * T]]))

    if limit <= 8:
        return finish()
    prev_smul = 0
    for blk in range(4):
        l1 = wload(slotA[:, :],
                   bass.AP(sw1.tensor, blk * 512, [[2048, 128], [262144, 8], [1, 512]]),
                   slot="A")
        l3 = wload(slotB[:, :],
                   bass.AP(sw3.tensor, blk * 512, [[2048, 128], [262144, 8], [1, 512]]),
                   slot="B")
        l2 = wload(sw2S[:, :],
                   bass.AP(sw2.tensor, blk * 512 * 1024, [[1024, 128], [131072, 4], [1, 1024]]),
                   wait_pe=em.cnt["pe"], slot="S")
        wwait("tensor", l1, l3)
        for tg in range(NG):
            for hcl in range(4):
                hc = blk * 4 + hcl
                bk.acquire("tensor", "P01a")
                for fc in range(FC):
                    MM(pA(slotA, 0, fc * 512 + hcl * 128, [128, [1, 128]]),
                       pA(xn, 0, fc * T + tg * 512, [128, [1, 512]]),
                       pA(P01, 0, 0, [128, [1, 512]]),
                       start=(fc == 0), stop=(fc == FC - 1))
                pe_g = em.cnt["pe"]
                bk.acquire("tensor", "P01b")
                for fc in range(FC):
                    MM(pA(slotB, 0, fc * 512 + hcl * 128, [128, [1, 128]]),
                       pA(xn, 0, fc * T + tg * 512, [128, [1, 512]]),
                       pA(P01, 0, 512, [128, [1, 512]]),
                       start=(fc == 0), stop=(fc == FC - 1))
                pe_u = em.cnt["pe"]
                if tg == NG - 1 and hcl == 3:
                    slot_free["A"] = slot_free["B"] = pe_u
                em.wait("scalar", "pe", pe_g)
                em.wait("scalar", "dv", prev_smul)
                ag = em.emit("scalar", lambda e, s, hc=hc: e.activation(
                    s1S[:, :], pA(P01, 0, 0, [128, [1, 512]]), AF.Silu,
                    bias=sb1s[:, hc:hc + 1]), inc="ac")
                bk.release("P01a", "ac", ag)
                em.wait("scalar", "pe", pe_u)
                au = em.emit("scalar", lambda e, s, hc=hc: e.activation(
                    u2S[:, :], pA(P01, 0, 512, [128, [1, 512]]), AF.Identity,
                    bias=sb3s[:, hc:hc + 1], scale=0.5), inc="ac")
                bk.release("P01b", "ac", au)
                em.wait("vector", "ac", au)
                prev_smul = em.emit("vector", lambda e, s, hcl=hcl: e.tensor_mul(
                    pA(sbkS, 0, hcl * 512, [128, [1, 512]]),
                    s1S[:, :], u2S[:, :]), inc="dv")
            dv_s = em.cnt["dv"]
            em.wait("tensor", "dv", dv_s)
            if tg == 0:
                wwait("tensor", l2)
            for mc in range(8):
                key = "P23a" if mc % 2 == 0 else "P23b"
                bk.acquire("tensor", key)
                for hcl in range(4):
                    MM(pA(sw2S, 0, hcl * 1024 + mc * 128, [128, [1, 128]]),
                       pA(sbkS, 0, hcl * 512, [128, [1, 512]]),
                       pA(P23, 0, 512 * (mc % 2), [128, [1, 512]]),
                       start=(hcl == 0), stop=(hcl == 3))
                pe_y = em.cnt["pe"]
                em.wait("vector", "pe", pe_y)
                dv_y = em.emit("vector", lambda e, s, mc=mc, tg=tg: e.tensor_add(
                    pA(acc, 0, mc * T + tg * 512, [128, [1, 512]]),
                    pA(acc, 0, mc * T + tg * 512, [128, [1, 512]]),
                    pA(P23, 0, 512 * (mc % 2), [128, [1, 512]])), inc="dv")
                bk.release(key, "dv", dv_y)
    dv_shared = em.cnt["dv"]
    if "accS" in tap_aps:
        em.wait("sync", "dv", dv_shared)
        dma("sync", tap_aps["accS"][:, :], bass.AP(acc, 0, [[FC * T, 128], [1, FC * T]]))

    if limit <= 9:
        return finish()
    def moe_slot_full(key):
        return {"A": slotA, "B": slotB, "X": wxS}[key][:, :]

    def moe_tile(key, coloff, n):
        t = {"A": slotA, "B": slotB, "X": wxS}[key]
        return pA(t, 0, coloff, [128, [1, n]])

    def wsrc_half(wt, e_i, half):
        return bass.AP(wt.tensor, e_i * (D * D) + half * 512,
                       [[1024, 128], [131072, 8], [1, 512]])

    halves = []
    for e_i in range(NE):
        halves += [(e_i, "w1", 0), (e_i, "w1", 1), (e_i, "w2", 0), (e_i, "w2", 1)]
    ROT = ["A", "B", "X"]
    slot_of = {hv: ROT[i % 3] for i, hv in enumerate(halves)}
    loaded = {}
    free_pe = {"A": em.cnt["pe"], "B": em.cnt["pe"], "X": em.cnt["pe"]}
    next_load = 0

    def emit_loads(upto):
        nonlocal next_load
        while next_load < upto:
            hv = halves[next_load]
            key = slot_of[hv]
            wt = w1 if hv[1] == "w1" else w2
            loaded[hv] = wload(moe_slot_full(key), wsrc_half(wt, hv[0], hv[2]),
                               wait_pe=free_pe[key], slot=key)
            next_load += 1

    emit_loads(2)
    hbanks = [(P01, 0, "P01a"), (P01, 512, "P01b"), (P23, 0, "P23a"), (P23, 512, "P23b")]
    for e_i in range(NE):
        r0, r1 = (3 * e_i) % 8, (3 * e_i + 3) % 8
        for ci, r in enumerate((r0, r1)):
            for hc in range(FC):
                wwait("tensor", loaded[(e_i, "w1", hc // 4)])
                b, o, key = hbanks[(ci * FC + hc) % 4]
                bk.acquire("tensor", key)
                for fc in range(FC):
                    MM(moe_tile(slot_of[(e_i, "w1", hc // 4)],
                                fc * 512 + (hc % 4) * 128, 128),
                       bass.AP(xn, fc * T + r, [[FC * T, 128], [8, 256]]),
                       pA(b, 0, o, [128, [1, 256]]),
                       start=(fc == 0), stop=(fc == FC - 1))
                pe_h = em.cnt["pe"]
                if ci == 1 and hc == 3:
                    free_pe[slot_of[(e_i, "w1", 0)]] = pe_h
                    emit_loads(4 * e_i + 4)
                em.wait("scalar", "pe", pe_h)
                ah = em.emit("scalar", lambda e, s, b=b, o=o, hc=hc, ci=ci, e_i=e_i: e.activation(
                    pA(heS, 0, hc * 512 + ci * 256, [128, [1, 256]]),
                    pA(b, 0, o, [128, [1, 256]]), AF.Gelu,
                    bias=b1s[:, e_i * 8 + hc:e_i * 8 + hc + 1]), inc="ac")
                bk.release(key, "ac", ah)
        free_pe[slot_of[(e_i, "w1", 1)]] = em.cnt["pe"]
        ac_h = em.cnt["ac"]
        em.wait("tensor", "ac", ac_h)
        for ci, r in enumerate((r0, r1)):
            for mc in range(8):
                wwait("tensor", loaded[(e_i, "w2", mc // 4)])
                key = "P45a" if mc % 2 == 0 else "P45b"
                bk.acquire("tensor", key)
                for hc in range(FC):
                    MM(moe_tile(slot_of[(e_i, "w2", mc // 4)],
                                hc * 512 + (mc % 4) * 128, 128),
                       pA(heS, 0, hc * 512 + ci * 256, [128, [1, 256]]),
                       pA(P45, 0, 512 * (mc % 2), [128, [1, 256]]),
                       start=(hc == 0), stop=(hc == FC - 1))
                pe_y = em.cnt["pe"]
                if ci == 1 and mc == 3:
                    free_pe[slot_of[(e_i, "w2", 0)]] = pe_y
                    emit_loads(min(4 * e_i + 6, len(halves)))
                em.wait("vector", "pe", pe_y)
                em.emit("vector", lambda e, s, mc=mc: e.tensor_scalar_mul(
                    tmpM[:, :], pA(P45, 0, 512 * (mc % 2), [128, [1, 256]]), 0.25), inc="dv")
                dv_y = em.emit("vector", lambda e, s, mc=mc, r=r: e.tensor_add(
                    pA(acc, 0, mc * T + r, [128, [8, 256]]),
                    pA(acc, 0, mc * T + r, [128, [8, 256]]),
                    tmpM[:, :]), inc="dv")
                bk.release(key, "dv", dv_y)
        free_pe[slot_of[(e_i, "w2", 1)]] = em.cnt["pe"]

    dv_all = em.cnt["dv"]
    em.wait("sync", "dv", dv_all)
    dfinal = dma("sync", bass.AP(out.tensor, 0, [[T, 128], [128 * T, 8], [1, T]]),
                 bass.AP(acc, 0, [[FC * T, 128], [T, 8], [1, T]]))
    if "back" in tap_aps:
        em.wait("sync", "pe", em.cnt["pe"])
        dfinal = dma("sync", tap_aps["back"][:, :],
                     bass.AP(backing, 0, [[BACK, 128], [1, BACK]]))
    if "xn" in tap_aps:
        for q in range(4):
            em.wait("vector", "pe", em.cnt["pe"])
            dvq = em.emit("vector", lambda e, s, q=q: e.tensor_copy(
                bass.AP(stg, 0, [[4096, 128], [1, 4096]]),
                bass.AP(xn, q * 4096, [[FC * T, 128], [1, 4096]])), inc="dv")
            em.wait("sync", "dv", dvq)
            dt = dma("sync", bass.AP(tap_aps["xn"].tensor, q * 4096,
                                     [[FC * T, 128], [1, 4096]]),
                     bass.AP(stg, 0, [[4096, 128], [1, 4096]]))
            em.wait("vector", "d", dt)
    em.wait("sync", "d", dfinal)

    semctxs, sems = [], {}
    for ch in em.cnt:
        cm = nc.semaphore("sem_" + ch)
        semctxs.append(cm)
        sems[ch] = cm.__enter__()
    blk_cm = nc.Block()
    block = blk_cm.__enter__()
    for eng in EM.ENGS:
        stream = em.streams[eng]
        getattr(block, eng)(lambda e, stream=stream: [f(e, sems) for f in stream])
    blk_cm.__exit__(None, None, None)
    for cm in semctxs:
        cm.__exit__(None, None, None)
    for cm in reversed(ctxs):
        cm.__exit__(None, None, None)
    return nc


_CACHE = {}


def _get_nc(taps=()):
    key = tuple(taps)
    if key not in _CACHE:
        _CACHE[key] = build_nc(taps=key)
    return _CACHE[key]


def host_inputs(full):
    f32 = np.float32
    x = np.asarray(full["L0_attn_x"], f32)
    shared = {
        "wqT": np.ascontiguousarray(np.asarray(full["L0_attn_Wq"], f32).T),
        "wkT": np.ascontiguousarray(np.asarray(full["L0_attn_Wk"], f32).T),
        "wvT": np.ascontiguousarray(np.asarray(full["L0_attn_Wv"], f32).T),
        "woT": np.ascontiguousarray(np.asarray(full["L0_attn_Wo"], f32).T),
        "sw1": np.ascontiguousarray(np.asarray(full["L0_ffn_sw1"], f32)),
        "sw3": np.ascontiguousarray(np.asarray(full["L0_ffn_sw3"], f32)),
        "sw2": np.ascontiguousarray(np.asarray(full["L0_ffn_sw2"], f32)),
        "w1": np.ascontiguousarray(np.asarray(full["L0_ffn_w1"], f32)),
        "w2": np.ascontiguousarray(np.asarray(full["L0_ffn_w2"], f32)),
        "sb1c": np.ascontiguousarray(np.asarray(full["L0_ffn_sb1"], f32).reshape(16, 128).T),
        "sb3c": np.ascontiguousarray(np.asarray(full["L0_ffn_sb3"], f32).reshape(16, 128).T),
        "sb2c": np.ascontiguousarray(np.asarray(full["L0_ffn_sb2"], f32).reshape(8, 128).T),
        "b1c": np.ascontiguousarray(np.asarray(full["L0_ffn_b1"], f32).reshape(NE, 8, 128).transpose(0, 2, 1)),
        "b2c": np.ascontiguousarray(np.asarray(full["L0_ffn_b2"], f32).reshape(NE, 8, 128).transpose(0, 2, 1)),
        "tmaskD": np.ascontiguousarray(np.triu(np.ones((128, 128), f32))),
        "eyeD": np.eye(128, dtype=f32),
        "onesD": np.ones((128, 128), f32),
    }
    zeros_prev = np.zeros((D, T), f32)
    in_maps = []
    for c in range(8):
        b, half = c // 2, c % 2
        m = dict(shared)
        m["xin"] = np.ascontiguousarray(x[b, half * T:(half + 1) * T].T)
        m["xprev"] = (np.ascontiguousarray(x[b, 0:T].T) if half == 1 else zeros_prev)
        m["maskA"] = np.full((128, 1), float(half), f32)
        in_maps.append(m)
    return in_maps


def kernel(**inputs):
    nc = _get_nc(())
    in_maps = host_inputs(inputs)
    res = run_bass_kernel_spmd(nc, in_maps, core_ids=list(range(8)))
    outp = np.empty((4, 4096, 1024), np.float32)
    for c in range(8):
        b, half = c // 2, c % 2
        outp[b, half * T:(half + 1) * T] = res.results[c]["out"].T
    return outp


# version 16 (final)
# speedup vs baseline: 1.0964x; 1.0002x over previous
import numpy as np
import concourse.bass as bass
import concourse.mybir as mybir
from concourse.bass_utils import run_bass_kernel_spmd

F32 = mybir.dt.float32
F32R = mybir.dt.float32r
AF = mybir.ActivationFunctionType

D = 1024
T = 2048
H = 16
FC = 8
NSC = 16
NG = 4
HS = 2048
NE = 8
EPS_ATT = 1e-8
EPS_LN = 1e-5
BACK = 11016
PS = 2568


class EM:
    ENGS = ("sync", "scalar", "vector", "tensor", "gpsimd")

    def __init__(self):
        self.streams = {e: [] for e in self.ENGS}
        self.cnt = {ch: 0 for ch in ("w", "wA", "wB", "wX", "wS", "d", "e", "pe", "dv", "ac")}
        self.lastwait = {e: {ch: 0 for ch in self.cnt} for e in self.ENGS}

    def emit(self, eng, fn, inc=None, k=1):
        if inc is not None:
            if eng in ("vector", "scalar"):
                self.wait(eng, inc, self.cnt[inc])
            self.cnt[inc] += k
            def wrapped(e, sems, fn=fn, inc=inc, k=k):
                fn(e, sems).then_inc(sems[inc], k)
            self.streams[eng].append(wrapped)
            return self.cnt[inc]
        self.streams[eng].append(lambda e, sems, fn=fn: fn(e, sems))
        return None

    def wait(self, eng, ch, val):
        if val is None or val <= 0:
            return
        assert val <= self.cnt[ch], (eng, ch, val, self.cnt[ch])
        if self.lastwait[eng][ch] >= val:
            return
        self.lastwait[eng][ch] = val
        self.streams[eng].append(
            lambda e, sems, ch=ch, val=val: e.wait_ge(sems[ch], val))


class Banks:
    def __init__(self, em):
        self.em = em
        self.free_at = {}

    def acquire(self, eng, *keys):
        for key in keys:
            ent = self.free_at.get(key)
            if ent is not None:
                self.em.wait(eng, ent[0], ent[1])

    def release(self, key, ch, val):
        self.free_at[key] = (ch, val)


SCAN_N = NSC
APH_BC = True
APH_NRM = True
APH_PROJ = True
APH_STATE = True
SCAN_WO = True
SCAN_ATR = True
SCAN_AT = True
SCAN_OT = True
SCAN_DEN = True
SCAN_DELTA = True
SCAN_KTR = True


def build_nc(taps=(), limit=99):
    nc = bass.Bass("TRN2", target_bir_lowering=False)

    xin = nc.dram_tensor("xin", [D, T], F32, kind="ExternalInput").ap()
    xprev = nc.dram_tensor("xprev", [D, T], F32, kind="ExternalInput").ap()
    maskA = nc.dram_tensor("maskA", [128, 1], F32, kind="ExternalInput").ap()
    wqT = nc.dram_tensor("wqT", [D, D], F32, kind="ExternalInput").ap()
    wkT = nc.dram_tensor("wkT", [D, D], F32, kind="ExternalInput").ap()
    wvT = nc.dram_tensor("wvT", [D, D], F32, kind="ExternalInput").ap()
    woT = nc.dram_tensor("woT", [D, D], F32, kind="ExternalInput").ap()
    sw1 = nc.dram_tensor("sw1", [D, HS], F32, kind="ExternalInput").ap()
    sw3 = nc.dram_tensor("sw3", [D, HS], F32, kind="ExternalInput").ap()
    sw2 = nc.dram_tensor("sw2", [HS, D], F32, kind="ExternalInput").ap()
    w1 = nc.dram_tensor("w1", [NE, D, D], F32, kind="ExternalInput").ap()
    w2 = nc.dram_tensor("w2", [NE, D, D], F32, kind="ExternalInput").ap()
    sb1c = nc.dram_tensor("sb1c", [128, 16], F32, kind="ExternalInput").ap()
    sb3c = nc.dram_tensor("sb3c", [128, 16], F32, kind="ExternalInput").ap()
    sb2c = nc.dram_tensor("sb2c", [128, 8], F32, kind="ExternalInput").ap()
    b1c = nc.dram_tensor("b1c", [NE, 128, 8], F32, kind="ExternalInput").ap()
    b2c = nc.dram_tensor("b2c", [NE, 128, 8], F32, kind="ExternalInput").ap()
    tmaskD = nc.dram_tensor("tmaskD", [128, 128], F32, kind="ExternalInput").ap()
    eyeD = nc.dram_tensor("eyeD", [128, 128], F32, kind="ExternalInput").ap()
    onesD = nc.dram_tensor("onesD", [128, 128], F32, kind="ExternalInput").ap()
    out = nc.dram_tensor("out", [D, T], F32, kind="ExternalOutput").ap()

    spk = "ExternalOutput" if "spills" in taps else "Internal"
    q_sp = nc.dram_tensor("q_sp", [NG, FC, 128, 512], F32, kind=spk).ap()
    k_sp = nc.dram_tensor("k_sp", [NG, FC, 128, 512], F32, kind=spk).ap()
    v_sp = nc.dram_tensor("v_sp", [NSC, 128, 1040], F32, kind=spk).ap()

    tap_aps = {}
    for tname in taps:
        if tname == "spills":
            continue
        cols = BACK if tname == "back" else FC * T
        tap_aps[tname] = nc.dram_tensor(
            "tap_" + tname, [128, cols], F32, kind="ExternalOutput").ap()

    em = EM()
    bk = Banks(em)
    ctxs = []

    def sbuf(name, shape, dtype):
        cm = nc.sbuf_tensor(name, shape, dtype)
        ctxs.append(cm)
        return cm.__enter__()

    def psum(name, shape):
        cm = nc.psum_tensor(name, shape, F32)
        ctxs.append(cm)
        return cm.__enter__()

    xn = sbuf("xn", [128, FC * T], F32R)
    acc = sbuf("acc", [128, FC * T], F32)
    slotA = sbuf("slotA", [128, 4096], F32R)
    slotB = sbuf("slotB", [128, 4096], F32R)
    consts = sbuf("consts", [128, 128], F32)
    tmask = sbuf("tmask", [128, 128], F32)
    eyeS = sbuf("eyeS", [128, 128], F32)
    maskS = sbuf("maskS", [128, 1], F32)
    sb1s = sbuf("sb1s", [128, 16], F32)
    sb3s = sbuf("sb3s", [128, 16], F32)
    sb2s = sbuf("sb2s", [128, 8], F32)
    b1s = sbuf("b1s", [128, NE * 8], F32)
    b2s = sbuf("b2s", [128, NE * 8], F32)
    b2cl = sbuf("b2cl", [128, NE * 8], F32)
    epss = sbuf("epss", [128, 1], F32)
    onesR = sbuf("onesR", [128, 2], F32R)
    backing = sbuf("backing", [128, BACK], F32)
    base = nc.lookup_mloc(backing).addr

    def overlay(name, cols, dtype, coloff):
        return nc.alloc_sbuf_tensor_at(name, [128, cols], dtype,
                                       offset=base + coloff * 4)

    stS = overlay("stS", 520, F32, 0)
    stsA = overlay("stsA", 2048, F32, 520)
    stsB = overlay("stsB", 2048, F32, 520)
    xaS = overlay("xaS", 4096, F32, PS)
    sqA = overlay("sqA", 1024, F32R, PS + 4096)
    tmpA = overlay("tmpA", 1024, F32, PS + 5120)
    xaF = overlay("xaF", 1024, F32, PS)
    xnA = overlay("xnA", 1024, F32R, PS + 1024)
    teA = overlay("teA", 512, F32, PS + 2048)
    ktA = overlay("ktA", 512, F32, PS + 2560)
    vtA = overlay("vtA", 520, F32, PS + 3072)
    sqB = overlay("sqB", 1024, F32R, PS)
    tmpB = overlay("tmpB", 1024, F32, PS + 1024)
    nrmB = overlay("nrmB", 4096, F32, PS + 2048)
    stg = overlay("stg", 4096, F32, PS)
    trS = overlay("trS", 512, F32, PS + 4096)
    vstg = overlay("vstg", 2080, F32, PS + 4608)
    qS = overlay("qS", 1024, F32, PS)
    kfS = overlay("kfS", 1024, F32, PS + 1024)
    vtS = overlay("vtS", 1040, F32, PS + 2048)
    ktS = overlay("ktS", 1024, F32, PS + 3088)
    amS = overlay("amS", 1024, F32, PS + 4112)
    atkS = overlay("atkS", 1024, F32, PS + 5136)
    denS = overlay("denS", 32, F32, PS + 6160)
    afmS = overlay("afmS", 2048, F32R, PS + 6192)
    sbkS = overlay("sbkS", 2048, F32R, PS)
    s1S = overlay("s1S", 512, F32, PS + 2048)
    u2S = overlay("u2S", 512, F32, PS + 2560)
    sw2S = overlay("sw2S", 4096, F32R, PS + 3072)
    wxS = overlay("wxS", 4096, F32R, PS)
    heS = overlay("heS", 4096, F32R, PS + 4096)
    tmpM = overlay("tmpM", 256, F32, PS + 8192)

    P01 = psum("P01", [128, 1024])
    P23 = psum("P23", [128, 1024])
    P45 = psum("P45", [128, 1024])
    P6 = psum("P6", [128, 512])
    P7 = psum("P7", [128, 512])

    def pA(t, p0, c0, dims):
        stride = t.shape[1]
        return bass.AP(t, p0 * stride + c0, [[stride, dims[0]]] + dims[1:])

    def dma(eng, dst, src, ch=None):
        if ch is None:
            ch = "w" if eng == "gpsimd" else "d"
        return em.emit(eng, lambda e, s, d=dst, sr=src: e.dma_start(out=d, in_=sr),
                       inc=ch, k=16)

    def MM(lhsT, rhs, dst, start, stop):
        return em.emit("tensor", lambda e, s, l=lhsT, r=rhs, d=dst, st=start, sp=stop:
                       e.matmul(d, l, r, start=st, stop=sp), inc="pe")

    slot_free = {"A": 0, "B": 0}

    def wload(dst_ap, src_ap, wait_pe=None, slot=None):
        if wait_pe is None:
            wait_pe = slot_free.get(slot, em.cnt["pe"]) if slot else em.cnt["pe"]
        em.wait("gpsimd", "pe", wait_pe)
        ch = ("w" + slot) if slot in ("A", "B", "X", "S") else "w"
        return (ch, dma("gpsimd", dst_ap, src_ap, ch=ch))

    def slot_used(slot):
        slot_free[slot] = em.cnt["pe"]

    def wwait(eng, *tokens):
        for ch, val in tokens:
            em.wait(eng, ch, val)

    def whalf(wt, eg):
        return bass.AP(wt.tensor, eg * 512, [[1024, 128], [131072, 8], [1, 512]])

    def finish():
        dv_all = em.cnt["dv"]
        em.wait("sync", "dv", dv_all)
        em.wait("sync", "pe", em.cnt["pe"])
        em.wait("sync", "ac", em.cnt["ac"])
        dfin = dma("sync", bass.AP(out.tensor, 0, [[T, 128], [128 * T, 8], [1, T]]),
                   bass.AP(acc, 0, [[FC * T, 128], [T, 8], [1, T]]))
        if "back" in tap_aps:
            dfin = dma("sync", tap_aps["back"][:, :],
                       bass.AP(backing, 0, [[BACK, 128], [1, BACK]]))
        em.wait("sync", "d", dfin)
        semctxs, sems = [], {}
        for ch in em.cnt:
            cm = nc.semaphore("sem_" + ch)
            semctxs.append(cm)
            sems[ch] = cm.__enter__()
        blk_cm = nc.Block()
        block = blk_cm.__enter__()
        for eng in EM.ENGS:
            stream = em.streams[eng]
            getattr(block, eng)(lambda e, stream=stream: [f(e, sems) for f in stream])
        blk_cm.__exit__(None, None, None)
        for cm in semctxs:
            cm.__exit__(None, None, None)
        for cm in reversed(ctxs):
            cm.__exit__(None, None, None)
        return nc

    dma("gpsimd", consts[:, :], onesD)
    dma("gpsimd", tmask[:, :], tmaskD)
    dma("gpsimd", eyeS[:, :], eyeD)
    dma("gpsimd", maskS[:, :], maskA)
    dma("gpsimd", sb1s[:, :], sb1c)
    dma("gpsimd", sb3s[:, :], sb3c)
    dma("gpsimd", sb2s[:, :], sb2c)
    dma("gpsimd", b1s[:, :], bass.AP(b1c.tensor, 0, [[8, 128], [1024, 8], [1, 8]]))
    dma("gpsimd", onesR[:, :], bass.AP(onesD.tensor, 0, [[128, 128], [1, 2]]))
    wconst = dma("gpsimd", b2s[:, :], bass.AP(b2c.tensor, 0, [[8, 128], [1024, 8], [1, 8]]))

    dx = dma("sync", bass.AP(acc, 0, [[FC * T, 128], [T, 8], [1, T]]),
             bass.AP(xin.tensor, 0, [[T, 128], [128 * T, 8], [1, T]]), ch="e")

    em.wait("vector", "w", wconst)
    em.emit("vector", lambda e, s: e.memset(epss[:, :], EPS_LN), inc="dv")
    em.emit("vector", lambda e, s: e.tensor_scalar_mul(sb3s[:, :], sb3s[:, :], 0.5), inc="dv")
    em.emit("vector", lambda e, s: e.tensor_scalar_mul(sb2s[:, :], sb2s[:, :], 0.5), inc="dv")
    for r in range(8):
        e0, e1 = (3 * r) % 8, (3 * r - 1) % 8
        em.emit("vector", lambda e, s, r=r, e0=e0, e1=e1: e.tensor_add(
            b2cl[:, r * 8:(r + 1) * 8], b2s[:, e0 * 8:(e0 + 1) * 8],
            b2s[:, e1 * 8:(e1 + 1) * 8]), inc="dv")
    em.emit("vector", lambda e, s: e.tensor_scalar_mul(b2cl[:, :], b2cl[:, :], 0.25), inc="dv")
    em.emit("vector", lambda e, s: e.memset(stS[:, :], 0.0), inc="dv")
    dv_sinit = em.cnt["dv"]

    BPAIR = [((P01, 0, "P01a"), (P01, 512, "P01b")),
             ((P23, 0, "P23a"), (P23, 512, "P23b")),
             ((P45, 0, "P45a"), (P45, 512, "P45b")),
             ((P6, 0, "P6"), (P7, 0, "P7"))]

    def emit_stats(st, sq, tmp, src_chunk, pre=None):
        sq_mm = [0, 0]
        for g in range(NG):
            (bS, oS, kS), (bQ, oQ, kQ) = BPAIR[g]
            if pre is not None:
                dn = pre(g)
                em.wait("tensor", "d", dn)
                em.wait("scalar", "d", dn)
            bk.acquire("tensor", kS, kQ)
            for fc in range(FC):
                par = fc % 2
                src = src_chunk(g, fc)
                MM(consts[:, 0:1], src, pA(bS, 0, oS, [1, [1, 512]]),
                   start=(fc == 0), stop=(fc == FC - 1))
                em.wait("scalar", "pe", sq_mm[par])
                an = em.emit("scalar", lambda e, s, src=src, par=par: e.activation(
                    pA(sq, 0, par * 512, [128, [1, 512]]), src, AF.Square), inc="ac")
                em.wait("tensor", "ac", an)
                sq_mm[par] = MM(onesR[:, :], pA(sq, 0, par * 512, [128, [1, 512]]),
                                pA(bQ, 0, oQ, [2, [1, 512]]),
                                start=(fc == 0), stop=(fc == FC - 1))
        pe_done = em.cnt["pe"]
        em.wait("vector", "pe", pe_done)
        for g in range(NG):
            (bS, oS, kS), (bQ, oQ, kQ) = BPAIR[g]
            em.emit("vector", lambda e, s, bS=bS, oS=oS, g=g: e.tensor_scalar_mul(
                pA(st, 0, g * 512, [1, [1, 512]]),
                pA(bS, 0, oS, [1, [1, 512]]), 1.0 / D), inc="dv")
            dvq = em.emit("vector", lambda e, s, bQ=bQ, oQ=oQ: e.tensor_scalar_mul(
                pA(tmp, 0, 0, [1, [1, 512]]),
                pA(bQ, 0, oQ, [1, [1, 512]]), 1.0 / D), inc="dv")
            bk.release(kS, "dv", dvq)
            bk.release(kQ, "dv", dvq)
            em.emit("vector", lambda e, s, g=g: e.tensor_mul(
                pA(tmp, 0, 512, [1, [1, 512]]),
                pA(st, 0, g * 512, [1, [1, 512]]),
                pA(st, 0, g * 512, [1, [1, 512]])), inc="dv")
            dvv = em.emit("vector", lambda e, s: e.tensor_sub(
                pA(tmp, 0, 0, [1, [1, 512]]),
                pA(tmp, 0, 0, [1, [1, 512]]),
                pA(tmp, 0, 512, [1, [1, 512]])), inc="dv")
            em.wait("scalar", "dv", dvv)
            an = em.emit("scalar", lambda e, s: e.activation(
                pA(tmp, 0, 512, [1, [1, 512]]),
                pA(tmp, 0, 0, [1, [1, 512]]), AF.Sqrt, bias=epss[0:1, 0:1]), inc="ac")
            em.wait("vector", "ac", an)
            em.emit("vector", lambda e, s, g=g: e.reciprocal(
                pA(st, 32, g * 512, [1, [1, 512]]),
                pA(tmp, 0, 512, [1, [1, 512]])), inc="dv")
            em.emit("vector", lambda e, s: e.drain())
        return em.cnt["dv"]

    if limit <= 0:
        return finish()
    def a1_pre(g):
        if g > 0:
            em.wait("sync", "pe", em.cnt["pe"])
            em.wait("sync", "ac", em.cnt["ac"])
        return dma("sync", bass.AP(xaS, 0, [[4096, 128], [512, 8], [1, 512]]),
                   bass.AP(xprev.tensor, g * 512, [[T, 128], [128 * T, 8], [1, 512]]))

    em.wait("tensor", "w", wconst)
    dvA = emit_stats(stsA, sqA, tmpA,
                     lambda g, fc: pA(xaS, 0, fc * 512, [128, [1, 512]]),
                     pre=a1_pre)

    if limit <= 1:
        return finish()
    em.wait("gpsimd", "dv", dvA)
    em.wait("tensor", "dv", dvA)
    dv_prev_state = dv_sinit
    pe_prev_proj = em.cnt["pe"]
    em.wait("sync", "ac", em.cnt["ac"])
    for eg in range(2):
        wk = wload(slotA[:, :], whalf(wkT, eg), slot="A")
        wv = wload(slotB[:, :], whalf(wvT, eg), slot="B")
        wwait("tensor", wk, wv)
        for sc in range(NSC):
            t0 = sc * 128
            em.wait("sync", "pe", pe_prev_proj)
            dn = dma("sync", bass.AP(xaF, 0, [[1024, 128], [128, 8], [1, 128]]),
                     bass.AP(xprev.tensor, t0, [[T, 128], [128 * T, 8], [1, 128]]))
            if not APH_BC:
                continue
            bk.acquire("tensor", "P6", "P7")
            MM(consts[0:1, :], pA(stsA, 0, t0, [1, [1, 128]]),
               pA(P6, 0, 0, [128, [1, 128]]), True, True)
            bmr = MM(consts[32:33, :], pA(stsA, 32, t0, [1, [1, 128]]),
                     pA(P7, 0, 0, [128, [1, 128]]), True, True)
            if not APH_NRM:
                bk.release("P6", "pe", bmr)
                bk.release("P7", "pe", bmr)
                continue
            em.wait("vector", "d", dn)
            em.wait("vector", "pe", bmr)
            em.emit("vector", lambda e, s: e.tensor_sub(
                pA(xaF, 0, 0, [128, [128, 8], [1, 128]]),
                pA(xaF, 0, 0, [128, [128, 8], [1, 128]]),
                pA(P6, 0, 0, [128, [0, 8], [1, 128]])), inc="dv")
            dvn = em.emit("vector", lambda e, s: e.tensor_mul(
                pA(xnA, 0, 0, [128, [128, 8], [1, 128]]),
                pA(xaF, 0, 0, [128, [128, 8], [1, 128]]),
                pA(P7, 0, 0, [128, [0, 8], [1, 128]])), inc="dv")
            bk.release("P6", "dv", dvn)
            bk.release("P7", "dv", dvn)
            em.wait("tensor", "dv", dvn)
            if not APH_PROJ:
                continue
            bk.acquire("tensor", "P01a")
            for fc in range(FC):
                MM(pA(xnA, 0, fc * 128, [128, [1, 128]]),
                   pA(slotA, 0, fc * 512, [128, [1, 512]]),
                   pA(P01, 0, 0, [128, [1, 512]]),
                   start=(fc == 0), stop=(fc == FC - 1))
            pe_k = em.cnt["pe"]
            bk.acquire("tensor", "P23a")
            for fc in range(FC):
                MM(pA(xnA, 0, fc * 128, [128, [1, 128]]),
                   pA(slotB, 0, fc * 512, [128, [1, 512]]),
                   pA(P23, 0, 0, [128, [1, 512]]),
                   start=(fc == 0), stop=(fc == FC - 1))
            pe_v = em.cnt["pe"]
            pe_prev_proj = pe_v
            em.wait("scalar", "pe", pe_k)
            a1 = em.emit("scalar", lambda e, s: e.activation(
                teA[:, :], pA(P01, 0, 0, [128, [1, 512]]), AF.Exp), inc="ac")
            em.wait("vector", "ac", a1)
            em.emit("vector", lambda e, s: e.tensor_scalar_min(
                teA[:, :], teA[:, :], 1.0), inc="dv")
            em.emit("vector", lambda e, s: e.tensor_scalar_max(
                ktA[:, :], pA(P01, 0, 0, [128, [1, 512]]), 0.0), inc="dv")
            dvk = em.emit("vector", lambda e, s: e.tensor_add(
                ktA[:, :], ktA[:, :], teA[:, :]), inc="dv")
            bk.release("P01a", "dv", dvk)
            em.wait("vector", "pe", pe_v)
            em.emit("vector", lambda e, s: e.tensor_copy(
                pA(vtA, 0, 0, [128, [65, 8], [1, 64]]),
                pA(P23, 0, 0, [128, [64, 8], [1, 64]])), inc="dv")
            dvv = em.emit("vector", lambda e, s: e.memset(
                pA(vtA, 0, 64, [128, [65, 8], [1, 1]]), 1.0), inc="dv")
            bk.release("P23a", "dv", dvv)
            if not APH_STATE:
                continue
            em.wait("tensor", "dv", max(dvk, dvv, dv_prev_state))
            bk.acquire("tensor", "P45a")
            for hl in range(8):
                MM(pA(ktA, 0, (hl // 2) * 128 + 64 * (hl % 2), [128, [1, 64]]),
                   pA(vtA, 0, hl * 65, [128, [1, 65]]),
                   pA(P45, 64 * (hl % 2), 128 * (hl // 2), [64, [1, 65]]),
                   True, True)
            pe_s = em.cnt["pe"]
            em.wait("vector", "pe", pe_s)
            dv_prev_state = em.emit("vector", lambda e, s, eg=eg: e.tensor_add(
                pA(stS, 0, 65 * 4 * eg, [128, [65, 4], [1, 65]]),
                pA(stS, 0, 65 * 4 * eg, [128, [65, 4], [1, 65]]),
                pA(P45, 0, 0, [128, [128, 4], [1, 65]])), inc="dv")
            bk.release("P45a", "dv", dv_prev_state)
        slot_used("A")
        slot_used("B")
    em.wait("vector", "dv", dv_prev_state)
    dv_state = em.emit("vector", lambda e, s: e.tensor_scalar_mul(
        stS[:, :], stS[:, :], maskS[:, 0:1]), inc="dv")

    if limit <= 2:
        return finish()
    em.wait("vector", "e", dx)
    em.wait("tensor", "e", dx)
    em.wait("scalar", "e", dx)
    em.wait("scalar", "pe", em.cnt["pe"])
    d_all = em.cnt["d"]
    dv_stats = emit_stats(stsB, sqB, tmpB,
                          lambda g, fc: pA(acc, 0, fc * T + g * 512, [128, [1, 512]]),
                          pre=lambda g: d_all)
    em.wait("tensor", "dv", dv_stats)
    for g in range(NG):
        B = P01 if g % 2 == 0 else P23
        ka, kb = ("P01a", "P01b") if g % 2 == 0 else ("P23a", "P23b")
        bk.acquire("tensor", ka, kb)
        MM(consts[0:1, :], pA(stsB, 0, g * 512, [1, [1, 512]]),
           pA(B, 0, 0, [128, [1, 512]]), True, True)
        bmr = MM(consts[32:33, :], pA(stsB, 32, g * 512, [1, [1, 512]]),
                 pA(B, 0, 512, [128, [1, 512]]), True, True)
        em.wait("vector", "pe", bmr)
        em.emit("vector", lambda e, s, g=g, B=B: e.tensor_sub(
            pA(nrmB, 0, 0, [128, [512, 8], [1, 512]]),
            pA(acc, 0, g * 512, [128, [T, 8], [1, 512]]),
            pA(B, 0, 0, [128, [0, 8], [1, 512]])), inc="dv")
        dvn = em.emit("vector", lambda e, s, g=g, B=B: e.tensor_mul(
            pA(xn, 0, g * 512, [128, [T, 8], [1, 512]]),
            pA(nrmB, 0, 0, [128, [512, 8], [1, 512]]),
            pA(B, 0, 512, [128, [0, 8], [1, 512]])), inc="dv")
        bk.release(ka, "dv", dvn)
        bk.release(kb, "dv", dvn)
    dv_xn = em.cnt["dv"]
    accinit_jobs = []
    for mc in range(8):
        accinit_jobs.append(lambda mc=mc: em.emit("vector", lambda e, s, mc=mc: e.tensor_scalar_add(
            pA(acc, 0, mc * T, [128, [1, T]]), pA(acc, 0, mc * T, [128, [1, T]]),
            sb2s[:, mc:mc + 1]), inc="dv"))
    for r in range(8):
        for mc in range(8):
            accinit_jobs.append(lambda r=r, mc=mc: em.emit("vector", lambda e, s, r=r, mc=mc: e.tensor_scalar_add(
                pA(acc, 0, mc * T + r, [128, [8, 256]]),
                pA(acc, 0, mc * T + r, [128, [8, 256]]),
                b2cl[:, r * 8 + mc:r * 8 + mc + 1]), inc="dv"))

    if limit <= 3:
        return finish()
    def proj_fm(wsrc, spill):
        h1 = wload(slotA[:, :], whalf(wsrc, 0), slot="A")
        h2 = wload(slotB[:, :], whalf(wsrc, 1), slot="B")
        wwait("tensor", h1, h2)
        em.wait("tensor", "dv", dv_xn)
        em.wait("scalar", "d", em.cnt["d"])
        em.wait("vector", "d", em.cnt["d"])
        banks = [(P01, 0, "P01a"), (P01, 512, "P01b"),
                 (P23, 0, "P23a"), (P23, 512, "P23b")]
        spdmas = []
        hg = 0
        for g in range(NG):
            for ech in range(2):
                stgo = 2048 * (hg % 2)
                if hg >= 2:
                    em.wait("vector", "d", spdmas[hg - 1])
                    em.wait("scalar", "d", spdmas[hg - 1])
                for el in range(4):
                    ec = ech * 4 + el
                    slot = slotA if ec < 4 else slotB
                    b, o, key = banks[ec % 4]
                    bk.acquire("tensor", key)
                    for fc in range(FC):
                        MM(pA(slot, 0, fc * 512 + (ec % 4) * 128, [128, [1, 128]]),
                           pA(xn, 0, fc * T + g * 512, [128, [1, 512]]),
                           pA(b, 0, o, [128, [1, 512]]),
                           start=(fc == 0), stop=(fc == FC - 1))
                    pe_mm = em.cnt["pe"]
                    em.wait("scalar", "pe", pe_mm)
                    a1 = em.emit("scalar", lambda e, s, b=b, o=o, stgo=stgo, el=el: e.activation(
                        pA(stg, 0, stgo + el * 512, [128, [1, 512]]),
                        pA(b, 0, o, [128, [1, 512]]), AF.Exp), inc="ac")
                    em.wait("vector", "ac", a1)
                    em.emit("vector", lambda e, s, stgo=stgo, el=el: e.tensor_scalar_min(
                        pA(stg, 0, stgo + el * 512, [128, [1, 512]]),
                        pA(stg, 0, stgo + el * 512, [128, [1, 512]]), 1.0), inc="dv")
                    em.emit("vector", lambda e, s, b=b, o=o: e.tensor_scalar_max(
                        trS[:, :], pA(b, 0, o, [128, [1, 512]]), 0.0), inc="dv")
                    dve = em.emit("vector", lambda e, s, stgo=stgo, el=el: e.tensor_add(
                        pA(stg, 0, stgo + el * 512, [128, [1, 512]]),
                        pA(stg, 0, stgo + el * 512, [128, [1, 512]]),
                        trS[:, :]), inc="dv")
                    bk.release(key, "dv", dve)
                dv_g = em.cnt["dv"]
                em.wait("sync", "dv", dv_g)
                spdmas.append(dma(
                    "sync",
                    bass.AP(spill.tensor, g * 524288 + ech * 4 * 65536,
                            [[512, 128], [65536, 4], [1, 512]]),
                    pA(stg, 0, stgo, [128, [512, 4], [1, 512]])))
                hg += 1
        slot_used("A")
        slot_used("B")
        return spdmas[-1]

    proj_fm(wkT, k_sp)

    if limit <= 5:
        return finish()
    h1 = wload(slotA[:, :], whalf(wvT, 0), slot="A")
    h2 = wload(slotB[:, :], whalf(wvT, 1), slot="B")
    wwait("tensor", h1, h2)
    em.wait("vector", "d", em.cnt["d"])
    vdmas = []
    for sc in range(NSC):
        stgo = 1040 * (sc % 2)
        if sc >= 2:
            em.wait("vector", "d", vdmas[sc - 1])
        PB = P01 if sc % 2 == 0 else P23
        ka, kb = ("P01a", "P01b") if sc % 2 == 0 else ("P23a", "P23b")
        bk.acquire("tensor", ka, kb)
        for eg in range(2):
            slot = slotA if eg == 0 else slotB
            for fc in range(FC):
                MM(pA(xn, 0, fc * T + sc * 128, [128, [1, 128]]),
                   pA(slot, 0, fc * 512, [128, [1, 512]]),
                   pA(PB, 0, eg * 512, [128, [1, 512]]),
                   start=(fc == 0), stop=(fc == FC - 1))
        pe_v = em.cnt["pe"]
        em.wait("vector", "pe", pe_v)
        em.emit("vector", lambda e, s, stgo=stgo, PB=PB: e.tensor_copy(
            pA(vstg, 0, stgo, [128, [65, 16], [1, 64]]),
            pA(PB, 0, 0, [128, [64, 16], [1, 64]])), inc="dv")
        dvv = em.emit("vector", lambda e, s, stgo=stgo: e.memset(
            pA(vstg, 0, stgo + 64, [128, [65, 16], [1, 1]]), 1.0), inc="dv")
        bk.release(ka, "dv", dvv)
        bk.release(kb, "dv", dvv)
        em.wait("sync", "dv", dvv)
        vdmas.append(dma("sync",
                         bass.AP(v_sp.tensor, sc * (128 * 1040), [[1040, 128], [1, 1040]]),
                         pA(vstg, 0, stgo, [128, [1, 1040]])))
        for _ in range(5):
            if accinit_jobs:
                accinit_jobs.pop(0)()
    while accinit_jobs:
        accinit_jobs.pop(0)()
    dv_accinit = em.cnt["dv"]
    slot_used("A")
    slot_used("B")

    if limit <= 6:
        return finish()
    proj_fm(wqT, q_sp)

    if limit <= 7:
        return finish()
    wo1 = wload(slotA[:, :], whalf(woT, 0), slot="A")
    wo2 = wload(slotB[:, :], whalf(woT, 1), slot="B")
    em.wait("sync", "d", em.cnt["d"])
    wwait("tensor", wo1, wo2)

    use_q = use_k = use_v = 0
    for sc in range(SCAN_N):
        g, t0 = sc // 4, (sc % 4) * 128
        em.wait("sync", "pe", use_q)
        dq = dma("sync", bass.AP(qS, 0, [[1024, 128], [128, 8], [1, 128]]),
                 bass.AP(q_sp.tensor, g * 524288 + t0, [[512, 128], [65536, 8], [1, 128]]))
        em.wait("sync", "pe", use_k)
        dk = dma("sync", bass.AP(kfS, 0, [[1024, 128], [128, 8], [1, 128]]),
                 bass.AP(k_sp.tensor, g * 524288 + t0, [[512, 128], [65536, 8], [1, 128]]))
        em.wait("sync", "pe", use_v)
        dv1 = dma("sync", bass.AP(vtS, 0, [[1040, 128], [1, 520]]),
                  bass.AP(v_sp.tensor, sc * (128 * 1040), [[1040, 128], [1, 520]]))
        dv2 = dma("sync", bass.AP(vtS, 520, [[1040, 128], [1, 520]]),
                  bass.AP(v_sp.tensor, sc * (128 * 1040) + 520, [[1040, 128], [1, 520]]))
        em.wait("tensor", "d", dv2)
        dv_ktok = 0
        if SCAN_KTR:
            bk.acquire("tensor", "P45a", "P45b")
            for ec in range(FC):
                em.emit("tensor", lambda e, s, ec=ec: e.transpose(
                    pA(P45, 0, ec * 128, [128, [1, 128]]),
                    pA(kfS, 0, ec * 128, [128, [1, 128]]),
                    eyeS[:, :]), inc="pe")
            bk.acquire("tensor", "P01a")
            pe_ktr = em.emit("tensor", lambda e, s: e.transpose(
                pA(P01, 0, 0, [128, [1, 128]]), eyeS[:, :], eyeS[:, :]), inc="pe")
            em.wait("vector", "pe", pe_ktr)
            dv_ktok = em.emit("vector", lambda e, s: e.tensor_copy(
                ktS[:, :], pA(P45, 0, 0, [128, [1, 1024]])), inc="dv")
            bk.release("P45a", "dv", dv_ktok)
            bk.release("P45b", "dv", dv_ktok)
        for w in range(2):
            heads = list(range(w * 8, w * 8 + 8))
            dv_am = 0
            if SCAN_AT:
                bk.acquire("tensor", "P01a", "P01b")
                for h in heads:
                    l = h % 8
                    MM(pA(kfS, 64 * (h % 2), (h // 2) * 128, [64, [1, 128]]),
                       pA(qS, 64 * (h % 2), (h // 2) * 128, [64, [1, 128]]),
                       pA(P01, 0, (l % 2) * 512 + (l // 2) * 128, [128, [1, 128]]),
                       True, True)
                pe_at = em.cnt["pe"]
                em.wait("vector", "pe", pe_at)
                dv_am = em.emit("vector", lambda e, s: e.tensor_mul(
                    pA(amS, 0, 0, [128, [128, 8], [1, 128]]),
                    pA(P01, 0, 0, [128, [128, 8], [1, 128]]),
                    pA(tmask, 0, 0, [128, [0, 8], [1, 128]])), inc="dv")
                bk.release("P01a", "dv", dv_am)
                bk.release("P01b", "dv", dv_am)
            em.wait("tensor", "dv", max(dv_ktok, dv_state))
            em.wait("tensor", "d", dv2)
            if SCAN_DELTA:
                bk.acquire("tensor", "P6")
                for h in heads:
                    MM(pA(ktS, 0, (h // 2) * 128 + 64 * (h % 2), [128, [1, 64]]),
                       pA(vtS, 0, h * 65, [128, [1, 65]]),
                       pA(P6, 64 * (h % 2), 128 * ((h // 2) % 4), [64, [1, 65]]),
                       True, True)
                pe_sd = em.cnt["pe"]
            if SCAN_OT:
                em.wait("tensor", "dv", dv_am)
                bk.acquire("tensor", "P23a", "P23b")
                for h in heads:
                    l = h % 8
                    MM(pA(amS, 0, (l % 2) * 512 + (l // 2) * 128, [128, [1, 128]]),
                       pA(vtS, 0, h * 65, [128, [1, 65]]),
                       pA(P23, 0, 128 * (h % 8), [128, [1, 65]]),
                       True, False)
                    MM(pA(qS, 64 * (h % 2), (h // 2) * 128, [64, [1, 128]]),
                       pA(stS, 64 * (h % 2), 65 * (h // 2), [64, [1, 65]]),
                       pA(P23, 0, 128 * (h % 8), [128, [1, 65]]),
                       False, True)
                pe_o = em.cnt["pe"]
                em.wait("vector", "pe", pe_o)
                if SCAN_DEN:
                    em.emit("vector", lambda e, s, w=w: e.tensor_scalar_add(
                        pA(denS, 0, 8 * w, [128, [1, 8], [1, 1]]),
                        pA(P23, 0, 64, [128, [128, 8], [1, 1]]), EPS_ATT), inc="dv")
                    em.emit("vector", lambda e, s: e.drain())
                    em.emit("vector", lambda e, s, w=w: e.reciprocal(
                        pA(denS, 0, 16 + 8 * w, [128, [1, 8]]),
                        pA(denS, 0, 8 * w, [128, [1, 8]])), inc="dv")
                    em.emit("vector", lambda e, s: e.drain())
                dv_div = em.emit("vector", lambda e, s, w=w: e.tensor_mul(
                    pA(atkS, 0, 512 * w, [128, [64, 8], [1, 64]]),
                    pA(P23, 0, 0, [128, [128, 8], [1, 64]]),
                    pA(denS, 0, 16 + 8 * w, [128, [1, 8], [0, 64]])), inc="dv")
                bk.release("P23a", "dv", dv_div)
                bk.release("P23b", "dv", dv_div)
            if SCAN_DELTA:
                if w == 1:
                    use_q = pe_o
                    use_k = pe_at
                    use_v = pe_o
                em.wait("vector", "pe", max(pe_sd, pe_o))
                dv_state = em.emit("vector", lambda e, s, w=w: e.tensor_add(
                    pA(stS, 0, 65 * 4 * w, [128, [65, 4], [1, 65]]),
                    pA(stS, 0, 65 * 4 * w, [128, [65, 4], [1, 65]]),
                    pA(P6, 0, 0, [128, [128, 4], [1, 65]])), inc="dv")
                bk.release("P6", "dv", dv_state)
        em.wait("tensor", "dv", dv_div)
        if SCAN_ATR:
            bk.acquire("tensor", "P45a", "P45b")
            for ec in range(FC):
                em.emit("tensor", lambda e, s, ec=ec: e.transpose(
                    pA(P45, 0, ec * 128, [128, [1, 128]]),
                    pA(atkS, 0, ec * 128, [128, [1, 128]]),
                    eyeS[:, :]), inc="pe")
            bk.acquire("tensor", "P01a")
            pe_atr = em.emit("tensor", lambda e, s: e.transpose(
                pA(P01, 0, 0, [128, [1, 128]]), eyeS[:, :], eyeS[:, :]), inc="pe")
            em.wait("vector", "pe", pe_atr)
            dv_afm = em.emit("vector", lambda e, s, sc=sc: e.tensor_copy(
                pA(afmS, 0, (sc % 2) * 128, [128, [256, 8], [1, 128]]),
                pA(P45, 0, 0, [128, [128, 8], [1, 128]])), inc="dv")
            bk.release("P45a", "dv", dv_afm)
            bk.release("P45b", "dv", dv_afm)
        if SCAN_WO and SCAN_ATR and sc % 2 == 1:
            tg0 = (sc - 1) * 128
            em.wait("tensor", "dv", dv_afm)
            for mc in range(8):
                slot = slotA if mc < 4 else slotB
                key = "P7" if mc % 2 == 0 else "P6"
                B = P7 if mc % 2 == 0 else P6
                bk.acquire("tensor", key)
                for ec in range(FC):
                    MM(pA(slot, 0, ec * 512 + (mc % 4) * 128, [128, [1, 128]]),
                       pA(afmS, 0, ec * 256, [128, [1, 256]]),
                       pA(B, 0, 0, [128, [1, 256]]),
                       start=(ec == 0), stop=(ec == FC - 1))
                pe_wo = em.cnt["pe"]
                em.wait("vector", "pe", pe_wo)
                dv_wo = em.emit("vector", lambda e, s, mc=mc, tg0=tg0, B=B: e.tensor_add(
                    pA(acc, 0, mc * T + tg0, [128, [1, 256]]),
                    pA(acc, 0, mc * T + tg0, [128, [1, 256]]),
                    pA(B, 0, 0, [128, [1, 256]])), inc="dv")
                bk.release(key, "dv", dv_wo)
    slot_used("A")
    slot_used("B")
    dv_attn = em.cnt["dv"]
    if "accA" in tap_aps:
        em.wait("sync", "dv", dv_attn)
        dma("sync", tap_aps["accA"][:, :], bass.AP(acc, 0, [[FC * T, 128], [1, FC * T]]))

    if limit <= 8:
        return finish()
    prev_smul = 0
    for blk in range(4):
        l1 = wload(slotA[:, :],
                   bass.AP(sw1.tensor, blk * 512, [[2048, 128], [262144, 8], [1, 512]]),
                   slot="A")
        l3 = wload(slotB[:, :],
                   bass.AP(sw3.tensor, blk * 512, [[2048, 128], [262144, 8], [1, 512]]),
                   slot="B")
        l2 = wload(sw2S[:, :],
                   bass.AP(sw2.tensor, blk * 512 * 1024, [[1024, 128], [131072, 4], [1, 1024]]),
                   wait_pe=em.cnt["pe"], slot="S")
        wwait("tensor", l1, l3)
        for tg in range(NG):
            for hcl in range(4):
                hc = blk * 4 + hcl
                bk.acquire("tensor", "P01a")
                for fc in range(FC):
                    MM(pA(slotA, 0, fc * 512 + hcl * 128, [128, [1, 128]]),
                       pA(xn, 0, fc * T + tg * 512, [128, [1, 512]]),
                       pA(P01, 0, 0, [128, [1, 512]]),
                       start=(fc == 0), stop=(fc == FC - 1))
                pe_g = em.cnt["pe"]
                bk.acquire("tensor", "P01b")
                for fc in range(FC):
                    MM(pA(slotB, 0, fc * 512 + hcl * 128, [128, [1, 128]]),
                       pA(xn, 0, fc * T + tg * 512, [128, [1, 512]]),
                       pA(P01, 0, 512, [128, [1, 512]]),
                       start=(fc == 0), stop=(fc == FC - 1))
                pe_u = em.cnt["pe"]
                if tg == NG - 1 and hcl == 3:
                    slot_free["A"] = slot_free["B"] = pe_u
                em.wait("scalar", "pe", pe_g)
                em.wait("scalar", "dv", prev_smul)
                ag = em.emit("scalar", lambda e, s, hc=hc: e.activation(
                    s1S[:, :], pA(P01, 0, 0, [128, [1, 512]]), AF.Silu,
                    bias=sb1s[:, hc:hc + 1]), inc="ac")
                bk.release("P01a", "ac", ag)
                em.wait("scalar", "pe", pe_u)
                au = em.emit("scalar", lambda e, s, hc=hc: e.activation(
                    u2S[:, :], pA(P01, 0, 512, [128, [1, 512]]), AF.Identity,
                    bias=sb3s[:, hc:hc + 1], scale=0.5), inc="ac")
                bk.release("P01b", "ac", au)
                em.wait("vector", "ac", au)
                prev_smul = em.emit("vector", lambda e, s, hcl=hcl: e.tensor_mul(
                    pA(sbkS, 0, hcl * 512, [128, [1, 512]]),
                    s1S[:, :], u2S[:, :]), inc="dv")
            dv_s = em.cnt["dv"]
            em.wait("tensor", "dv", dv_s)
            if tg == 0:
                wwait("tensor", l2)
            for mc in range(8):
                key = "P23a" if mc % 2 == 0 else "P23b"
                bk.acquire("tensor", key)
                for hcl in range(4):
                    MM(pA(sw2S, 0, hcl * 1024 + mc * 128, [128, [1, 128]]),
                       pA(sbkS, 0, hcl * 512, [128, [1, 512]]),
                       pA(P23, 0, 512 * (mc % 2), [128, [1, 512]]),
                       start=(hcl == 0), stop=(hcl == 3))
                pe_y = em.cnt["pe"]
                em.wait("vector", "pe", pe_y)
                dv_y = em.emit("vector", lambda e, s, mc=mc, tg=tg: e.tensor_add(
                    pA(acc, 0, mc * T + tg * 512, [128, [1, 512]]),
                    pA(acc, 0, mc * T + tg * 512, [128, [1, 512]]),
                    pA(P23, 0, 512 * (mc % 2), [128, [1, 512]])), inc="dv")
                bk.release(key, "dv", dv_y)
    dv_shared = em.cnt["dv"]
    if "accS" in tap_aps:
        em.wait("sync", "dv", dv_shared)
        dma("sync", tap_aps["accS"][:, :], bass.AP(acc, 0, [[FC * T, 128], [1, FC * T]]))

    if limit <= 9:
        return finish()
    def moe_slot_full(key):
        return {"A": slotA, "B": slotB, "X": wxS}[key][:, :]

    def moe_tile(key, coloff, n):
        t = {"A": slotA, "B": slotB, "X": wxS}[key]
        return pA(t, 0, coloff, [128, [1, n]])

    def wsrc_half(wt, e_i, half):
        return bass.AP(wt.tensor, e_i * (D * D) + half * 512,
                       [[1024, 128], [131072, 8], [1, 512]])

    halves = []
    for e_i in range(NE):
        halves += [(e_i, "w1", 0), (e_i, "w1", 1), (e_i, "w2", 0), (e_i, "w2", 1)]
    ROT = ["A", "B", "X"]
    slot_of = {hv: ROT[i % 3] for i, hv in enumerate(halves)}
    loaded = {}
    free_pe = {"A": em.cnt["pe"], "B": em.cnt["pe"], "X": em.cnt["pe"]}
    next_load = 0

    def emit_loads(upto):
        nonlocal next_load
        while next_load < upto:
            hv = halves[next_load]
            key = slot_of[hv]
            wt = w1 if hv[1] == "w1" else w2
            loaded[hv] = wload(moe_slot_full(key), wsrc_half(wt, hv[0], hv[2]),
                               wait_pe=free_pe[key], slot=key)
            next_load += 1

    emit_loads(2)
    hbanks = [(P01, 0, "P01a"), (P01, 512, "P01b"), (P23, 0, "P23a"), (P23, 512, "P23b")]
    for e_i in range(NE):
        r0, r1 = (3 * e_i) % 8, (3 * e_i + 3) % 8
        for ci, r in enumerate((r0, r1)):
            for hc in range(FC):
                wwait("tensor", loaded[(e_i, "w1", hc // 4)])
                b, o, key = hbanks[(ci * FC + hc) % 4]
                bk.acquire("tensor", key)
                for fc in range(FC):
                    MM(moe_tile(slot_of[(e_i, "w1", hc // 4)],
                                fc * 512 + (hc % 4) * 128, 128),
                       bass.AP(xn, fc * T + r, [[FC * T, 128], [8, 256]]),
                       pA(b, 0, o, [128, [1, 256]]),
                       start=(fc == 0), stop=(fc == FC - 1))
                pe_h = em.cnt["pe"]
                if ci == 1 and hc == 3:
                    free_pe[slot_of[(e_i, "w1", 0)]] = pe_h
                    emit_loads(4 * e_i + 4)
                em.wait("scalar", "pe", pe_h)
                ah = em.emit("scalar", lambda e, s, b=b, o=o, hc=hc, ci=ci, e_i=e_i: e.activation(
                    pA(heS, 0, hc * 512 + ci * 256, [128, [1, 256]]),
                    pA(b, 0, o, [128, [1, 256]]), AF.Gelu,
                    bias=b1s[:, e_i * 8 + hc:e_i * 8 + hc + 1]), inc="ac")
                bk.release(key, "ac", ah)
        free_pe[slot_of[(e_i, "w1", 1)]] = em.cnt["pe"]
        ac_h = em.cnt["ac"]
        em.wait("tensor", "ac", ac_h)
        for mc in range(8):
            wwait("tensor", loaded[(e_i, "w2", mc // 4)])
            key = "P45a" if mc % 2 == 0 else "P45b"
            bk.acquire("tensor", key)
            for hc in range(FC):
                MM(moe_tile(slot_of[(e_i, "w2", mc // 4)],
                            hc * 512 + (mc % 4) * 128, 128),
                   pA(heS, 0, hc * 512, [128, [1, 512]]),
                   pA(P45, 0, 512 * (mc % 2), [128, [1, 512]]),
                   start=(hc == 0), stop=(hc == FC - 1))
            pe_y = em.cnt["pe"]
            if mc == 3:
                free_pe[slot_of[(e_i, "w2", 0)]] = pe_y
                emit_loads(min(4 * e_i + 6, len(halves)))
            em.wait("vector", "pe", pe_y)
            dv_y = 0
            for ci, r in enumerate((r0, r1)):
                em.emit("vector", lambda e, s, mc=mc, ci=ci: e.tensor_scalar_mul(
                    tmpM[:, :],
                    pA(P45, 0, 512 * (mc % 2) + ci * 256, [128, [1, 256]]), 0.25), inc="dv")
                dv_y = em.emit("vector", lambda e, s, mc=mc, r=r: e.tensor_add(
                    pA(acc, 0, mc * T + r, [128, [8, 256]]),
                    pA(acc, 0, mc * T + r, [128, [8, 256]]),
                    tmpM[:, :]), inc="dv")
            bk.release(key, "dv", dv_y)
        free_pe[slot_of[(e_i, "w2", 1)]] = em.cnt["pe"]

    dv_all = em.cnt["dv"]
    em.wait("sync", "dv", dv_all)
    dfinal = dma("sync", bass.AP(out.tensor, 0, [[T, 128], [128 * T, 8], [1, T]]),
                 bass.AP(acc, 0, [[FC * T, 128], [T, 8], [1, T]]))
    if "back" in tap_aps:
        em.wait("sync", "pe", em.cnt["pe"])
        dfinal = dma("sync", tap_aps["back"][:, :],
                     bass.AP(backing, 0, [[BACK, 128], [1, BACK]]))
    if "xn" in tap_aps:
        for q in range(4):
            em.wait("vector", "pe", em.cnt["pe"])
            dvq = em.emit("vector", lambda e, s, q=q: e.tensor_copy(
                bass.AP(stg, 0, [[4096, 128], [1, 4096]]),
                bass.AP(xn, q * 4096, [[FC * T, 128], [1, 4096]])), inc="dv")
            em.wait("sync", "dv", dvq)
            dt = dma("sync", bass.AP(tap_aps["xn"].tensor, q * 4096,
                                     [[FC * T, 128], [1, 4096]]),
                     bass.AP(stg, 0, [[4096, 128], [1, 4096]]))
            em.wait("vector", "d", dt)
    em.wait("sync", "d", dfinal)

    semctxs, sems = [], {}
    for ch in em.cnt:
        cm = nc.semaphore("sem_" + ch)
        semctxs.append(cm)
        sems[ch] = cm.__enter__()
    blk_cm = nc.Block()
    block = blk_cm.__enter__()
    for eng in EM.ENGS:
        stream = em.streams[eng]
        getattr(block, eng)(lambda e, stream=stream: [f(e, sems) for f in stream])
    blk_cm.__exit__(None, None, None)
    for cm in semctxs:
        cm.__exit__(None, None, None)
    for cm in reversed(ctxs):
        cm.__exit__(None, None, None)
    return nc


_CACHE = {}


def _get_nc(taps=()):
    key = tuple(taps)
    if key not in _CACHE:
        _CACHE[key] = build_nc(taps=key)
    return _CACHE[key]


def host_inputs(full):
    f32 = np.float32
    x = np.asarray(full["L0_attn_x"], f32)
    shared = {
        "wqT": np.ascontiguousarray(np.asarray(full["L0_attn_Wq"], f32).T),
        "wkT": np.ascontiguousarray(np.asarray(full["L0_attn_Wk"], f32).T),
        "wvT": np.ascontiguousarray(np.asarray(full["L0_attn_Wv"], f32).T),
        "woT": np.ascontiguousarray(np.asarray(full["L0_attn_Wo"], f32).T),
        "sw1": np.ascontiguousarray(np.asarray(full["L0_ffn_sw1"], f32)),
        "sw3": np.ascontiguousarray(np.asarray(full["L0_ffn_sw3"], f32)),
        "sw2": np.ascontiguousarray(np.asarray(full["L0_ffn_sw2"], f32)),
        "w1": np.ascontiguousarray(np.asarray(full["L0_ffn_w1"], f32)),
        "w2": np.ascontiguousarray(np.asarray(full["L0_ffn_w2"], f32)),
        "sb1c": np.ascontiguousarray(np.asarray(full["L0_ffn_sb1"], f32).reshape(16, 128).T),
        "sb3c": np.ascontiguousarray(np.asarray(full["L0_ffn_sb3"], f32).reshape(16, 128).T),
        "sb2c": np.ascontiguousarray(np.asarray(full["L0_ffn_sb2"], f32).reshape(8, 128).T),
        "b1c": np.ascontiguousarray(np.asarray(full["L0_ffn_b1"], f32).reshape(NE, 8, 128).transpose(0, 2, 1)),
        "b2c": np.ascontiguousarray(np.asarray(full["L0_ffn_b2"], f32).reshape(NE, 8, 128).transpose(0, 2, 1)),
        "tmaskD": np.ascontiguousarray(np.triu(np.ones((128, 128), f32))),
        "eyeD": np.eye(128, dtype=f32),
        "onesD": np.ones((128, 128), f32),
    }
    zeros_prev = np.zeros((D, T), f32)
    in_maps = []
    for c in range(8):
        b, half = c // 2, c % 2
        m = dict(shared)
        m["xin"] = np.ascontiguousarray(x[b, half * T:(half + 1) * T].T)
        m["xprev"] = (np.ascontiguousarray(x[b, 0:T].T) if half == 1 else zeros_prev)
        m["maskA"] = np.full((128, 1), float(half), f32)
        in_maps.append(m)
    return in_maps


def kernel(**inputs):
    nc = _get_nc(())
    in_maps = host_inputs(inputs)
    res = run_bass_kernel_spmd(nc, in_maps, core_ids=list(range(8)))
    outp = np.empty((4, 4096, 1024), np.float32)
    for c in range(8):
        b, half = c // 2, c % 2
        outp[b, half * T:(half + 1) * T] = res.results[c]["out"].T
    return outp
